# version 02
# baseline (speedup 1.0000x reference)
import numpy as np
import concourse.bass as bass
import concourse.mybir as mybir
from concourse.bass_utils import run_bass_kernel_spmd
from concourse.bass_core_rust import IndirectOffsetOnAxis

F32 = mybir.dt.float32
BF16 = mybir.dt.bfloat16
U32 = mybir.dt.uint32
I32 = mybir.dt.int32
U8 = mybir.dt.uint8
AF = mybir.ActivationFunctionType
ALU = mybir.AluOpType
AX = mybir.AxisListType

NC = 8
B = 2
T = 8192
D = 1024
TC = 2048
TX = TC + 2
NEG = -3.0e38

_DEV_STAGE = None
_DEBUG_B0 = False
_DEBUG_FFN = False

import math
_EPS0 = math.exp(-1.0 / 12.0)
EPS_R = _EPS0 / 2.0
EPS_K = _EPS0 + 0.3 / 11.0
EPS_V = _EPS0

ENGINES = ("pe", "act", "dve", "pool", "sp")


class Em:
    def __init__(self, nc):
        self.nc = nc
        self.ops = []
        self.cnt = {}
        for e in ENGINES:
            self.cnt[e] = 0
            self.cnt["q" + e] = 0
        self.cnt["cc"] = 0

    def all_tokens(self):
        return [(s, v) for s, v in self.cnt.items() if v > 0]

    def op(self, eng, fn, deps="all"):
        waits = self.all_tokens() if deps == "all" else list(deps)
        self.cnt[eng] += 1
        tok = (eng, self.cnt[eng])
        self.ops.append((eng, fn, [(eng, 1)], waits))
        return tok

    def dma(self, eng, fn, deps="all"):
        waits = self.all_tokens() if deps == "all" else list(deps)
        q = "q" + eng
        self.cnt[q] += 16
        tok = (q, self.cnt[q])
        self.ops.append((eng, fn, [(q, 16)], waits))
        return tok

    def cc(self, fn, deps="all"):
        waits = self.all_tokens() if deps == "all" else list(deps)
        self.cnt["cc"] += 1
        tok = ("cc", self.cnt["cc"])
        self.ops.append(("pool", fn, [("cc", 1)], waits))
        return tok

    def build(self):
        nc = self.nc
        import contextlib
        stack = contextlib.ExitStack()
        sem = {}
        for s in self.cnt:
            if self.cnt[s] > 0:
                sem[s] = stack.enter_context(nc.semaphore(name="sem_" + s))
        block = stack.enter_context(nc.Block())
        last = {e: {} for e in ENGINES}

        def mk(ename):
            def section(eng_obj):
                for (eng, fn, incs, waits) in self.ops:
                    if eng != ename:
                        continue
                    for (s, v) in waits:
                        if s in sem and last[ename].get(s, 0) < v:
                            eng_obj.wait_ge(sem[s], v)
                            last[ename][s] = v
                    if fn is not None:
                        inst = fn(eng_obj)
                        for (s, amt) in incs:
                            inst.then_inc(sem[s], amt)
                for s, v in self.cnt.items():
                    if v > 0 and (s.startswith("q") or s == "cc") and s in sem and last[ename].get(s, 0) < v:
                        eng_obj.wait_ge(sem[s], v)
                        last[ename][s] = v
            return section

        secs = {"pe": "tensor", "act": "scalar", "dve": "vector",
                "pool": "gpsimd", "sp": "sync"}
        for ename in ENGINES:
            if any(o[0] == ename for o in self.ops):
                getattr(block, secs[ename])(mk(ename))
        stack.close()
        return nc


def _chunks(n, size=512):
    return [(i, min(size, n - i)) for i in range(0, n, size)]


class Ctx:

    def __init__(self, nc):
        import contextlib
        self.nc = nc
        self.em = Em(nc)
        self.st = contextlib.ExitStack()
        self._ones = None
        self._ident = None

    def sb(self, name, shape, dt=F32):
        return self.st.enter_context(self.nc.sbuf_tensor(name, shape, dt))

    def ps(self, name, shape, dt=F32):
        return self.st.enter_context(self.nc.psum_tensor(name, shape, dt))

    def ones_col(self):
        if self._ones is None:
            self._ones = self.sb("g_ones", [128, 128], F32)
            self.em.op('pool', lambda e: e.memset(self._ones[:], 1.0))
        return self._ones

    def ident(self, dt=F32):
        if self._ident is None:
            iot = self.sb("g_iot", [128, 128], F32)
            pcol = self.sb("g_pcol", [128, 1], F32)
            idf = self.sb("g_idf", [128, 128], F32)
            idb = self.sb("g_idb", [128, 128], BF16)
            self.em.op('pool', lambda e: e.iota(iot[:], [[1, 128]], base=0, channel_multiplier=0, allow_small_or_imprecise_dtypes=True))
            self.em.op('pool', lambda e: e.iota(pcol[:], [[0, 1]], base=0, channel_multiplier=1, allow_small_or_imprecise_dtypes=True))
            self.em.op('dve', lambda e: e.tensor_scalar(out=idf[:], in0=iot[:], scalar1=pcol[:], scalar2=None, op0=ALU.is_equal))
            self.em.op('dve', lambda e: e.tensor_copy(out=idb[:], in_=idf[:]))
            self._ident = {F32: idf, BF16: idb}
        return self._ident[dt]



def emit_ln(cx, xdram, W, xn_sb, spill=None, name="ln", shifts=None, g_cols=None, b_cols=None):
    em, nc = cx.em, cx.nc
    ones = cx.ones_col()
    import contextlib
    with contextlib.ExitStack() as st:
        xc = st.enter_context(nc.sbuf_tensor(name + "_xc", [128, 8 * 512], F32))
        sq = st.enter_context(nc.sbuf_tensor(name + "_sq", [128, 512], F32))
        p1 = st.enter_context(nc.psum_tensor(name + "_p1", [1, 512], F32))
        p2 = st.enter_context(nc.psum_tensor(name + "_p2", [1, 512], F32))
        r1 = st.enter_context(nc.sbuf_tensor(name + "_r1", [1, 512], F32))
        r2 = st.enter_context(nc.sbuf_tensor(name + "_r2", [1, 512], F32))
        bc1 = st.enter_context(nc.psum_tensor(name + "_bc1", [128, 512], F32))
        bc2 = st.enter_context(nc.psum_tensor(name + "_bc2", [128, 512], F32))
        tmp = st.enter_context(nc.sbuf_tensor(name + "_tmp", [128, 512], F32))
        for (off, w) in _chunks(W, 512):
            for i in range(8):
                sh = 0 if shifts is None else shifts[i]
                em.dma('sp', lambda e, i=i, off=off, w=w, sh=sh: e.dma_start(
                    out=xc[:, i * 512:i * 512 + w], in_=xdram[i * 128:(i + 1) * 128, off + sh:off + sh + w]))
            for i in range(8):
                em.op('pe', lambda e, i=i, w=w: e.matmul(p1[0:1, 0:w], ones[:, 0:1], xc[:, i * 512:i * 512 + w], start=(i == 0), stop=(i == 7)))
            for i in range(8):
                em.op('act', lambda e, i=i, w=w: e.activation(out=sq[:, 0:w], in_=xc[:, i * 512:i * 512 + w], func=AF.Square))
                em.op('pe', lambda e, i=i, w=w: e.matmul(p2[0:1, 0:w], ones[:, 0:1], sq[:, 0:w], start=(i == 0), stop=(i == 7)))
            em.op('dve', lambda e, w=w: e.tensor_scalar(out=r1[0:1, 0:w], in0=p1[0:1, 0:w], scalar1=1.0 / D, scalar2=None, op0=ALU.mult))
            em.op('act', lambda e, w=w: e.activation(out=r2[0:1, 0:w], in_=r1[0:1, 0:w], func=AF.Square))
            em.op('dve', lambda e, w=w: e.scalar_tensor_tensor(out=r2[0:1, 0:w], in0=p2[0:1, 0:w], scalar=1.0 / D, in1=r2[0:1, 0:w], op0=ALU.mult, op1=ALU.subtract))
            em.op('dve', lambda e, w=w: e.tensor_scalar(out=r2[0:1, 0:w], in0=r2[0:1, 0:w], scalar1=1e-5, scalar2=None, op0=ALU.add))
            em.op('act', lambda e, w=w: e.activation(out=r2[0:1, 0:w], in_=r2[0:1, 0:w], func=AF.Ln))
            em.op('act', lambda e, w=w: e.activation(out=r2[0:1, 0:w], in_=r2[0:1, 0:w], func=AF.Exp, scale=-0.5))
            em.op('dve', lambda e, w=w: e.scalar_tensor_tensor(out=r1[0:1, 0:w], in0=r1[0:1, 0:w], scalar=-1.0, in1=r2[0:1, 0:w], op0=ALU.mult, op1=ALU.mult))
            em.op('pe', lambda e, w=w: e.matmul(bc1[:, 0:w], ones[0:1, :], r2[0:1, 0:w], start=True, stop=True))
            em.op('pe', lambda e, w=w: e.matmul(bc2[:, 0:w], ones[0:1, :], r1[0:1, 0:w], start=True, stop=True))
            for i in range(8):
                em.op('dve', lambda e, i=i, w=w: e.tensor_tensor(out=tmp[:, 0:w], in0=xc[:, i * 512:i * 512 + w], in1=bc1[:, 0:w], op=ALU.mult))
                em.op('dve', lambda e, i=i, off=off, w=w: e.tensor_tensor(out=tmp[:, 0:w], in0=tmp[:, 0:w], in1=bc2[:, 0:w], op=ALU.add))
                if g_cols is not None:
                    em.op('dve', lambda e, i=i, w=w: e.tensor_scalar(out=tmp[:, 0:w], in0=tmp[:, 0:w], scalar1=g_cols[:, i:i + 1], scalar2=b_cols[:, i:i + 1], op0=ALU.mult, op1=ALU.add))
                em.op('dve', lambda e, i=i, off=off, w=w: e.tensor_copy(out=xn_sb[:, i * W + off:i * W + off + w], in_=tmp[:, 0:w]))
                if spill is not None:
                    em.dma('sp', lambda e, i=i, off=off, w=w: e.dma_start(out=spill[i * 128:(i + 1) * 128, off:off + w], in_=tmp[:, 0:w]))



def _emit_rwkv_tile(cx, i, xn, wT_dram, cols, trem_sb, ccol_e, ccol_v, contrib, tot_sb, with_carry):
    em, nc = cx.em, cx.nc
    import contextlib
    with contextlib.ExitStack() as st:
        sb = lambda n, s, dt=F32: st.enter_context(nc.sbuf_tensor(n + str(i), s, dt))
        wtile = sb("rw_w", [128, 8 * 3 * 128], BF16)
        need_r = contrib is not None
        kinds = (("r", 0), ("k", 1), ("v", 2)) if need_r else (("k", 1), ("v", 2))
        for (nm, j) in kinds:
            em.dma('pool', lambda e, nm=nm, j=j: e.dma_start(
                out=wtile.ap().rearrange("p (a k f) -> p a k f", a=8, k=3)[:, :, j, :],
                in_=wT_dram[nm][i].rearrange("a p f -> p a f")))
        yr = sb("yr", [128, TX]) if need_r else None
        yk = sb("yk", [128, TX]); yv = sb("yv", [128, TX])
        pp = st.enter_context(nc.psum_tensor("rw_pp" + str(i), [128, 512], F32))
        for (dst, j) in (((yr, 0),) if need_r else ()) + ((yk, 1), (yv, 2)):
            for (off, w) in _chunks(TX, 512):
                for dt_ in range(8):
                    em.op('pe', lambda e, dst=dst, j=j, off=off, w=w, dt_=dt_: e.matmul(
                        pp[:, 0:w], wtile[:, dt_ * 384 + j * 128: dt_ * 384 + j * 128 + 128],
                        xn[:, dt_ * TX + off: dt_ * TX + off + w], start=(dt_ == 0), stop=(dt_ == 7)))
                em.op('act', lambda e, dst=dst, off=off, w=w: e.activation(out=dst[:, off:off + w], in_=pp[:, 0:w], func=AF.Copy))
        kf = sb("kf", [128, TC]); ee = sb("ee", [128, TC]); vv = sb("vv", [128, TC])
        ev = sb("ev", [128, TC]); eu = sb("eu", [128, TC])
        em.op('dve', lambda e: e.scalar_tensor_tensor(out=kf[:], in0=yk[:, 0:TC], scalar=EPS_K / (1.0 - EPS_K), in1=yk[:, 2:TX], op0=ALU.mult, op1=ALU.add))
        em.op('dve', lambda e: e.tensor_scalar(out=kf[:], in0=kf[:], scalar1=(1.0 - EPS_K), scalar2=cols[:, 1:2], op0=ALU.mult, op1=ALU.add))
        em.op('dve', lambda e: e.scalar_tensor_tensor(out=vv[:], in0=yv[:, 0:TC], scalar=EPS_V / (1.0 - EPS_V), in1=yv[:, 2:TX], op0=ALU.mult, op1=ALU.add))
        em.op('dve', lambda e: e.tensor_scalar(out=vv[:], in0=vv[:], scalar1=(1.0 - EPS_V), scalar2=cols[:, 2:3], op0=ALU.mult, op1=ALU.add))
        em.op('dve', lambda e: e.tensor_scalar(out=ee[:], in0=trem_sb[:], scalar1=cols[:, 5:6], scalar2=None, op0=ALU.mult))
        em.op('dve', lambda e: e.tensor_tensor(out=ee[:], in0=kf[:], in1=ee[:], op=ALU.add))
        em.op('dve', lambda e: e.tensor_scalar(out=ee[:], in0=ee[:], scalar1=10.0, scalar2=-20.0, op0=ALU.min, op1=ALU.max))
        em.op('act', lambda e: e.activation(out=ee[:], in_=ee[:], func=AF.Exp))
        em.op('dve', lambda e: e.tensor_tensor(out=ev[:], in0=ee[:], in1=vv[:], op=ALU.mult))
        if not with_carry:
            scn = sb("scn", [128, TC])
            em.op('dve', lambda e: e.tensor_tensor_scan(out=scn[:], data0=ee[:], data1=ee[:], initial=0.0, op0=ALU.add, op1=ALU.bypass))
            em.op('dve', lambda e: e.tensor_copy(out=tot_sb[:, 2 * i:2 * i + 1], in_=scn[:, TC - 1:TC]))
            em.op('dve', lambda e: e.tensor_tensor_scan(out=scn[:], data0=ev[:], data1=ev[:], initial=0.0, op0=ALU.add, op1=ALU.bypass))
            em.op('dve', lambda e: e.tensor_copy(out=tot_sb[:, 2 * i + 1:2 * i + 2], in_=scn[:, TC - 1:TC]))
            return
        em.op('dve', lambda e: e.tensor_scalar(out=eu[:], in0=kf[:], scalar1=cols[:, 4:5], scalar2=10.0, op0=ALU.add, op1=ALU.min))
        em.op('dve', lambda e: e.tensor_scalar(out=eu[:], in0=eu[:], scalar1=-20.0, scalar2=None, op0=ALU.max))
        em.op('act', lambda e: e.activation(out=eu[:], in_=eu[:], func=AF.Exp))
        sA = sb("sA", [128, TC + 1]); sB = sb("sB", [128, TC + 1])
        em.op('dve', lambda e: e.tensor_copy(out=sB[:, 0:1], in_=ccol_e[:, i:i + 1]))
        em.op('dve', lambda e: e.tensor_tensor_scan(out=sB[:, 1:TC + 1], data0=ee[:], data1=ee[:], initial=ccol_e[:, i:i + 1], op0=ALU.add, op1=ALU.bypass))
        em.op('dve', lambda e: e.tensor_copy(out=sA[:, 0:1], in_=ccol_v[:, i:i + 1]))
        em.op('dve', lambda e: e.tensor_tensor_scan(out=sA[:, 1:TC + 1], data0=ev[:], data1=ev[:], initial=ccol_v[:, i:i + 1], op0=ALU.add, op1=ALU.bypass))
        em.op('dve', lambda e: e.tensor_tensor(out=ee[:], in0=sB[:, 0:TC], in1=eu[:], op=ALU.add))
        em.op('dve', lambda e: e.tensor_tensor(out=eu[:], in0=eu[:], in1=vv[:], op=ALU.mult))
        em.op('dve', lambda e: e.tensor_tensor(out=eu[:], in0=sA[:, 0:TC], in1=eu[:], op=ALU.add))
        em.op('act', lambda e: e.activation(out=ee[:], in_=ee[:], func=AF.Ln))
        em.op('act', lambda e: e.activation(out=ee[:], in_=ee[:], func=AF.Exp, scale=-1.0))
        em.op('dve', lambda e: e.tensor_tensor(out=eu[:], in0=eu[:], in1=ee[:], op=ALU.mult))
        rr = kf
        em.op('dve', lambda e: e.scalar_tensor_tensor(out=rr[:], in0=yr[:, 0:TC], scalar=EPS_R / (1.0 - EPS_R), in1=yr[:, 2:TX], op0=ALU.mult, op1=ALU.add))
        em.op('act', lambda e: e.activation(out=rr[:], in_=rr[:], func=AF.Sigmoid, scale=(1.0 - EPS_R), bias=cols[:, 0:1]))
        em.op('dve', lambda e, i=i: e.tensor_tensor(out=contrib[:, i * TC:(i + 1) * TC], in0=rr[:], in1=eu[:], op=ALU.mult))


def build_b0scan():
    nc = bass.Bass("TRN2", target_bir_lowering=False)
    cx = Ctx(nc)
    em = cx.em
    ap = lambda n, s, dt=F32, kind="ExternalInput": nc.dram_tensor(n, s, dt, kind=kind).ap()
    xT = ap("xT", [D, TX])
    WkTt = ap("WkTt", [8, 8, 128, 128])
    WvTt = ap("WvTt", [8, 8, 128, 128])
    colsd = ap("colsd", [128, 8 * 8])
    trem = ap("trem", [128, TC])
    totals = ap("totals", [128, 16], kind="ExternalOutput")

    xn = cx.sb("xn", [128, 8 * TX], BF16)
    emit_ln(cx, xT, TX, xn)
    cols_all = cx.sb("cols_all", [128, 64])
    em.dma('sp', lambda e: e.dma_start(out=cols_all[:], in_=colsd[:, :]))
    trem_sb = cx.sb("trem_sb", [128, TC])
    em.dma('sp', lambda e: e.dma_start(out=trem_sb[:], in_=trem[:, :]))
    for i in range(8):
        em.op('act', lambda e, i=i: e.activation(out=cols_all[:, i * 8 + 5:i * 8 + 6], in_=cols_all[:, i * 8 + 3:i * 8 + 4], func=AF.Relu))
        em.op('dve', lambda e, i=i: e.tensor_scalar(out=cols_all[:, i * 8 + 5:i * 8 + 6], in0=cols_all[:, i * 8 + 5:i * 8 + 6], scalar1=-1.0, scalar2=None, op0=ALU.mult))
    tot_sb = cx.sb("tot_sb", [128, 16])
    wT = {"k": WkTt, "v": WvTt}
    for i in range(8):
        _emit_rwkv_tile(cx, i, xn, wT, cols_all[:, i * 8:(i + 1) * 8], trem_sb, None, None, None, tot_sb, with_carry=False)
    em.dma('sp', lambda e: e.dma_start(out=totals[:, :], in_=tot_sb[:]))
    em.build()
    cx.st.close()
    return nc


def build_b0main():
    nc = bass.Bass("TRN2", target_bir_lowering=False)
    cx = Ctx(nc)
    em = cx.em
    ap = lambda n, s, dt=F32, kind="ExternalInput": nc.dram_tensor(n, s, dt, kind=kind).ap()
    xT = ap("xT", [D, TX])
    WrTt = ap("WrTt", [8, 8, 128, 128]); WkTt = ap("WkTt", [8, 8, 128, 128])
    WvTt = ap("WvTt", [8, 8, 128, 128]); WoTt = ap("WoTt", [8, 8, 128, 128])
    colsd = ap("colsd", [128, 64])
    trem = ap("trem", [128, TC])
    carry_e = ap("carry_e", [128, 64])
    carry_v = ap("carry_v", [128, 64])
    routerTt = ap("routerTt", [128, 64])
    sgTt = ap("sgTt", [16, 8, 128, 128]); suTt = ap("suTt", [16, 8, 128, 128])
    sdT = ap("sdT", [2 * D, D])
    egTt = ap("egTt", [8, 16, 8, 128, 128]); euTt = ap("euTt", [8, 16, 8, 128, 128])
    edT = ap("edT", [8, 2 * D, D])
    x1_out = ap("x1", [D, TC], kind="ExternalOutput")
    if _DEBUG_B0:
        s_dbg = ap("s_dbg", [D, TC], kind="ExternalOutput")
        xn_dbg = ap("xn_dbg", [D, TX], kind="ExternalOutput")
        g_dbg = ap("g_dbg", [128, 16 * 1024], kind="ExternalOutput")
        comb_dbg = ap("comb_dbg", [128, 128], kind="ExternalOutput")

    s_spill = nc.dram_tensor("s_spill", [D, TC], F32, kind="Internal").ap()
    xn_spill = nc.dram_tensor("xn_spill", [D, TX], F32, kind="Internal").ap()

    xn = cx.sb("xn", [128, 8 * TX], BF16)
    emit_ln(cx, xT, TX, xn, spill=xn_spill)

    cols_all = cx.sb("cols_all", [128, 64])
    em.dma('sp', lambda e: e.dma_start(out=cols_all[:], in_=colsd[:, :]))
    trem_sb = cx.sb("trem_sb", [128, TC])
    em.dma('sp', lambda e: e.dma_start(out=trem_sb[:], in_=trem[:, :]))
    for i in range(8):
        em.op('act', lambda e, i=i: e.activation(out=cols_all[:, i * 8 + 5:i * 8 + 6], in_=cols_all[:, i * 8 + 3:i * 8 + 4], func=AF.Relu))
        em.op('dve', lambda e, i=i: e.tensor_scalar(out=cols_all[:, i * 8 + 5:i * 8 + 6], in0=cols_all[:, i * 8 + 5:i * 8 + 6], scalar1=-1.0, scalar2=None, op0=ALU.mult))
    ccol_e = cx.sb("ccol_e", [128, 8]); ccol_v = cx.sb("ccol_v", [128, 8])
    car_e = cx.sb("car_e", [128, 64]); car_v = cx.sb("car_v", [128, 64])
    em.dma('sp', lambda e: e.dma_start(out=car_e[:], in_=carry_e[:, :]))
    em.dma('sp', lambda e: e.dma_start(out=car_v[:], in_=carry_v[:, :]))
    for i in range(8):
        em.op('dve', lambda e, i=i: e.reduce_sum(out=ccol_e[:, i:i + 1], in_=car_e[:, i * 8:(i + 1) * 8], axis=AX.X))
        em.op('dve', lambda e, i=i: e.reduce_sum(out=ccol_v[:, i:i + 1], in_=car_v[:, i * 8:(i + 1) * 8], axis=AX.X))

    contrib = cx.sb("contrib", [128, 8 * TC], BF16)
    wT = {"r": WrTt, "k": WkTt, "v": WvTt}
    for i in range(8):
        _emit_rwkv_tile(cx, i, xn, wT, cols_all[:, i * 8:(i + 1) * 8], trem_sb, ccol_e, ccol_v, contrib, None, with_carry=True)

    import contextlib
    with contextlib.ExitStack() as st:
        wo_sb = st.enter_context(nc.sbuf_tensor("wo_sb", [128, 8 * 128], BF16))
        s_f = st.enter_context(nc.sbuf_tensor("s_f", [128, TC], F32))
        pp = st.enter_context(nc.psum_tensor("ps_s", [128, 512], F32))
        for ot in range(8):
            em.dma('pool', lambda e, ot=ot: e.dma_start(out=wo_sb.ap().rearrange("p (a f) -> p a f", a=8), in_=WoTt[ot].rearrange("a p f -> p a f")))
            for (off, w) in _chunks(TC, 512):
                for dt_ in range(8):
                    em.op('pe', lambda e, ot=ot, off=off, w=w, dt_=dt_: e.matmul(
                        pp[:, 0:w], wo_sb[:, dt_ * 128:(dt_ + 1) * 128],
                        contrib[:, dt_ * TC + off: dt_ * TC + off + w], start=(dt_ == 0), stop=(dt_ == 7)))
                em.op('dve', lambda e, ot=ot, off=off, w=w: e.tensor_scalar(out=s_f[:, off:off + w], in0=pp[:, 0:w], scalar1=cols_all[:, ot * 8 + 6:ot * 8 + 7], scalar2=None, op0=ALU.add))
            em.dma('sp', lambda e, ot=ot: e.dma_start(out=s_spill[ot * 128:(ot + 1) * 128, :], in_=s_f[:]))

    comb = cx.sb("comb", [128, 128])
    with contextlib.ExitStack() as st:
        rt_sb = st.enter_context(nc.sbuf_tensor("rt_sb", [128, 64], F32))
        em.dma('sp', lambda e: e.dma_start(out=rt_sb[:], in_=routerTt[:, :]))
        xf_t = st.enter_context(nc.sbuf_tensor("xf_t", [128, 8 * 128], F32))
        pl = st.enter_context(nc.psum_tensor("pl", [128, 512], F32))
        lg = st.enter_context(nc.sbuf_tensor("lg", [128, 8], F32))
        elg = st.enter_context(nc.sbuf_tensor("elg", [128, 8], F32))
        mcol = st.enter_context(nc.sbuf_tensor("mcol", [128, 5], F32))
        for tt in range(16):
            for i in range(8):
                em.dma('sp', lambda e, tt=tt, i=i: e.dma_start(out=xf_t[:, i * 128:(i + 1) * 128], in_=xn_spill[i * 128:(i + 1) * 128, 1 + tt * 128:1 + (tt + 1) * 128]))
            for i in range(8):
                em.op('pe', lambda e, i=i: e.matmul(pl[:, 0:8], xf_t[:, i * 128:(i + 1) * 128], rt_sb[:, i * 8:(i + 1) * 8], start=(i == 0), stop=(i == 7)))
            em.op('dve', lambda e: e.reduce_max(out=mcol[:, 0:1], in_=pl[:, 0:8], axis=AX.X, negate=True))
            em.op('act', lambda e: e.activation(out=elg[:], in_=pl[:, 0:8], func=AF.Exp, bias=mcol[:, 0:1]))
            em.op('dve', lambda e: e.reduce_max(out=mcol[:, 4:5], in_=elg[:], axis=AX.X))
            em.op('dve', lambda e: e.tensor_scalar(out=lg[:], in0=elg[:], scalar1=mcol[:, 4:5], scalar2=None, op0=ALU.is_ge))
            em.op('dve', lambda e: e.scalar_tensor_tensor(out=lg[:], in0=lg[:], scalar=-2.0, in1=elg[:], op0=ALU.mult, op1=ALU.add))
            em.op('dve', lambda e: e.reduce_max(out=mcol[:, 1:2], in_=lg[:], axis=AX.X))
            em.op('dve', lambda e: e.tensor_scalar(out=lg[:], in0=elg[:], scalar1=mcol[:, 1:2], scalar2=None, op0=ALU.is_ge))
            em.op('dve', lambda e: e.scalar_tensor_tensor(out=lg[:], in0=elg[:], scalar=1.0, in1=lg[:], op0=ALU.bypass, op1=ALU.mult, accum_out=mcol[:, 2:3]))
            em.op('dve', lambda e: e.reciprocal(out=mcol[:, 3:4], in_=mcol[:, 2:3]))
            em.op('dve', lambda e, tt=tt: e.tensor_scalar(out=comb[:, tt * 8:(tt + 1) * 8], in0=lg[:], scalar1=mcol[:, 3:4], scalar2=None, op0=ALU.mult))

    acc = cx.sb("acc", [128, 16 * 1024])
    for h in range(4):
        em.op('pool', lambda e, h=h: e.memset(acc[:, h * 4096:(h + 1) * 4096], 0.0))

    with contextlib.ExitStack() as st:
        wg = st.enter_context(nc.sbuf_tensor("wg", [128, 8 * 256], BF16))
        ed_sb = st.enter_context(nc.sbuf_tensor("ed_sb", [128, 16 * 1024], BF16))
        hb = st.enter_context(nc.sbuf_tensor("hb", [128, 16 * 1024], BF16))
        pg = st.enter_context(nc.psum_tensor("pg", [128, 512], F32))
        pu = st.enter_context(nc.psum_tensor("pu", [128, 512], F32))
        py = st.enter_context(nc.psum_tensor("py", [128, 512], F32))
        sg_t = st.enter_context(nc.sbuf_tensor("sg_t", [128, 512], BF16))

        def expert(eidx):
            if eidx == 8:
                g_dram, u_dram, d_dram = sgTt, suTt, sdT
            else:
                g_dram, u_dram, d_dram = egTt[eidx], euTt[eidx], edT[eidx]
            for i in range(16):
                em.dma('pool', lambda e, i=i, d_dram=d_dram: e.dma_start(out=ed_sb[:, i * 1024:(i + 1) * 1024], in_=d_dram[i * 128:(i + 1) * 128, :]))
            for half in range(2):
                for ht in range(16):
                    em.dma('pool', lambda e, ht=ht, g_dram=g_dram: e.dma_start(
                        out=wg.ap().rearrange("p (a two f) -> p a two f", a=8, two=2)[:, :, 0, :],
                        in_=g_dram[ht].rearrange("a p f -> p a f")))
                    em.dma('pool', lambda e, ht=ht, u_dram=u_dram: e.dma_start(
                        out=wg.ap().rearrange("p (a two f) -> p a two f", a=8, two=2)[:, :, 1, :],
                        in_=u_dram[ht].rearrange("a p f -> p a f")))
                    for cch in range(2):
                        off = 1 + half * 1024 + cch * 512
                        for i in range(8):
                            em.op('pe', lambda e, i=i, off=off: e.matmul(pg[:], wg[:, i * 256:i * 256 + 128], xn[:, i * TX + off:i * TX + off + 512], start=(i == 0), stop=(i == 7)))
                        for i in range(8):
                            em.op('pe', lambda e, i=i, off=off: e.matmul(pu[:], wg[:, i * 256 + 128:i * 256 + 256], xn[:, i * TX + off:i * TX + off + 512], start=(i == 0), stop=(i == 7)))
                        em.op('act', lambda e: e.activation(out=sg_t[:], in_=pg[:], func=AF.Silu))
                        em.op('dve', lambda e, ht=ht, cch=cch: e.tensor_tensor(out=hb[:, ht * 1024 + cch * 512: ht * 1024 + (cch + 1) * 512], in0=pu[:], in1=sg_t[:], op=ALU.mult))
                for ts in range(8):
                    tt_g = half * 8 + ts
                    for och in range(2):
                        for h2 in range(16):
                            em.op('pe', lambda e, ts=ts, och=och, h2=h2: e.matmul(
                                py[:], hb[:, h2 * 1024 + ts * 128: h2 * 1024 + (ts + 1) * 128],
                                ed_sb[:, h2 * 1024 + och * 512: h2 * 1024 + (och + 1) * 512],
                                start=(h2 == 0), stop=(h2 == 15)))
                        if eidx == 8:
                            em.op('dve', lambda e, tt_g=tt_g, och=och: e.tensor_tensor(
                                out=acc[:, tt_g * 1024 + och * 512: tt_g * 1024 + (och + 1) * 512],
                                in0=acc[:, tt_g * 1024 + och * 512: tt_g * 1024 + (och + 1) * 512],
                                in1=py[:], op=ALU.add))
                        else:
                            em.op('dve', lambda e, tt_g=tt_g, och=och, eidx=eidx: e.scalar_tensor_tensor(
                                out=acc[:, tt_g * 1024 + och * 512: tt_g * 1024 + (och + 1) * 512],
                                in0=py[:], scalar=comb[:, tt_g * 8 + eidx: tt_g * 8 + eidx + 1],
                                in1=acc[:, tt_g * 1024 + och * 512: tt_g * 1024 + (och + 1) * 512],
                                op0=ALU.mult, op1=ALU.add))

        for eidx in range(9):
            expert(eidx)

    if _DEBUG_B0:
        em.dma('sp', lambda e: e.dma_start(out=g_dbg[:, :], in_=acc[:]))
        em.dma('sp', lambda e: e.dma_start(out=comb_dbg[:, :], in_=comb[:]))
        import contextlib as _ctl2
        with _ctl2.ExitStack() as st_d:
            tmpd = st_d.enter_context(nc.sbuf_tensor("tmpd", [128, TC], F32))
            for ot in range(8):
                em.dma('sp', lambda e, ot=ot: e.dma_start(out=tmpd[:], in_=s_spill[ot * 128:(ot + 1) * 128, :]))
                em.dma('sp', lambda e, ot=ot: e.dma_start(out=s_dbg[ot * 128:(ot + 1) * 128, :], in_=tmpd[:]))
                em.dma('sp', lambda e, ot=ot: e.dma_start(out=tmpd[:], in_=xn_spill[ot * 128:(ot + 1) * 128, 0:TC]))
                em.dma('sp', lambda e, ot=ot: e.dma_start(out=xn_dbg[ot * 128:(ot + 1) * 128, 0:TC], in_=tmpd[:]))
    with contextlib.ExitStack() as st:
        identf = cx.ident(F32)
        ptr = st.enter_context(nc.psum_tensor("ptr", [128, 128], F32))
        gfm = st.enter_context(nc.sbuf_tensor("gfm", [128, TC], F32))
        sfm = st.enter_context(nc.sbuf_tensor("sfm", [128, TC], F32))
        xfm = st.enter_context(nc.sbuf_tensor("xfm", [128, TC], F32))
        for ot in range(8):
            for tt in range(16):
                em.op('pe', lambda e, ot=ot, tt=tt: e.transpose(ptr[:], acc[:, tt * 1024 + ot * 128:tt * 1024 + (ot + 1) * 128], identf[:]))
                em.op('act', lambda e, tt=tt: e.activation(out=gfm[:, tt * 128:(tt + 1) * 128], in_=ptr[:], func=AF.Silu))
            em.dma('sp', lambda e, ot=ot: e.dma_start(out=sfm[:], in_=s_spill[ot * 128:(ot + 1) * 128, :]))
            em.dma('sp', lambda e, ot=ot: e.dma_start(out=xfm[:], in_=xT[ot * 128:(ot + 1) * 128, 1:TC + 1]))
            em.op('dve', lambda e: e.tensor_tensor(out=gfm[:], in0=gfm[:], in1=sfm[:], op=ALU.mult))
            em.op('dve', lambda e: e.tensor_tensor(out=gfm[:], in0=gfm[:], in1=xfm[:], op=ALU.add))
            em.dma('sp', lambda e, ot=ot: e.dma_start(out=x1_out[ot * 128:(ot + 1) * 128, :], in_=gfm[:]))

    em.build()
    cx.st.close()
    return nc



def _tile_wT(w):
    wT = np.ascontiguousarray(w.T)
    t = wT.reshape(8, 128, 8, 128).transpose(2, 0, 1, 3)
    return np.ascontiguousarray(t)


def _tile_upproj(wT, nh):
    t = wT.reshape(8, 128, nh, 128).transpose(2, 0, 1, 3)
    return np.ascontiguousarray(t)


def _prep_common(inp):
    x = inp["L0_attn_x"]
    pr = {}
    xT = []
    for c in range(NC):
        b, q = c // 4, c % 4
        t0 = q * TC
        lo, hi = t0 - 1, t0 + TC + 1
        sl = x[b, max(lo, 0):min(hi, T)]
        if lo < 0:
            sl = np.concatenate([np.zeros((1, D), np.float32), sl], 0)
        if hi > T:
            sl = np.concatenate([sl, np.zeros((1, D), np.float32)], 0)
        xT.append(np.ascontiguousarray(sl.T))
    pr["xT"] = xT
    trem = []
    for c in range(NC):
        q = c % 4
        t0 = q * TC
        row = (T - (np.arange(t0, t0 + TC) + 1)).astype(np.float32)
        trem.append(np.ascontiguousarray(np.broadcast_to(row[None, :], (128, TC)).copy()))
    pr["trem"] = trem
    for nm, key in (("WrTt", "Wr"), ("WkTt", "Wk"), ("WvTt", "Wv"), ("WoTt", "Wo")):
        pr[nm] = _tile_wT(inp["L0_attn_" + key])
    cols = np.zeros((128, 64), np.float32)
    for i in range(8):
        sl = slice(i * 128, (i + 1) * 128)
        cols[:, i * 8 + 0] = inp["L0_attn_br"][sl]
        cols[:, i * 8 + 1] = inp["L0_attn_bk"][sl]
        cols[:, i * 8 + 2] = inp["L0_attn_bv"][sl]
        cols[:, i * 8 + 3] = inp["L0_attn_time_decay"][sl]
        cols[:, i * 8 + 4] = inp["L0_attn_U"][sl]
        cols[:, i * 8 + 6] = inp["L0_attn_bo"][sl]
    pr["colsd"] = np.ascontiguousarray(cols)
    rt = inp["L0_ffn_router_w"].T
    pr["routerTt"] = np.ascontiguousarray(rt.reshape(8, 128, 8).reshape(8, 128, 8).transpose(1, 0, 2).reshape(128, 64))
    pr["sgTt"] = _tile_upproj(np.ascontiguousarray(inp["L0_ffn_sg"].T), 16)
    pr["suTt"] = _tile_upproj(np.ascontiguousarray(inp["L0_ffn_su"].T), 16)
    pr["sdT"] = np.ascontiguousarray(inp["L0_ffn_sd"].T)
    pr["egTt"] = np.ascontiguousarray(np.stack([_tile_upproj(np.ascontiguousarray(inp["L0_ffn_eg"][e].T), 16) for e in range(8)]))
    pr["euTt"] = np.ascontiguousarray(np.stack([_tile_upproj(np.ascontiguousarray(inp["L0_ffn_eu"][e].T), 16) for e in range(8)]))
    pr["edT"] = np.ascontiguousarray(inp["L0_ffn_ed"].transpose(0, 2, 1))
    return pr


def _run_b0(inp, pr):
    nc1 = build_b0scan()
    maps1 = []
    for c in range(NC):
        maps1.append({"xT": pr["xT"][c], "WkTt": pr["WkTt"], "WvTt": pr["WvTt"],
                      "colsd": pr["colsd"], "trem": pr["trem"][c]})
    r1 = run_bass_kernel_spmd(nc1, maps1, core_ids=list(range(NC)))
    totals = [r1.results[c]["totals"] for c in range(NC)]

    carry_e, carry_v = [], []
    for c in range(NC):
        b, q = c // 4, c % 4
        ae = np.zeros((128, 64), np.float32)
        av = np.zeros((128, 64), np.float32)
        for qq in range(q):
            tt = totals[b * 4 + qq]
            for i in range(8):
                ae[:, i * 8 + qq] = tt[:, 2 * i]
                av[:, i * 8 + qq] = tt[:, 2 * i + 1]
        carry_e.append(np.ascontiguousarray(ae))
        carry_v.append(np.ascontiguousarray(av))

    nc2 = build_b0main()
    maps2 = []
    for c in range(NC):
        m = {"xT": pr["xT"][c], "trem": pr["trem"][c], "carry_e": carry_e[c], "carry_v": carry_v[c]}
        for nm in ("WrTt", "WkTt", "WvTt", "WoTt", "colsd", "routerTt", "sgTt", "suTt", "sdT", "egTt", "euTt", "edT"):
            m[nm] = pr[nm]
        maps2.append(m)
    r2 = run_bass_kernel_spmd(nc2, maps2, core_ids=list(range(NC)))
    x1 = [r2.results[c]["x1"] for c in range(NC)]
    return x1





def emit_ln_sb(cx, src_cb, W, dst_cb, name, g_cols=None, b_cols=None, f32_cb=None):
    em, nc = cx.em, cx.nc
    ones = cx.ones_col()
    import contextlib
    with contextlib.ExitStack() as st:
        sq = st.enter_context(nc.sbuf_tensor(name + "_sq", [128, 512], F32))
        p1 = st.enter_context(nc.psum_tensor(name + "_p1", [1, 512], F32))
        p2 = st.enter_context(nc.psum_tensor(name + "_p2", [1, 512], F32))
        r1 = st.enter_context(nc.sbuf_tensor(name + "_r1", [1, 512], F32))
        r2 = st.enter_context(nc.sbuf_tensor(name + "_r2", [1, 512], F32))
        bc1 = st.enter_context(nc.psum_tensor(name + "_bc1", [128, 512], F32))
        bc2 = st.enter_context(nc.psum_tensor(name + "_bc2", [128, 512], F32))
        tmp = st.enter_context(nc.sbuf_tensor(name + "_tmp", [128, 512], F32))
        for (off, w) in _chunks(W, 512):
            for i in range(8):
                em.op('pe', lambda e, i=i, off=off, w=w: e.matmul(p1[0:1, 0:w], ones[:, 0:1], src_cb(i, off, w), start=(i == 0), stop=(i == 7)))
            for i in range(8):
                em.op('act', lambda e, i=i, off=off, w=w: e.activation(out=sq[:, 0:w], in_=src_cb(i, off, w), func=AF.Square))
                em.op('pe', lambda e, i=i, w=w: e.matmul(p2[0:1, 0:w], ones[:, 0:1], sq[:, 0:w], start=(i == 0), stop=(i == 7)))
            em.op('dve', lambda e, w=w: e.tensor_scalar(out=r1[0:1, 0:w], in0=p1[0:1, 0:w], scalar1=1.0 / D, scalar2=None, op0=ALU.mult))
            em.op('act', lambda e, w=w: e.activation(out=r2[0:1, 0:w], in_=r1[0:1, 0:w], func=AF.Square))
            em.op('dve', lambda e, w=w: e.scalar_tensor_tensor(out=r2[0:1, 0:w], in0=p2[0:1, 0:w], scalar=1.0 / D, in1=r2[0:1, 0:w], op0=ALU.mult, op1=ALU.subtract))
            em.op('dve', lambda e, w=w: e.tensor_scalar(out=r2[0:1, 0:w], in0=r2[0:1, 0:w], scalar1=1e-5, scalar2=None, op0=ALU.add))
            em.op('act', lambda e, w=w: e.activation(out=r2[0:1, 0:w], in_=r2[0:1, 0:w], func=AF.Ln))
            em.op('act', lambda e, w=w: e.activation(out=r2[0:1, 0:w], in_=r2[0:1, 0:w], func=AF.Exp, scale=-0.5))
            em.op('dve', lambda e, w=w: e.scalar_tensor_tensor(out=r1[0:1, 0:w], in0=r1[0:1, 0:w], scalar=-1.0, in1=r2[0:1, 0:w], op0=ALU.mult, op1=ALU.mult))
            em.op('pe', lambda e, w=w: e.matmul(bc1[:, 0:w], ones[0:1, :], r2[0:1, 0:w], start=True, stop=True))
            em.op('pe', lambda e, w=w: e.matmul(bc2[:, 0:w], ones[0:1, :], r1[0:1, 0:w], start=True, stop=True))
            for i in range(8):
                em.op('dve', lambda e, i=i, off=off, w=w: e.tensor_tensor(out=tmp[:, 0:w], in0=src_cb(i, off, w), in1=bc1[:, 0:w], op=ALU.mult))
                em.op('dve', lambda e, i=i, off=off, w=w: e.tensor_tensor(out=tmp[:, 0:w], in0=tmp[:, 0:w], in1=bc2[:, 0:w], op=ALU.add))
                if g_cols is not None:
                    em.op('dve', lambda e, i=i, off=off, w=w: e.tensor_scalar(out=tmp[:, 0:w], in0=tmp[:, 0:w], scalar1=g_cols[:, i:i + 1], scalar2=b_cols[:, i:i + 1], op0=ALU.mult, op1=ALU.add))
                if f32_cb is not None:
                    em.op('dve', lambda e, i=i, off=off, w=w: e.tensor_copy(out=f32_cb(i, off, w), in_=tmp[:, 0:w]))
                em.op('dve', lambda e, i=i, off=off, w=w: e.tensor_copy(out=dst_cb(i, off, w), in_=tmp[:, 0:w]))



def build_b1pre():
    nc = bass.Bass("TRN2", target_bir_lowering=False)
    cx = Ctx(nc)
    em = cx.em
    ap = lambda n, s, dt=F32, kind="ExternalInput": nc.dram_tensor(n, s, dt, kind=kind).ap()
    x1T = ap("x1T", [D, TC])
    qkvw = ap("qkvw", [12, 8, 128, 128])
    wvw = ap("wvw", [8, 8, 128, 128])
    wvb = ap("wvb", [128, 8])
    xn1_out = ap("xn1", [D, TC], kind="ExternalOutput")
    q_out = ap("q_o", [512, TC], kind="ExternalOutput")
    k_out = ap("k_o", [512, TC], kind="ExternalOutput")
    v_out = ap("v_o", [512, TC], kind="ExternalOutput")
    lm_out = ap("lm_o", [512, 128], kind="ExternalOutput")
    val_out = ap("val_o", [D, TC], kind="ExternalOutput")

    xb = cx.sb("xb", [128, 8 * TC], BF16)
    xn1_sp = nc.dram_tensor("xn1_sp", [D, TC], F32, kind="Internal").ap()
    emit_ln(cx, x1T, TC, xb, spill=xn1_sp)
    import contextlib
    with contextlib.ExitStack() as st:
        tbuf = st.enter_context(nc.sbuf_tensor("tbuf", [128, TC], F32))
        for i in range(8):
            em.dma('sp', lambda e, i=i: e.dma_start(out=tbuf[:], in_=xn1_sp[i * 128:(i + 1) * 128, :]))
            em.dma('sp', lambda e, i=i: e.dma_start(out=xn1_out[i * 128:(i + 1) * 128, :], in_=tbuf[:]))

    with contextlib.ExitStack() as st:
        wt = st.enter_context(nc.sbuf_tensor("wt", [128, 8 * 128], BF16))
        pp = st.enter_context(nc.psum_tensor("ppq", [128, 512], F32))
        ob = st.enter_context(nc.sbuf_tensor("ob", [128, TC], F32))
        red = st.enter_context(nc.sbuf_tensor("red", [128, 64], F32))
        SC = 0.125
        for it in range(12):
            em.dma('pool', lambda e, it=it: e.dma_start(out=wt.ap().rearrange("p (a f) -> p a f", a=8), in_=qkvw[it].rearrange("a p f -> p a f")))
            for (off, w) in _chunks(TC, 512):
                for dt_ in range(8):
                    em.op('pe', lambda e, it=it, off=off, w=w, dt_=dt_: e.matmul(
                        pp[:, 0:w], wt[:, dt_ * 128:(dt_ + 1) * 128], xb[:, dt_ * TC + off:dt_ * TC + off + w],
                        start=(dt_ == 0), stop=(dt_ == 7)))
                em.op('act', lambda e, it=it, off=off, w=w: e.activation(out=ob[:, off:off + w], in_=pp[:, 0:w], func=AF.Copy, scale=(SC if it < 4 else 1.0)))
            dst = q_out if it < 4 else (k_out if it < 8 else v_out)
            em.dma('sp', lambda e, it=it, dst=dst: e.dma_start(out=dst[(it % 4) * 128:(it % 4) * 128 + 128, :], in_=ob[:]))
            if it < 8:
                em.op('dve', lambda e: e.tensor_reduce(out=red[:], in_=ob.ap().rearrange("p (g s) -> p g s", s=32), axis=AX.X, op=ALU.add))
                em.op('dve', lambda e: e.tensor_scalar(out=red[:], in0=red[:], scalar1=1.0 / 32, scalar2=None, op0=ALU.mult))
                em.dma('sp', lambda e, it=it: e.dma_start(out=lm_out[(it % 4) * 128:(it % 4) * 128 + 128, (0 if it < 4 else 64):(64 if it < 4 else 128)], in_=red[:]))
        wvb_sb = st.enter_context(nc.sbuf_tensor("wvb_sb", [128, 8], F32))
        em.dma('sp', lambda e: e.dma_start(out=wvb_sb[:], in_=wvb[:, :]))
        for ot in range(8):
            em.dma('pool', lambda e, ot=ot: e.dma_start(out=wt.ap().rearrange("p (a f) -> p a f", a=8), in_=wvw[ot].rearrange("a p f -> p a f")))
            for (off, w) in _chunks(TC, 512):
                for dt_ in range(8):
                    em.op('pe', lambda e, ot=ot, off=off, w=w, dt_=dt_: e.matmul(
                        pp[:, 0:w], wt[:, dt_ * 128:(dt_ + 1) * 128], xb[:, dt_ * TC + off:dt_ * TC + off + w],
                        start=(dt_ == 0), stop=(dt_ == 7)))
                em.op('dve', lambda e, ot=ot, off=off, w=w: e.tensor_scalar(out=ob[:, off:off + w], in0=pp[:, 0:w], scalar1=wvb_sb[:, ot:ot + 1], scalar2=None, op0=ALU.add))
            em.dma('sp', lambda e, ot=ot: e.dma_start(out=val_out[ot * 128:(ot + 1) * 128, :], in_=ob[:]))
    em.build()
    cx.st.close()
    return nc


def _prep_b1(inp):
    pr = {}
    qkvw = inp["L1_attn_qkv_w"]
    pr["qkvw"] = np.ascontiguousarray(qkvw.reshape(8, 128, 12, 128).transpose(2, 0, 1, 3))
    wvw = inp["L1_attn_wv_w"]
    pr["wvw"] = np.ascontiguousarray(wvw.reshape(8, 128, 8, 128).transpose(2, 0, 1, 3))
    pr["wvb"] = np.ascontiguousarray(inp["L1_attn_wv_b"].reshape(8, 128).T)
    return pr


def _run_b1pre(x1, pr1):
    nc = build_b1pre()
    maps = []
    for c in range(NC):
        maps.append({"x1T": x1[c], "qkvw": pr1["qkvw"], "wvw": pr1["wvw"], "wvb": pr1["wvb"]})
    r = run_bass_kernel_spmd(nc, maps, core_ids=list(range(NC)))
    return [r.results[c] for c in range(NC)]



TE = 2304


def build_ffn_attn():
    nc = bass.Bass("TRN2", target_bir_lowering=False)
    cx = Ctx(nc)
    em = cx.em
    import contextlib
    ap = lambda n, s, dt=F32, kind="ExternalInput": nc.dram_tensor(n, s, dt, kind=kind).ap()
    xT = ap("xT", [D, 2305])
    wqk = ap("wqk", [16, 8, 128, 128])
    wv = ap("wv", [8, 128, D])
    wo = ap("wo", [8, 8, 128, 128])
    qscol = ap("qscol", [128, 2])
    tabq = ap("tabq", [128, 2 * 2048])
    tabk = ap("tabk", [128, 2 * TE])
    pswap = ap("pswap", [128, 128])
    masks = ap("masks", [128, 8 * 128])
    xout = ap("xout", [D, TC], kind="ExternalOutput")

    ones = cx.ones_col()
    identf0 = cx.ident(F32)
    bd = cx.sb("bd", [128, 4])
    bdT = cx.sb("bdT", [2, 128])
    em.op('pool', lambda e: e.memset(bd[:], 0.0))
    em.op('pool', lambda e: e.memset(bd[0:64, 0:1], 1.0))
    em.op('pool', lambda e: e.memset(bd[64:128, 1:2], 1.0))
    with nc.psum_tensor("pbdT", [2, 128], F32) as pbdT:
        em.op('pe', lambda e: e.transpose(pbdT[0:2, 0:128], bd[:, 0:2], identf0[:]))
        em.op('dve', lambda e: e.tensor_copy(out=bdT[:], in_=pbdT[0:2, 0:128]))

    mask_sb = cx.sb("mask_sb", [128, 8 * 128], F32)
    em.dma('sp', lambda e: e.dma_start(out=mask_sb[:], in_=masks[:, :]))
    qs_sb = cx.sb("qs_sb", [128, 2], F32)
    em.dma('sp', lambda e: e.dma_start(out=qs_sb[:], in_=qscol[:, :]))
    psw = cx.sb("psw", [128, 128], F32)
    em.dma('sp', lambda e: e.dma_start(out=psw[:], in_=pswap[:, :]))
    tq = cx.sb("tq", [128, 2 * 2048], BF16)
    tk = cx.sb("tk", [128, 2 * TE], BF16)
    em.dma('pool', lambda e: e.dma_start(out=tq[:], in_=tabq[:, :]))
    em.dma('pool', lambda e: e.dma_start(out=tk[:], in_=tabk[:, :]))
    c30 = cx.sb("c30", [128, 1])
    em.op('pool', lambda e: e.memset(c30[:], 1e-30))
    ones_bf = cx.sb("ones_bf", [128, 1], BF16)
    em.op('pool', lambda e: e.memset(ones_bf[:], 1.0))

    wv_sb = cx.sb("wv_sb", [128, 8 * 1024], BF16)
    for i in range(8):
        em.dma('pool', lambda e, i=i: e.dma_start(out=wv_sb[:, i * 1024:(i + 1) * 1024], in_=wv[i]))
    qf = cx.sb("qf", [128, 8 * 2048], BF16)
    kf = cx.sb("kf", [128, 8 * TE], BF16)
    vbuf = cx.sb("vbuf", [128, 18 * 16 * 64], BF16)

    with contextlib.ExitStack() as st:
        sb = lambda n, s, dt=F32: st.enter_context(nc.sbuf_tensor(n, s, dt))
        xb = sb("xb", [128, 8 * TE], BF16)
        for i in range(8):
            sh = 1 if i < 4 else 0
            em.dma('pool', lambda e, i=i, sh=sh: e.dma_start(out=xb[:, i * TE:(i + 1) * TE], in_=xT[i * 128:(i + 1) * 128, sh:sh + TE]))
        wt = sb("wt", [128, 8 * 128], BF16)
        pp = st.enter_context(nc.psum_tensor("ppq", [128, 512], F32))
        psp = st.enter_context(nc.psum_tensor("psp", [128, 512], F32))
        pnrm = st.enter_context(nc.psum_tensor("pnrm", [2, 512], F32))
        pbc = st.enter_context(nc.psum_tensor("pbc", [128, 512], F32))
        qraw = sb("qraw", [128, 512]); sqb = sb("sqb", [128, 512])
        nrm = sb("nrm", [2, 512]); t1b = sb("t1b", [128, 512]); t2b = sb("t2b", [128, 512])
        for it in range(16):
            em.dma('pool', lambda e, it=it: e.dma_start(out=wt.ap().rearrange("p (a f) -> p a f", a=8), in_=wqk[it].rearrange("a p f -> p a f")))
            rng = 2048 if it < 8 else TE
            base_sh = 256 if it < 8 else 0
            tab = tq if it < 8 else tk
            tw = 2048 if it < 8 else TE
            qsc = qs_sb[:, 0:1] if it < 8 else qs_sb[:, 1:2]
            dstbuf = qf if it < 8 else kf
            for (off, w) in _chunks(rng, 512):
                for dt_ in range(8):
                    em.op('pe', lambda e, it=it, off=off, w=w, dt_=dt_, base_sh=base_sh: e.matmul(
                        pp[:, 0:w], wt[:, dt_ * 128:(dt_ + 1) * 128], xb[:, dt_ * TE + base_sh + off:dt_ * TE + base_sh + off + w],
                        start=(dt_ == 0), stop=(dt_ == 7)))
                em.op('dve', lambda e, w=w: e.tensor_copy(out=qraw[:, 0:w], in_=pp[:, 0:w]))
                em.op('act', lambda e, w=w: e.activation(out=sqb[:, 0:w], in_=qraw[:, 0:w], func=AF.Square))
                em.op('pe', lambda e, w=w: e.matmul(pnrm[0:2, 0:w], bd[:, 0:2], sqb[:, 0:w], start=True, stop=True))
                em.op('act', lambda e, w=w: e.activation(out=nrm[0:2, 0:w], in_=pnrm[0:2, 0:w], func=AF.Ln, bias=c30[0:2, 0:1]))
                em.op('act', lambda e, w=w: e.activation(out=nrm[0:2, 0:w], in_=nrm[0:2, 0:w], func=AF.Exp, scale=0.5))
                em.op('dve', lambda e, w=w: e.tensor_scalar(out=nrm[0:2, 0:w], in0=nrm[0:2, 0:w], scalar1=1e-12, scalar2=None, op0=ALU.max))
                em.op('dve', lambda e, w=w: e.reciprocal(out=nrm[0:2, 0:w], in_=nrm[0:2, 0:w]))
                em.op('pe', lambda e, w=w: e.matmul(pbc[:, 0:w], bdT[0:2, :], nrm[0:2, 0:w], start=True, stop=True))
                em.op('dve', lambda e, w=w, qsc=qsc: e.scalar_tensor_tensor(out=qraw[:, 0:w], in0=qraw[:, 0:w], scalar=qsc, in1=pbc[:, 0:w], op0=ALU.mult, op1=ALU.mult))
                em.op('pe', lambda e, w=w: e.matmul(psp[:, 0:w], psw[:], qraw[:, 0:w], start=True, stop=True))
                em.op('dve', lambda e, off=off, w=w, tab=tab: e.tensor_tensor(out=t1b[:, 0:w], in0=qraw[:, 0:w], in1=tab[:, off:off + w], op=ALU.mult))
                em.op('dve', lambda e, off=off, w=w, tab=tab, tw=tw: e.tensor_tensor(out=t2b[:, 0:w], in0=psp[:, 0:w], in1=tab[:, tw + off:tw + off + w], op=ALU.mult))
                em.op('dve', lambda e, it=it, off=off, w=w, dstbuf=dstbuf, rng=rng: e.tensor_tensor(
                    out=dstbuf[:, (it % 8) * rng + off:(it % 8) * rng + off + w], in0=t1b[:, 0:w], in1=t2b[:, 0:w], op=ALU.add))
        for tt in range(18):
            for och in range(2):
                for dt_ in range(8):
                    em.op('pe', lambda e, tt=tt, och=och, dt_=dt_: e.matmul(
                        pp[:], xb[:, dt_ * TE + tt * 128:dt_ * TE + (tt + 1) * 128],
                        wv_sb[:, dt_ * 1024 + och * 512:dt_ * 1024 + (och + 1) * 512], start=(dt_ == 0), stop=(dt_ == 7)))
                for hh in range(8):
                    h = och * 8 + hh
                    em.op('dve', lambda e, tt=tt, h=h, hh=hh: e.tensor_copy(
                        out=vbuf[:, (tt * 16 + h) * 64:(tt * 16 + h) * 64 + 64], in_=pp[:, hh * 64:(hh + 1) * 64]))

    avT = cx.sb("avT", [128, 8 * 2048], BF16)
    with contextlib.ExitStack() as st:
        sb = lambda n, s, dt=F32: st.enter_context(nc.sbuf_tensor(n, s, dt))
        expT = sb("expT", [128, 4 * 256], BF16)
        exin = sb("exin", [128, 128])
        mcol = sb("mcol", [128, 2])
        mrow_s = sb("mrow_s", [1, 256])
        zbcs = sb("zbcs", [128, 2048])
        zrow = sb("zrow", [128, 2048])
        ptm = st.enter_context(nc.psum_tensor("ptm", [128, 512], F32))
        pst = st.enter_context(nc.psum_tensor("pst", [128, 256], F32))
        pav = st.enter_context(nc.psum_tensor("pav", [128, 2048], F32))
        pz = st.enter_context(nc.psum_tensor("pz", [1, 256], F32))
        pzb = st.enter_context(nc.psum_tensor("pzb", [128, 512], F32))
        identf = cx.ident(F32)
        BLK = {(0, 0): 0, (1, 0): 1, (2, 0): 2, (1, 1): 3, (2, 1): 4, (3, 1): 5}
        STD = {(0, 0): 6, (1, 0): None, (2, 0): 7, (1, 1): 6, (2, 1): None, (3, 1): 7}
        SEQ = [(0, 0, 128), (1, 0, 256), (2, 0, 256), (3, 128, 256)]
        for h in range(16):
            tile, bh = h // 2, (h % 2) * 64
            for w_ in range(8):
                for it_ in range(2):
                    em.op('pe', lambda e, tile=tile, bh=bh, w_=w_, it_=it_: e.matmul(
                        ptm[:, :],
                        qf[bh:bh + 64, tile * 2048 + w_ * 256 + it_ * 128: tile * 2048 + w_ * 256 + it_ * 128 + 128],
                        kf[bh:bh + 64, tile * TE + w_ * 256: tile * TE + w_ * 256 + 512], start=True, stop=True))
                    em.op('dve', lambda e, it_=it_: e.reduce_max(out=mcol[:, it_:it_ + 1], in_=ptm[:], axis=AX.X, negate=True))
                    em.op('pe', lambda e, it_=it_: e.transpose(pz[0:1, 0:128], mcol[:, it_:it_ + 1], identf[:]))
                    em.op('dve', lambda e, it_=it_: e.tensor_copy(out=mrow_s[0:1, it_ * 128:(it_ + 1) * 128], in_=pz[0:1, 0:128]))
                for (jt, it_), mstd in STD.items():
                    midx = BLK[(jt, it_)] if w_ == 0 else mstd
                    em.op('pe', lambda e, jt=jt, it_=it_, w_=w_, tile=tile, bh=bh: e.matmul(
                        pst[:, 0:128], kf[bh:bh + 64, tile * TE + w_ * 256 + jt * 128: tile * TE + w_ * 256 + (jt + 1) * 128],
                        qf[bh:bh + 64, tile * 2048 + w_ * 256 + it_ * 128: tile * 2048 + w_ * 256 + (it_ + 1) * 128],
                        start=True, stop=False))
                    em.op('pe', lambda e, jt=jt, it_=it_: e.matmul(
                        pst[:, 0:128], ones[0:1, 0:128], mrow_s[0:1, it_ * 128:(it_ + 1) * 128], start=False, stop=True))
                    if midx is not None:
                        em.op('dve', lambda e, midx=midx: e.tensor_tensor(out=exin[:], in0=pst[:, 0:128], in1=mask_sb[:, midx * 128:(midx + 1) * 128], op=ALU.add))
                        em.op('act', lambda e, jt=jt, it_=it_: e.activation(out=expT[:, jt * 256 + it_ * 128: jt * 256 + (it_ + 1) * 128], in_=exin[:], func=AF.Exp))
                    else:
                        em.op('act', lambda e, jt=jt, it_=it_: e.activation(out=expT[:, jt * 256 + it_ * 128: jt * 256 + (it_ + 1) * 128], in_=pst[:, 0:128], func=AF.Exp))
                for si, (jt, i0, i1) in enumerate(SEQ):
                    vt = 2 * w_ + jt
                    em.op('pe', lambda e, jt=jt, i0=i0, i1=i1, vt=vt, h=h, w_=w_, si=si, bh=bh: e.matmul(
                        pav[bh:bh + 64, w_ * 256 + i0: w_ * 256 + i1],
                        vbuf[:, (vt * 16 + h) * 64:(vt * 16 + h) * 64 + 64],
                        expT[:, jt * 256 + i0: jt * 256 + i1], start=(si == 0), stop=(si == 3)))
                    em.op('pe', lambda e, jt=jt, i0=i0, i1=i1, w_=w_, si=si: e.matmul(
                        pz[0:1, i0:i1], ones_bf[:, 0:1], expT[:, jt * 256 + i0: jt * 256 + i1], start=(si == 0), stop=(si == 3)))
                em.op('dve', lambda e, w_=w_: e.tensor_copy(out=zrow[0:1, w_ * 256:(w_ + 1) * 256], in_=pz[0:1, 0:256]))
            em.op('dve', lambda e: e.tensor_scalar(out=zrow[0:1, :], in0=zrow[0:1, :], scalar1=1e-38, scalar2=None, op0=ALU.add))
            em.op('dve', lambda e: e.reciprocal(out=zrow[0:1, :], in_=zrow[0:1, :]))
            for (off, w) in _chunks(2048, 512):
                em.op('pe', lambda e, off=off, w=w, bh=bh: e.matmul(pzb[bh:bh + 64, 0:w], ones[0:1, 0:64], zrow[0:1, off:off + w], start=True, stop=True))
                em.op('dve', lambda e, off=off, w=w, bh=bh: e.tensor_copy(out=zbcs[bh:bh + 64, 0:w], in_=pzb[bh:bh + 64, 0:w]))
                em.op('dve', lambda e, off=off, w=w, tile=tile, bh=bh: e.tensor_tensor(
                    out=avT[bh:bh + 64, tile * 2048 + off: tile * 2048 + off + w], in0=pav[bh:bh + 64, off:off + w], in1=zbcs[bh:bh + 64, 0:w], op=ALU.mult))

    with contextlib.ExitStack() as st:
        sb = lambda n, s, dt=F32: st.enter_context(nc.sbuf_tensor(n, s, dt))
        wt2 = sb("wt2", [128, 8 * 128], BF16)
        xfr = sb("xfr", [128, 512])
        ob2 = sb("ob2", [128, 512])
        pow_ = st.enter_context(nc.psum_tensor("pow", [128, 512], F32))
        for ot in range(8):
            em.dma('pool', lambda e, ot=ot: e.dma_start(out=wt2.ap().rearrange("p (a f) -> p a f", a=8), in_=wo[ot].rearrange("a p f -> p a f")))
            for (off, w) in _chunks(TC, 512):
                for it in range(8):
                    em.op('pe', lambda e, ot=ot, off=off, w=w, it=it: e.matmul(
                        pow_[:, 0:w], wt2[:, it * 128:(it + 1) * 128], avT[:, it * 2048 + off:it * 2048 + off + w],
                        start=(it == 0), stop=(it == 7)))
                em.dma('sp', lambda e, ot=ot, off=off, w=w: e.dma_start(out=xfr[:, 0:w], in_=xT[ot * 128:(ot + 1) * 128, 257 + off:257 + off + w]))
                em.op('dve', lambda e, off=off, w=w: e.tensor_tensor(out=ob2[:, 0:w], in0=pow_[:, 0:w], in1=xfr[:, 0:w], op=ALU.add))
                em.dma('sp', lambda e, ot=ot, off=off, w=w: e.dma_start(out=xout[ot * 128:(ot + 1) * 128, off:off + w], in_=ob2[:, 0:w]))
    em.build()
    cx.st.close()
    return nc



def build_ffn_ffn():
    nc = bass.Bass("TRN2", target_bir_lowering=False)
    cx = Ctx(nc)
    em = cx.em
    import contextlib
    ap = lambda n, s, dt=F32, kind="ExternalInput": nc.dram_tensor(n, s, dt, kind=kind).ap()
    xT = ap("xT", [D, TC + 1])
    w1a = ap("w1a", [22, 8, 128, 128])
    w1g = ap("w1g", [22, 8, 128, 128])
    w2t = ap("w2t", [22, 128, D])
    lncols = ap("lncols", [128, 16])
    bcols1 = ap("bcols1", [128, 44])
    bcols2 = ap("bcols2", [128, 8])
    xout = ap("xout", [D, TC], kind="ExternalOutput")
    if _DEBUG_FFN:
        xn_dbg = ap("xn_dbg", [D, TC], kind="ExternalOutput")
        hb_dbg = ap("hb_dbg", [128, 22 * 512], kind="ExternalOutput")

    lnc = cx.sb("lnc", [128, 16], F32)
    em.dma('sp', lambda e: e.dma_start(out=lnc[:], in_=lncols[:, :]))
    b1c = cx.sb("b1c", [128, 44], F32)
    em.dma('sp', lambda e: e.dma_start(out=b1c[:], in_=bcols1[:, :]))
    b2c = cx.sb("b2c", [128, 8], F32)
    em.dma('sp', lambda e: e.dma_start(out=b2c[:], in_=bcols2[:, :]))

    xnb = cx.sb("xnb", [128, 8 * TC], BF16)
    shifts = [1, 1, 1, 1, 0, 0, 0, 0]
    emit_ln(cx, xT, TC, xnb, name="lnf", shifts=shifts, g_cols=lnc[:, 0:8], b_cols=lnc[:, 8:16])
    if _DEBUG_FFN:
        import contextlib as _cl
        with _cl.ExitStack() as _st:
            _tb = _st.enter_context(nc.sbuf_tensor("_tb", [128, TC], F32))
            for i in range(8):
                em.op('dve', lambda e, i=i: e.tensor_copy(out=_tb[:], in_=xnb[:, i * TC:(i + 1) * TC]))
                em.dma('sp', lambda e, i=i: e.dma_start(out=xn_dbg[i * 128:(i + 1) * 128, :], in_=_tb[:]))

    w2_sb = cx.sb("w2_sb", [128, 22 * 1024], BF16)
    for ht in range(22):
        em.dma('pool', lambda e, ht=ht: e.dma_start(out=w2_sb[:, ht * 1024:(ht + 1) * 1024], in_=w2t[ht]))

    import contextlib
    with contextlib.ExitStack() as st:
        sb = lambda n, s, dt=F32: st.enter_context(nc.sbuf_tensor(n, s, dt))
        wta = sb("wta", [128, 8 * 128], BF16)
        wtg = sb("wtg", [128, 8 * 128], BF16)
        hbuf = sb("hbuf", [128, 22 * 512], BF16)
        gel = sb("gel", [128, 512])
        xfr = sb("xfr", [128, 512])
        ob = sb("ob", [128, 512])
        pa = st.enter_context(nc.psum_tensor("pa", [128, 512], F32))
        pg = st.enter_context(nc.psum_tensor("pg2", [128, 512], F32))
        py = st.enter_context(nc.psum_tensor("py2", [128, 512], F32))
        for (off, w) in _chunks(TC, 512):
            for ht in range(22):
                em.dma('pool', lambda e, ht=ht: e.dma_start(out=wta.ap().rearrange("p (a f) -> p a f", a=8), in_=w1a[ht].rearrange("a p f -> p a f")))
                em.dma('pool', lambda e, ht=ht: e.dma_start(out=wtg.ap().rearrange("p (a f) -> p a f", a=8), in_=w1g[ht].rearrange("a p f -> p a f")))
                for dt_ in range(8):
                    em.op('pe', lambda e, ht=ht, off=off, w=w, dt_=dt_: e.matmul(
                        pa[:, 0:w], wta[:, dt_ * 128:(dt_ + 1) * 128], xnb[:, dt_ * TC + off:dt_ * TC + off + w], start=(dt_ == 0), stop=(dt_ == 7)))
                for dt_ in range(8):
                    em.op('pe', lambda e, ht=ht, off=off, w=w, dt_=dt_: e.matmul(
                        pg[:, 0:w], wtg[:, dt_ * 128:(dt_ + 1) * 128], xnb[:, dt_ * TC + off:dt_ * TC + off + w], start=(dt_ == 0), stop=(dt_ == 7)))
                em.op('act', lambda e, ht=ht, w=w: e.activation(out=gel[:, 0:w], in_=pg[:, 0:w], func=AF.Gelu, bias=b1c[:, 22 + ht:23 + ht]))
                em.op('dve', lambda e, ht=ht, w=w: e.scalar_tensor_tensor(out=hbuf[:, ht * 512:ht * 512 + w], in0=pa[:, 0:w], scalar=b1c[:, ht:ht + 1], in1=gel[:, 0:w], op0=ALU.add, op1=ALU.mult))
            if _DEBUG_FFN and off == 0:
                em.dma('pool', lambda e: e.dma_start(out=hb_dbg[:, :], in_=hbuf[:]))
            for ot in range(8):
                for ht in range(22):
                    em.op('pe', lambda e, ot=ot, ht=ht, w=w: e.matmul(
                        py[:, 0:w], w2_sb[:, ht * 1024 + ot * 128: ht * 1024 + (ot + 1) * 128], hbuf[:, ht * 512:ht * 512 + w],
                        start=(ht == 0), stop=(ht == 21)))
                em.dma('sp', lambda e, ot=ot, off=off, w=w: e.dma_start(out=xfr[:, 0:w], in_=xT[ot * 128:(ot + 1) * 128, 1 + off:1 + off + w]))
                em.op('dve', lambda e, ot=ot, w=w: e.tensor_scalar(out=ob[:, 0:w], in0=py[:, 0:w], scalar1=b2c[:, ot:ot + 1], scalar2=None, op0=ALU.add))
                em.op('dve', lambda e, w=w: e.tensor_tensor(out=ob[:, 0:w], in0=ob[:, 0:w], in1=xfr[:, 0:w], op=ALU.add))
                em.dma('sp', lambda e, ot=ot, off=off, w=w: e.dma_start(out=xout[ot * 128:(ot + 1) * 128, off:off + w], in_=ob[:, 0:w]))
    em.build()
    cx.st.close()
    return nc


def _prep_ffn(inp):
    pr = {}
    Wqkv = inp["L1_ffn_Wqkv"]
    wqk, wv, wo, w1a, w1g, w2t = [], [], [], [], [], []
    for i in range(6):
        qk = Wqkv[i][:, 0:2048]
        wqk.append(qk.reshape(8, 128, 16, 128).transpose(2, 0, 1, 3))
        wv.append(np.ascontiguousarray(Wqkv[i][:, 2048:3072].reshape(8, 128, 1024)))
        wo.append(inp["L1_ffn_Wo"][i].reshape(8, 128, 8, 128).transpose(2, 0, 1, 3))
        W1 = inp["L1_ffn_W1"][i]
        a = np.zeros((1024, 2816), np.float32); g = np.zeros((1024, 2816), np.float32)
        a[:, 0:2730] = W1[:, 0:2730]; g[:, 0:2730] = W1[:, 2730:5460]
        w1a.append(a.reshape(8, 128, 22, 128).transpose(2, 0, 1, 3))
        w1g.append(g.reshape(8, 128, 22, 128).transpose(2, 0, 1, 3))
        W2 = inp["L1_ffn_W2"][i]
        W2p = np.zeros((2816, 1024), np.float32); W2p[0:2730] = W2
        w2t.append(W2p.reshape(22, 128, 1024))
    pr["wqk"] = np.ascontiguousarray(np.stack(wqk))
    pr["wv"] = np.ascontiguousarray(np.stack(wv))
    pr["wo"] = np.ascontiguousarray(np.stack(wo))
    pr["w1a"] = np.ascontiguousarray(np.stack(w1a))
    pr["w1g"] = np.ascontiguousarray(np.stack(w1g))
    pr["w2t"] = np.ascontiguousarray(np.stack(w2t))
    qsc = []
    for i in range(6):
        col = np.zeros((128, 2), np.float32)
        col[:, 0] = np.tile(inp["L1_ffn_q_scale"][i], 2) * 8.0
        col[:, 1] = np.tile(inp["L1_ffn_k_scale"][i], 2)
        qsc.append(col)
    pr["qscol"] = np.ascontiguousarray(np.stack(qsc))
    lnc, b1c, b2c = [], [], []
    for i in range(6):
        l = np.zeros((128, 16), np.float32)
        l[:, 0:8] = inp["L1_ffn_ln_g"][i].reshape(8, 128).T
        l[:, 8:16] = inp["L1_ffn_ln_b"][i].reshape(8, 128).T
        lnc.append(l)
        b1 = inp["L1_ffn_b1"][i]
        ba = np.zeros((2816,), np.float32); bg = np.zeros((2816,), np.float32)
        ba[0:2730] = b1[0:2730]; bg[0:2730] = b1[2730:5460]
        bb = np.zeros((128, 44), np.float32)
        bb[:, 0:22] = ba.reshape(22, 128).T
        bb[:, 22:44] = bg.reshape(22, 128).T
        b1c.append(bb)
        b2c.append(inp["L1_ffn_b2"][i].reshape(8, 128).T)
    pr["lncols"] = np.ascontiguousarray(np.stack(lnc))
    pr["bcols1"] = np.ascontiguousarray(np.stack(b1c))
    pr["bcols2"] = np.ascontiguousarray(np.stack(b2c))
    n = T
    half = np.arange(0, 64, 2, dtype=np.float32)
    inv_freq = 1.0 / 10000.0 ** (half / 64)
    t = np.arange(n, dtype=np.float32)
    freqs = t[:, None] * inv_freq[None, :]
    freqs = np.concatenate([freqs, freqs], -1)
    base_scale = (half + 0.4 * 64) / (1.4 * 64)
    power = (t - n // 2) / (256 // 2)
    scale = base_scale[None, :] ** power[:, None]
    scale = np.concatenate([scale, scale], -1)
    cos = np.cos(freqs); sin = np.sin(freqs)
    sign = np.concatenate([-np.ones(32, np.float32), np.ones(32, np.float32)])
    CS = (cos * scale).T
    SS = (sin * scale * sign[None, :]).T
    CI = (cos / scale).T
    SI = (sin / scale * sign[None, :]).T
    pr["rope"] = (CS, SS, CI, SI)
    NEGF = np.float32(-3.0e38)
    jr = np.arange(128)[:, None]; ir = np.arange(128)[None, :]
    MA = np.where(ir <= jr, 0.0, NEGF).astype(np.float32)
    MB = np.where(ir >= jr, 0.0, NEGF).astype(np.float32)
    Z0 = np.zeros((128, 128), np.float32)
    NEGT = np.full((128, 128), NEGF, np.float32)
    std6 = [MA, Z0, MB, MA, Z0, MB]
    m_std = np.concatenate(std6 + [MA, MB], axis=1)
    w0_6 = [NEGT, NEGT, MB, NEGT, Z0, MB]
    m_w0 = np.concatenate(w0_6 + [MA, MB], axis=1)
    pr["mask_std"] = np.ascontiguousarray(m_std)
    pr["mask_w0"] = np.ascontiguousarray(m_w0)
    pp_ = np.arange(128)
    pr["pswap"] = np.ascontiguousarray((pp_[None, :] == (pp_[:, None] ^ 32)).astype(np.float32))
    return pr


def _ffn_core_inputs(prf, c, li):
    b, q = c // 4, c % 4
    t0 = q * TC
    CS, SS, CI, SI = prf["rope"]
    qlo = t0
    tabq = np.concatenate([np.tile(CS[:, qlo:qlo + 2048], (2, 1)), np.tile(SS[:, qlo:qlo + 2048], (2, 1))], axis=1)
    klo = t0 - 256
    if klo >= 0:
        CIs, SIs = CI[:, klo:klo + TE], SI[:, klo:klo + TE]
    else:
        CIs = np.concatenate([np.zeros((64, -klo), np.float32), CI[:, 0:TE + klo]], 1)
        SIs = np.concatenate([np.zeros((64, -klo), np.float32), SI[:, 0:TE + klo]], 1)
    tabk = np.concatenate([np.tile(CIs, (2, 1)), np.tile(SIs, (2, 1))], axis=1)
    m = {
        "wqk": prf["wqk"][li], "wv": prf["wv"][li], "wo": prf["wo"][li],
        "qscol": prf["qscol"][li],
        "tabq": np.ascontiguousarray(tabq.astype(np.float32)),
        "tabk": np.ascontiguousarray(tabk.astype(np.float32)),
        "masks": prf["mask_w0"] if q == 0 else prf["mask_std"],
        "pswap": prf["pswap"],
    }
    return m


def _run_ffn(xcur, prf):
    nca = build_ffn_attn()
    ncf = build_ffn_ffn()
    for li in range(6):
        maps = []
        for c in range(NC):
            b, q = c // 4, c % 4
            if q == 0:
                halo = np.zeros((D, 257), np.float32)
            else:
                halo = xcur[c - 1][:, TC - 257:]
            xe = np.ascontiguousarray(np.concatenate([halo, xcur[c]], axis=1))
            m = _ffn_core_inputs(prf, c, li)
            m["xT"] = xe
            maps.append(m)
        r = run_bass_kernel_spmd(nca, maps, core_ids=list(range(NC)))
        xcur = [np.ascontiguousarray(r.results[c]["xout"]) for c in range(NC)]
        maps = []
        for c in range(NC):
            b, q = c // 4, c % 4
            halo = np.zeros((D, 1), np.float32) if q == 0 else xcur[c - 1][:, TC - 1:]
            xe = np.ascontiguousarray(np.concatenate([halo, xcur[c]], axis=1))
            maps.append({"xT": xe, "w1a": prf["w1a"][li], "w1g": prf["w1g"][li], "w2t": prf["w2t"][li],
                         "lncols": prf["lncols"][li], "bcols1": prf["bcols1"][li], "bcols2": prf["bcols2"][li]})
        r = run_bass_kernel_spmd(ncf, maps, core_ids=list(range(NC)))
        xcur = [np.ascontiguousarray(r.results[c]["xout"]) for c in range(NC)]
    return xcur




def build_b1mid():
    nc = bass.Bass("TRN2", target_bir_lowering=False)
    cx = Ctx(nc)
    em = cx.em
    import contextlib
    ap = lambda n, s, dt=F32, kind="ExternalInput": nc.dram_tensor(n, s, dt, kind=kind).ap()
    lm_in = ap("lm_in", [128, 8 * 512])
    k_head = ap("k_head", [64, T])
    v_head = ap("v_head", [T, 64])
    z2_o = ap("z2_o", [128, 128], kind="ExternalOutput")

    ones = cx.ones_col()
    identf = cx.ident(F32)
    ones_bf = cx.sb("ones_bf", [128, 1], BF16)
    em.op('pool', lambda e: e.memset(ones_bf[:], 1.0))

    lm_sb = cx.sb("lm_sb", [128, 8 * 512], BF16)
    em.dma('pool', lambda e: e.dma_start(out=lm_sb[:], in_=lm_in[:, :]))
    k_sb = cx.sb("k_sb", [128, T], BF16)
    em.dma('pool', lambda e: e.dma_start(out=k_sb[0:64, :], in_=k_head[:, :]))
    v_sb = cx.sb("v_sb", [128, 64 * 64], BF16)
    em.dma('pool', lambda e: e.dma_start(out=v_sb.ap().rearrange("p (a f) -> p a f", a=64), in_=v_head.rearrange("(a p) f -> p a f", p=128)))

    a2buf = cx.sb("a2buf", [128, 16 * 512], F32)
    rsbuf = cx.sb("rsbuf", [128, 34], F32)
    csbuf = cx.sb("csbuf", [1, 16 * 256], F32)
    sc = cx.sb("sc", [1, 8], F32)
    rsc_col = cx.sb("rsc_col", [128, 1], F32)

    with contextlib.ExitStack() as st:
        sb = lambda n, s, dt=F32: st.enter_context(nc.sbuf_tensor(n, s, dt))
        pl2 = st.enter_context(nc.psum_tensor("pl2", [128, 256], F32))
        pcs = st.enter_context(nc.psum_tensor("pcs", [1, 256], F32))
        mc = sb("mc", [128, 2])
        for p in range(16):
            tile, bh = p // 2, (p % 2) * 64
            for it in range(2):
                em.op('pe', lambda e, tile=tile, bh=bh, it=it: e.matmul(
                    pl2[:, :], lm_sb[bh:bh + 64, tile * 512 + it * 128: tile * 512 + (it + 1) * 128],
                    lm_sb[bh:bh + 64, tile * 512 + 256: tile * 512 + 512], start=True, stop=True))
                em.op('dve', lambda e: e.reduce_max(out=mc[:, 0:1], in_=pl2[:], axis=AX.X, negate=True))
                em.op('act', lambda e, p=p, it=it: e.activation(out=a2buf[:, p * 512 + it * 256: p * 512 + (it + 1) * 256], in_=pl2[:], func=AF.Exp, bias=mc[:, 0:1]))
                em.op('dve', lambda e, p=p, it=it: e.reduce_sum(out=mc[:, 1:2], in_=a2buf[:, p * 512 + it * 256: p * 512 + (it + 1) * 256], axis=AX.X))
                em.op('dve', lambda e: e.reciprocal(out=mc[:, 1:2], in_=mc[:, 1:2]))
                em.op('dve', lambda e, p=p, it=it: e.tensor_scalar(out=a2buf[:, p * 512 + it * 256: p * 512 + (it + 1) * 256], in0=a2buf[:, p * 512 + it * 256: p * 512 + (it + 1) * 256], scalar1=mc[:, 1:2], scalar2=None, op0=ALU.mult))
                em.op('dve', lambda e, p=p, it=it: e.reduce_sum(out=rsbuf[:, p * 2 + it:p * 2 + it + 1], in_=a2buf[:, p * 512 + it * 256: p * 512 + (it + 1) * 256], axis=AX.X))
            for it in range(2):
                em.op('pe', lambda e, p=p, it=it: e.matmul(pcs[0:1, :], ones[:, 0:1], a2buf[:, p * 512 + it * 256: p * 512 + (it + 1) * 256], start=(it == 0), stop=(it == 1)))
            em.op('dve', lambda e, p=p: e.tensor_copy(out=csbuf[0:1, p * 256:(p + 1) * 256], in_=pcs[0:1, :]))
        em.op('dve', lambda e: e.reduce_max(out=rsbuf[:, 32:33], in_=rsbuf[:, 0:32], axis=AX.X))
        em.op('pe', lambda e: e.transpose(pcs[0:1, 0:128], rsbuf[:, 32:33], identf[:]))
        em.op('dve', lambda e: e.reduce_max(out=sc[0:1, 0:1], in_=pcs[0:1, 0:128], axis=AX.X))
        em.op('dve', lambda e: e.reduce_max(out=sc[0:1, 1:2], in_=csbuf[0:1, :], axis=AX.X))
        em.op('dve', lambda e: e.tensor_tensor(out=sc[0:1, 0:1], in0=sc[0:1, 0:1], in1=sc[0:1, 1:2], op=ALU.mult))
        em.op('dve', lambda e: e.reciprocal(out=sc[0:1, 0:1], in_=sc[0:1, 0:1]))
        em.op('pe', lambda e: e.matmul(pl2[:, 0:1], ones[0:1, :], sc[0:1, 0:1], start=True, stop=True))
        em.op('dve', lambda e: e.tensor_copy(out=rsc_col[:], in_=pl2[:, 0:1]))

    xb_ = cx.sb("xb_", [128, 512], F32)
    em.op('dve', lambda e: e.tensor_copy(out=xb_[:], in_=a2buf[:, 0:512]))
    xT_ = cx.sb("xT_", [128, 512], F32)
    zb_ = cx.sb("zb_", [128, 512], F32)
    zT_ = cx.sb("zT_", [128, 512], F32)
    xz_ = cx.sb("xz_", [128, 512], F32)
    xzT_ = cx.sb("xzT_", [128, 512], F32)
    t1_ = cx.sb("t1_", [128, 512], F32)
    t2_ = cx.sb("t2_", [128, 512], F32)

    with contextlib.ExitStack() as st:
        ptA = st.enter_context(nc.psum_tensor("ptA", [128, 256], F32))
        ptB = st.enter_context(nc.psum_tensor("ptB", [128, 128], F32))

        def T256(dst, src):
            for a_ in range(2):
                for b_ in range(2):
                    em.op('pe', lambda e, a_=a_, b_=b_: e.transpose(ptB[:, :], src[:, a_ * 256 + b_ * 128: a_ * 256 + (b_ + 1) * 128], identf[:]))
                    em.op('dve', lambda e, a_=a_, b_=b_: e.tensor_copy(out=dst[:, b_ * 256 + a_ * 128: b_ * 256 + (a_ + 1) * 128], in_=ptB[:, :]))

        def MM(dst, lhsT_buf, rhs_buf, scale=1.0, diag=None):
            for it in range(2):
                for kt in range(2):
                    em.op('pe', lambda e, it=it, kt=kt: e.matmul(
                        ptA[:, :], lhsT_buf[:, kt * 256 + it * 128: kt * 256 + (it + 1) * 128],
                        rhs_buf[:, kt * 256: (kt + 1) * 256], start=(kt == 0), stop=(kt == 1)))
                if diag is None:
                    em.op('act', lambda e, it=it, scale=scale: e.activation(out=dst[:, it * 256:(it + 1) * 256], in_=ptA[:, :], func=AF.Copy, scale=scale))
                else:
                    em.op('act', lambda e, it=it: e.activation(out=dst[:, it * 256:(it + 1) * 256], in_=ptA[:, :], func=AF.Copy, scale=-1.0))
                    em.op('dve', lambda e, it=it, diag=diag: e.scalar_tensor_tensor(
                        out=dst[:, it * 256 + it * 128: it * 256 + (it + 1) * 128], in0=identf[:], scalar=float(diag),
                        in1=dst[:, it * 256 + it * 128: it * 256 + (it + 1) * 128], op0=ALU.mult, op1=ALU.add))

        T256(xT_, xb_)
        em.op('dve', lambda e: e.tensor_scalar(out=zb_[:], in0=xT_[:], scalar1=rsc_col[:, 0:1], scalar2=None, op0=ALU.mult))
        for _i in range(6):
            MM(xz_, xT_, zb_)
            T256(xzT_, xz_)
            em.op('act', lambda e: e.activation(out=t1_[:], in_=xz_[:], func=AF.Copy, scale=-1.0))
            for it in range(2):
                em.op('dve', lambda e, it=it: e.scalar_tensor_tensor(
                    out=t1_[:, it * 256 + it * 128: it * 256 + (it + 1) * 128], in0=identf[:], scalar=7.0,
                    in1=t1_[:, it * 256 + it * 128: it * 256 + (it + 1) * 128], op0=ALU.mult, op1=ALU.add))
            MM(t2_, xzT_, t1_, diag=15.0)
            MM(t1_, xzT_, t2_, diag=13.0)
            T256(zT_, zb_)
            MM(zb_, zT_, t1_, scale=0.25)
    with contextlib.ExitStack() as st:
        sb = lambda n, s, dt=F32: st.enter_context(nc.sbuf_tensor(n, s, dt))
        ptm = st.enter_context(nc.psum_tensor("ptm3", [128, 512], F32))
        pst = st.enter_context(nc.psum_tensor("pst3", [128, 256], F32))
        pz3 = st.enter_context(nc.psum_tensor("pz3", [64, 256], F32))
        pzz = st.enter_context(nc.psum_tensor("pzz", [1, 256], F32))
        ptr_ = st.enter_context(nc.psum_tensor("ptr_", [1, 128], F32))
        pzb3 = st.enter_context(nc.psum_tensor("pzb3", [64, 256], F32))
        mcols = sb("mcols", [128, 18])
        mrow = sb("mrow", [1, 256])
        expt = sb("expt", [128, 256], BF16)
        z3f = sb("z3f", [64, 256])
        z3n = sb("z3n", [128, 128])
        zbc3 = sb("zbc3", [64, 256])
        zr3 = sb("zr3", [1, 256])
        for it in range(2):
            for jc in range(16):
                em.op('pe', lambda e, it=it, jc=jc: e.matmul(
                    ptm[:, :], lm_sb[0:64, it * 128:(it + 1) * 128], k_sb[0:64, jc * 512:(jc + 1) * 512], start=True, stop=True))
                em.op('dve', lambda e, jc=jc: e.reduce_max(out=mcols[:, jc:jc + 1], in_=ptm[:], axis=AX.X))
            em.op('dve', lambda e: e.reduce_max(out=mcols[:, 16:17], in_=mcols[:, 0:16], axis=AX.X, negate=True))
            em.op('pe', lambda e: e.transpose(ptr_[0:1, 0:128], mcols[:, 16:17], identf[:]))
            em.op('dve', lambda e, it=it: e.tensor_copy(out=mrow[0:1, it * 128:(it + 1) * 128], in_=ptr_[0:1, 0:128]))
        for jt in range(64):
            em.op('pe', lambda e, jt=jt: e.matmul(pst[:, :], k_sb[0:64, jt * 128:(jt + 1) * 128], lm_sb[0:64, 0:256], start=True, stop=False))
            em.op('pe', lambda e: e.matmul(pst[:, :], ones[0:1, :], mrow[0:1, :], start=False, stop=True))
            em.op('act', lambda e: e.activation(out=expt[:], in_=pst[:], func=AF.Exp))
            em.op('pe', lambda e, jt=jt: e.matmul(pz3[0:64, :], v_sb[:, jt * 64:(jt + 1) * 64], expt[:], start=(jt == 0), stop=(jt == 63)))
            em.op('pe', lambda e, jt=jt: e.matmul(pzz[0:1, :], ones_bf[:, 0:1], expt[:], start=(jt == 0), stop=(jt == 63)))
        em.op('dve', lambda e: e.tensor_copy(out=zr3[0:1, :], in_=pzz[0:1, :]))
        em.op('dve', lambda e: e.reciprocal(out=zr3[0:1, :], in_=zr3[0:1, :]))
        em.op('pe', lambda e: e.matmul(pzb3[0:64, :], ones[0:1, 0:64], zr3[0:1, :], start=True, stop=True))
        em.op('dve', lambda e: e.tensor_copy(out=zbc3[0:64, :], in_=pzb3[0:64, :]))
        em.op('dve', lambda e: e.tensor_copy(out=z3f[0:64, :], in_=pz3[0:64, :]))
        em.op('dve', lambda e: e.tensor_tensor(out=z3f[0:64, :], in0=z3f[0:64, :], in1=zbc3[0:64, :], op=ALU.mult))
        for l in range(2):
            em.op('pe', lambda e, l=l: e.transpose(ptm[:, 0:64], z3f[0:64, l * 128:(l + 1) * 128], identf[0:64, 0:64]))
            em.op('dve', lambda e, l=l: e.tensor_copy(out=z3n[:, l * 64:(l + 1) * 64], in_=ptm[:, 0:64]))
        z2b = sb("z2b", [128, 128])
        T256(xT_, zb_)
        for it in range(2):
            for kt in range(2):
                em.op('pe', lambda e, it=it, kt=kt: e.matmul(
                    pst[:, 0:64], xT_[:, kt * 256 + it * 128: kt * 256 + (it + 1) * 128], z3n[:, kt * 64:(kt + 1) * 64], start=(kt == 0), stop=(kt == 1)))
            em.op('dve', lambda e, it=it: e.tensor_copy(out=z2b[:, it * 64:(it + 1) * 64], in_=pst[:, 0:64]))
        em.dma('sp', lambda e: e.dma_start(out=z2_o[:, :], in_=z2b[:]))
    em.build()
    cx.st.close()
    return nc



def build_b1post():
    nc = bass.Bass("TRN2", target_bir_lowering=False)
    cx = Ctx(nc)
    em = cx.em
    import contextlib
    ap = lambda n, s, dt=F32, kind="ExternalInput": nc.dram_tensor(n, s, dt, kind=kind).ap()
    z2_in = ap("z2_in", [128, 8 * 128])
    q_in = ap("q_in", [512, TC])
    kl_in = ap("kl_in", [128, 4 * 256])
    v_in = ap("v_in", [512, TC + 32])
    cw_in = ap("cw_in", [128, 4 * 33])
    outw = ap("outw", [8, 4, 128, 128])
    outb = ap("outb", [128, 8])
    xn1T = ap("xn1T", [D, TC])
    wq_in = ap("wq_in", [8, 128, 256])
    wk_in = ap("wk_in", [8, 128, 256])
    enc_out = ap("enc_o", [D, TC], kind="ExternalOutput")
    qe_out = ap("qe_o", [TC, 256], kind="ExternalOutput")
    ke_out = ap("ke_o", [TC, 256], kind="ExternalOutput")

    ones = cx.ones_col()
    identf = cx.ident(F32)
    ones_bf = cx.sb("ones_bf", [128, 1], BF16)
    em.op('pool', lambda e: e.memset(ones_bf[:], 1.0))

    z2_sb = cx.sb("z2_sb", [128, 8 * 128], BF16)
    em.dma('pool', lambda e: e.dma_start(out=z2_sb[:], in_=z2_in[:, :]))
    qb = cx.sb("qb", [128, 4 * TC], BF16)
    for g in range(4):
        em.dma('pool', lambda e, g=g: e.dma_start(out=qb[:, g * TC:(g + 1) * TC], in_=q_in[g * 128:(g + 1) * 128, :]))
    kl_sb = cx.sb("kl_sb", [128, 4 * 256], BF16)
    em.dma('pool', lambda e: e.dma_start(out=kl_sb[:], in_=kl_in[:, :]))
    vb = cx.sb("vb", [128, 4 * (TC + 32)], F32)
    for g in range(4):
        em.dma('sp', lambda e, g=g: e.dma_start(out=vb[:, g * (TC + 32):(g + 1) * (TC + 32)], in_=v_in[g * 128:(g + 1) * 128, :]))
    cw_sb = cx.sb("cw_sb", [128, 4 * 33], F32)
    em.dma('sp', lambda e: e.dma_start(out=cw_sb[:], in_=cw_in[:, :]))
    ob_sb = cx.sb("ob_sb", [128, 8], F32)
    em.dma('sp', lambda e: e.dma_start(out=ob_sb[:], in_=outb[:, :]))

    nyT = cx.sb("nyT", [128, 4 * TC], BF16)
    with contextlib.ExitStack() as st:
        sb = lambda n, s, dt=F32: st.enter_context(nc.sbuf_tensor(n, s, dt))
        cacc = sb("cacc", [128, TC])
        ptm = st.enter_context(nc.psum_tensor("ptm4", [128, 512], F32))
        pst = st.enter_context(nc.psum_tensor("pst4", [128, 512], F32))
        pat = st.enter_context(nc.psum_tensor("pat4", [128, TC], F32))
        pz = st.enter_context(nc.psum_tensor("pz4", [1, 512], F32))
        pzb = st.enter_context(nc.psum_tensor("pzb4", [128, 512], F32))
        mc = sb("mc4", [128, 16])
        mrow = sb("mrow4", [1, TC])
        a1t = sb("a1t", [128, 2 * 512], BF16)
        zrow = sb("zrow4", [128, TC])
        zbcs = sb("zbcs4", [128, 512])
        tno = sb("tno", [128, 512])
        for h in range(8):
            g, bh = h // 2, (h % 2) * 64
            for it in range(16):
                em.op('pe', lambda e, g=g, bh=bh, it=it: e.matmul(
                    ptm[:, 0:256], qb[bh:bh + 64, g * TC + it * 128: g * TC + (it + 1) * 128],
                    kl_sb[bh:bh + 64, g * 256:(g + 1) * 256], start=True, stop=True))
                em.op('dve', lambda e, it=it: e.reduce_max(out=mc[:, 0:1], in_=ptm[:, 0:256], axis=AX.X, negate=True))
                em.op('pe', lambda e: e.transpose(pz[0:1, 0:128], mc[:, 0:1], identf[:]))
                em.op('dve', lambda e, it=it: e.tensor_copy(out=mrow[0:1, it * 128:(it + 1) * 128], in_=pz[0:1, 0:128]))
            for (off, w) in _chunks(TC, 512):
                for lt in range(2):
                    em.op('pe', lambda e, g=g, bh=bh, lt=lt, off=off, w=w: e.matmul(
                        pst[:, 0:w], kl_sb[bh:bh + 64, g * 256 + lt * 128: g * 256 + (lt + 1) * 128],
                        qb[bh:bh + 64, g * TC + off: g * TC + off + w], start=True, stop=False))
                    em.op('pe', lambda e, lt=lt, off=off, w=w: e.matmul(
                        pst[:, 0:w], ones[0:1, :], mrow[0:1, off:off + w], start=False, stop=True))
                    em.op('act', lambda e, lt=lt, w=w: e.activation(out=a1t[:, lt * 512:lt * 512 + w], in_=pst[:, 0:w], func=AF.Exp))
                for lt in range(2):
                    em.op('pe', lambda e, lt=lt, h=h, bh=bh, off=off, w=w: e.matmul(
                        pat[bh:bh + 64, off:off + w], z2_sb[:, (h * 2 + lt) * 64:(h * 2 + lt) * 64 + 64],
                        a1t[:, lt * 512:lt * 512 + w], start=(lt == 0), stop=(lt == 1)))
                    em.op('pe', lambda e, lt=lt, off=off, w=w: e.matmul(
                        pz[0:1, 0:w], ones_bf[:, 0:1], a1t[:, lt * 512:lt * 512 + w], start=(lt == 0), stop=(lt == 1)))
                em.op('dve', lambda e, off=off, w=w: e.tensor_copy(out=zrow[0:1, off:off + w], in_=pz[0:1, 0:w]))
            em.op('dve', lambda e: e.reciprocal(out=zrow[0:1, :], in_=zrow[0:1, :]))
            if h % 2 == 1:
                for half in range(2):
                    bh2 = half * 64
                    pass
            for (off, w) in _chunks(TC, 512):
                em.op('pe', lambda e, off=off, w=w, bh=bh: e.matmul(pzb[bh:bh + 64, 0:w], ones[0:1, 0:64], zrow[0:1, off:off + w], start=True, stop=True))
                em.op('dve', lambda e, off=off, w=w, bh=bh: e.tensor_copy(out=zbcs[bh:bh + 64, 0:w], in_=pzb[bh:bh + 64, 0:w]))
                em.op('dve', lambda e, off=off, w=w, bh=bh, g=g: e.scalar_tensor_tensor(
                    out=cacc[bh:bh + 64, off:off + w], in0=vb[bh:bh + 64, g * (TC + 32) + off: g * (TC + 32) + off + w],
                    scalar=cw_sb[bh:bh + 64, g * 33:g * 33 + 1], in1=vb[bh:bh + 64, g * (TC + 32) + off: g * (TC + 32) + off + w], op0=ALU.mult, op1=ALU.bypass))
                for k in range(1, 33):
                    em.op('dve', lambda e, off=off, w=w, bh=bh, g=g, k=k: e.scalar_tensor_tensor(
                        out=cacc[bh:bh + 64, off:off + w], in0=vb[bh:bh + 64, g * (TC + 32) + off + k: g * (TC + 32) + off + k + w],
                        scalar=cw_sb[bh:bh + 64, g * 33 + k:g * 33 + k + 1], in1=cacc[bh:bh + 64, off:off + w], op0=ALU.mult, op1=ALU.add))
                em.op('dve', lambda e, off=off, w=w, bh=bh: e.tensor_tensor(out=tno[bh:bh + 64, 0:w], in0=pat[bh:bh + 64, off:off + w], in1=zbcs[bh:bh + 64, 0:w], op=ALU.mult))
                em.op('dve', lambda e, off=off, w=w, bh=bh, g=g: e.tensor_tensor(out=nyT[bh:bh + 64, g * TC + off: g * TC + off + w], in0=tno[bh:bh + 64, 0:w], in1=cacc[bh:bh + 64, off:off + w], op=ALU.add))

    encb = cx.sb("encb", [128, 8 * TC], BF16)
    encf_sp = nc.dram_tensor("encf_sp", [D, TC], F32, kind="Internal").ap()
    with contextlib.ExitStack() as st:
        sb = lambda n, s, dt=F32: st.enter_context(nc.sbuf_tensor(n, s, dt))
        wt = sb("wt5", [128, 4 * 128], BF16)
        xnf = sb("xnf5", [128, 512])
        ef = sb("ef5", [128, 512])
        pxg = st.enter_context(nc.psum_tensor("pxg", [128, 512], F32))
        for ot in range(8):
            em.dma('pool', lambda e, ot=ot: e.dma_start(out=wt.ap().rearrange("p (a f) -> p a f", a=4), in_=outw[ot].rearrange("a p f -> p a f")))
            for (off, w) in _chunks(TC, 512):
                for it in range(4):
                    em.op('pe', lambda e, ot=ot, off=off, w=w, it=it: e.matmul(
                        pxg[:, 0:w], wt[:, it * 128:(it + 1) * 128], nyT[:, it * TC + off:it * TC + off + w],
                        start=(it == 0), stop=(it == 3)))
                em.dma('sp', lambda e, ot=ot, off=off, w=w: e.dma_start(out=xnf[:, 0:w], in_=xn1T[ot * 128:(ot + 1) * 128, off:off + w]))
                em.op('dve', lambda e, ot=ot, off=off, w=w: e.tensor_scalar(out=ef[:, 0:w], in0=pxg[:, 0:w], scalar1=ob_sb[:, ot:ot + 1], scalar2=None, op0=ALU.add))
                em.op('dve', lambda e, off=off, w=w: e.tensor_tensor(out=ef[:, 0:w], in0=ef[:, 0:w], in1=xnf[:, 0:w], op=ALU.add))
                em.dma('sp', lambda e, ot=ot, off=off, w=w: e.dma_start(out=enc_out[ot * 128:(ot + 1) * 128, off:off + w], in_=ef[:, 0:w]))
                em.dma('sp', lambda e, ot=ot, off=off, w=w: e.dma_start(out=encf_sp[ot * 128:(ot + 1) * 128, off:off + w], in_=ef[:, 0:w]))

    with contextlib.ExitStack() as st:
        sb = lambda n, s, dt=F32: st.enter_context(nc.sbuf_tensor(n, s, dt))
        wq_sb = sb("wq_sb", [128, 8 * 256])
        wk_sb = sb("wk_sb", [128, 8 * 256])
        enct = sb("enct", [128, 8 * 128])
        qe = sb("qe", [128, 256])
        pqe = st.enter_context(nc.psum_tensor("pqe", [128, 256], F32))
        pke = st.enter_context(nc.psum_tensor("pke", [128, 256], F32))
        for i in range(8):
            em.dma('sp', lambda e, i=i: e.dma_start(out=wq_sb[:, i * 256:(i + 1) * 256], in_=wq_in[i]))
            em.dma('sp', lambda e, i=i: e.dma_start(out=wk_sb[:, i * 256:(i + 1) * 256], in_=wk_in[i]))
        for tt in range(16):
            for i in range(8):
                em.dma('sp', lambda e, tt=tt, i=i: e.dma_start(out=enct[:, i * 128:(i + 1) * 128], in_=encf_sp[i * 128:(i + 1) * 128, tt * 128:(tt + 1) * 128]))
            for i in range(8):
                em.op('pe', lambda e, tt=tt, i=i: e.matmul(pqe[:], enct[:, i * 128:(i + 1) * 128], wq_sb[:, i * 256:(i + 1) * 256], start=(i == 0), stop=(i == 7)))
            for i in range(8):
                em.op('pe', lambda e, tt=tt, i=i: e.matmul(pke[:], enct[:, i * 128:(i + 1) * 128], wk_sb[:, i * 256:(i + 1) * 256], start=(i == 0), stop=(i == 7)))
            em.op('dve', lambda e: e.tensor_copy(out=qe[:], in_=pqe[:]))
            em.dma('sp', lambda e, tt=tt: e.dma_start(out=qe_out[tt * 128:(tt + 1) * 128, :], in_=qe[:]))
            em.op('dve', lambda e: e.tensor_copy(out=qe[:], in_=pke[:]))
            em.dma('sp', lambda e, tt=tt: e.dma_start(out=ke_out[tt * 128:(tt + 1) * 128, :], in_=qe[:]))
    em.build()
    cx.st.close()
    return nc



NEDGE = 16384
NG = NEDGE // 128


def build_b1edge():
    nc = bass.Bass("TRN2", target_bir_lowering=False)
    cx = Ctx(nc)
    em = cx.em
    import contextlib
    ap = lambda n, s, dt=F32, kind="ExternalInput": nc.dram_tensor(n, s, dt, kind=kind).ap()
    qe_all = ap("qe_all", [T, 256])
    ke_all = ap("ke_all", [T, 256])
    ridx = ap("ridx", [128, NG], U32)
    cidx = ap("cidx", [128, NG], U32)
    vals = ap("vals", [128, NG])
    apart = ap("apart", [128, 64], kind="ExternalOutput")

    i64hi = cx.sb("i64hi", [128, 128], F32)
    i64lo = cx.sb("i64lo", [128, 64], F32)
    em.op('pool', lambda e: e.iota(i64hi[:], [[64, 128]], base=0, channel_multiplier=0, allow_small_or_imprecise_dtypes=True))
    em.op('pool', lambda e: e.iota(i64lo[:], [[1, 64]], base=0, channel_multiplier=0, allow_small_or_imprecise_dtypes=True))

    rix = cx.sb("rix", [128, NG], U32)
    cix = cx.sb("cix", [128, NG], U32)
    vls = cx.sb("vls", [128, NG], F32)
    rfl = cx.sb("rfl", [128, NG], F32)
    em.dma('sp', lambda e: e.dma_start(out=rix[:], in_=ridx[:, :]))
    em.dma('sp', lambda e: e.dma_start(out=cix[:], in_=cidx[:, :]))
    em.dma('sp', lambda e: e.dma_start(out=vls[:], in_=vals[:, :]))
    em.op('dve', lambda e: e.tensor_copy(out=rfl[:], in_=rix[:]))

    with contextlib.ExitStack() as st:
        sb = lambda n, s, dt=F32: st.enter_context(nc.sbuf_tensor(n, s, dt))
        qg = sb("qg", [128, 256]); kg = sb("kg", [128, 256])
        dcol = sb("dcol", [128, 4])
        dtile = sb("dtile", [128, 128]); m1 = sb("m1", [128, 128])
        ohhi = sb("ohhi", [128, 128]); ohlo = sb("ohlo", [128, 64])
        trash = sb("trash", [128, 256])
        pA = st.enter_context(nc.psum_tensor("pA", [128, 64], F32))
        for g_ in range(NG):
            em.dma('pool', lambda e, g_=g_: e.indirect_dma_start(out=qg[:], out_offset=None, in_=qe_all[:, :], in_offset=IndirectOffsetOnAxis(ap=rix[:, g_:g_ + 1], axis=0)))
            em.dma('pool', lambda e, g_=g_: e.indirect_dma_start(out=kg[:], out_offset=None, in_=ke_all[:, :], in_offset=IndirectOffsetOnAxis(ap=cix[:, g_:g_ + 1], axis=0)))
            em.op('dve', lambda e: e.scalar_tensor_tensor(out=trash[:], in0=qg[:], scalar=1.0, in1=kg[:], op0=ALU.mult, op1=ALU.mult, accum_out=dcol[:, 0:1]))
            em.op('dve', lambda e, g_=g_: e.tensor_scalar(out=dcol[:, 1:2], in0=dcol[:, 0:1], scalar1=vls[:, g_:g_ + 1], scalar2=0.0625, op0=ALU.mult, op1=ALU.mult))
            em.op('dve', lambda e, g_=g_: e.tensor_scalar(out=dtile[:], in0=i64hi[:], scalar1=rfl[:, g_:g_ + 1], scalar2=None, op0=ALU.subtract))
            em.op('dve', lambda e: e.tensor_scalar(out=m1[:], in0=dtile[:], scalar1=-63.5, scalar2=None, op0=ALU.is_ge))
            em.op('dve', lambda e: e.scalar_tensor_tensor(out=ohhi[:], in0=dtile[:], scalar=0.5, in1=m1[:], op0=ALU.is_lt, op1=ALU.mult))
            em.op('dve', lambda e: e.scalar_tensor_tensor(out=m1[:], in0=ohhi[:], scalar=1.0, in1=i64hi[:], op0=ALU.mult, op1=ALU.mult, accum_out=dcol[:, 2:3]))
            em.op('dve', lambda e, g_=g_: e.tensor_scalar(out=dcol[:, 3:4], in0=rfl[:, g_:g_ + 1], scalar1=dcol[:, 2:3], scalar2=None, op0=ALU.subtract))
            em.op('dve', lambda e: e.tensor_scalar(out=ohlo[:], in0=i64lo[:], scalar1=dcol[:, 3:4], scalar2=None, op0=ALU.is_equal))
            em.op('dve', lambda e: e.tensor_scalar(out=ohlo[:], in0=ohlo[:], scalar1=dcol[:, 1:2], scalar2=None, op0=ALU.mult))
            em.op('pe', lambda e, g_=g_: e.matmul(pA[:], ohhi[:], ohlo[:], start=(g_ == 0), stop=(g_ == NG - 1)))
        acp = sb("acp", [128, 64])
        em.op('dve', lambda e: e.tensor_copy(out=acp[:], in_=pA[:]))
        em.dma('sp', lambda e: e.dma_start(out=apart[:, :], in_=acp[:]))
    em.build()
    cx.st.close()
    return nc



def build_b1mix():
    nc = bass.Bass("TRN2", target_bir_lowering=False)
    cx = Ctx(nc)
    em = cx.em
    import contextlib
    ap = lambda n, s, dt=F32, kind="ExternalInput": nc.dram_tensor(n, s, dt, kind=kind).ap()
    a_all = ap("a_all", [8, T])
    a_loc = ap("a_loc", [8, TC])
    valT = ap("valT", [D, TC])
    encT = ap("encT", [D, TC])
    s1_out = ap("s1_o", [D, TC], kind="ExternalOutput")

    ones = cx.ones_col()
    a_sb = cx.sb("a_sb", [8, T], F32)
    em.dma('sp', lambda e: e.dma_start(out=a_sb[:], in_=a_all[:, :]))
    al_sb = cx.sb("al_sb", [8, TC], F32)
    em.dma('sp', lambda e: e.dma_start(out=al_sb[:], in_=a_loc[:, :]))
    srow = cx.sb("srow", [1, T], F32)
    lrow = cx.sb("lrow", [1, TC], F32)
    stat = cx.sb("stat", [1, 4], F32)

    with contextlib.ExitStack() as st:
        sb = lambda n, s, dt=F32: st.enter_context(nc.sbuf_tensor(n, s, dt))
        pr_ = st.enter_context(nc.psum_tensor("pr5", [1, 512], F32))
        pb_ = st.enter_context(nc.psum_tensor("pb5", [128, 512], F32))
        for (off, w) in _chunks(T, 512):
            em.op('pe', lambda e, off=off, w=w: e.matmul(pr_[0:1, 0:w], ones[0:8, 0:1], a_sb[0:8, off:off + w], start=True, stop=True))
            em.op('dve', lambda e, off=off, w=w: e.tensor_copy(out=srow[0:1, off:off + w], in_=pr_[0:1, 0:w]))
        for (off, w) in _chunks(TC, 512):
            em.op('pe', lambda e, off=off, w=w: e.matmul(pr_[0:1, 0:w], ones[0:8, 0:1], al_sb[0:8, off:off + w], start=True, stop=True))
            em.op('dve', lambda e, off=off, w=w: e.tensor_copy(out=lrow[0:1, off:off + w], in_=pr_[0:1, 0:w]))
        em.op('dve', lambda e: e.reduce_max(out=stat[0:1, 0:1], in_=srow[0:1, :], axis=AX.X, negate=True))
        em.op('act', lambda e: e.activation(out=srow[0:1, :], in_=srow[0:1, :], func=AF.Exp, bias=stat[0:1, 0:1]))
        em.op('dve', lambda e: e.reduce_sum(out=stat[0:1, 1:2], in_=srow[0:1, :], axis=AX.X))
        em.op('dve', lambda e: e.reciprocal(out=stat[0:1, 1:2], in_=stat[0:1, 1:2]))
        em.op('act', lambda e: e.activation(out=lrow[0:1, :], in_=lrow[0:1, :], func=AF.Exp, bias=stat[0:1, 0:1]))
        em.op('dve', lambda e: e.tensor_scalar(out=lrow[0:1, :], in0=lrow[0:1, :], scalar1=stat[0:1, 1:2], scalar2=None, op0=ALU.mult))
        vf = sb("vf", [128, 512]); ef = sb("ef", [128, 512])
        xl = sb("xl", [128, 512]); sw = sb("sw", [128, 512]); t1 = sb("t1", [128, 512])
        for i in range(8):
            for (off, w) in _chunks(TC, 512):
                em.dma('sp', lambda e, i=i, off=off, w=w: e.dma_start(out=vf[:, 0:w], in_=valT[i * 128:(i + 1) * 128, off:off + w]))
                em.dma('sp', lambda e, i=i, off=off, w=w: e.dma_start(out=ef[:, 0:w], in_=encT[i * 128:(i + 1) * 128, off:off + w]))
                em.op('pe', lambda e, off=off, w=w: e.matmul(pb_[:, 0:w], ones[0:1, :], lrow[0:1, off:off + w], start=True, stop=True))
                em.op('dve', lambda e, w=w: e.tensor_tensor(out=xl[:, 0:w], in0=vf[:, 0:w], in1=pb_[:, 0:w], op=ALU.mult))
                em.op('act', lambda e, w=w: e.activation(out=sw[:, 0:w], in_=xl[:, 0:w], func=AF.Sigmoid, scale=-1.0))
                em.op('act', lambda e, w=w: e.activation(out=sw[:, 0:w], in_=sw[:, 0:w], func=AF.Square))
                em.op('dve', lambda e, w=w: e.tensor_tensor(out=t1[:, 0:w], in0=xl[:, 0:w], in1=ef[:, 0:w], op=ALU.subtract))
                em.op('dve', lambda e, w=w: e.tensor_tensor(out=t1[:, 0:w], in0=t1[:, 0:w], in1=sw[:, 0:w], op=ALU.mult))
                em.op('dve', lambda e, w=w: e.tensor_tensor(out=t1[:, 0:w], in0=t1[:, 0:w], in1=ef[:, 0:w], op=ALU.add))
                em.op('dve', lambda e, w=w: e.tensor_scalar(out=t1[:, 0:w], in0=t1[:, 0:w], scalar1=2.0, scalar2=None, op0=ALU.mult))
                em.dma('sp', lambda e, i=i, off=off, w=w: e.dma_start(out=s1_out[i * 128:(i + 1) * 128, off:off + w], in_=t1[:, 0:w]))
    em.build()
    cx.st.close()
    return nc



def build_final():
    nc = bass.Bass("TRN2", target_bir_lowering=False)
    cx = Ctx(nc)
    em = cx.em
    import contextlib
    ap = lambda n, s, dt=F32, kind="ExternalInput": nc.dram_tensor(n, s, dt, kind=kind).ap()
    x1T = ap("x1T", [D, TC])
    s1T = ap("s1T", [D, TC])
    y6T = ap("y6T", [D, TC])
    out = ap("out", [D, TC], kind="ExternalOutput")
    with contextlib.ExitStack() as st:
        sb = lambda n, s, dt=F32: st.enter_context(nc.sbuf_tensor(n, s, dt))
        a = sb("a6", [128, TC]); b = sb("b6", [128, TC]); c = sb("c6", [128, TC])
        for i in range(8):
            em.dma('sp', lambda e, i=i: e.dma_start(out=a[:], in_=y6T[i * 128:(i + 1) * 128, :]))
            em.dma('sp', lambda e, i=i: e.dma_start(out=b[:], in_=s1T[i * 128:(i + 1) * 128, :]))
            em.dma('sp', lambda e, i=i: e.dma_start(out=c[:], in_=x1T[i * 128:(i + 1) * 128, :]))
            em.op('act', lambda e: e.activation(out=a[:], in_=a[:], func=AF.Silu))
            em.op('dve', lambda e: e.tensor_tensor(out=a[:], in0=a[:], in1=b[:], op=ALU.mult))
            em.op('dve', lambda e: e.tensor_tensor(out=a[:], in0=a[:], in1=c[:], op=ALU.add))
            em.dma('sp', lambda e, i=i: e.dma_start(out=out[i * 128:(i + 1) * 128, :], in_=a[:]))
    em.build()
    cx.st.close()
    return nc



def _asm_fm(parts, batch_rows=True):
    return np.concatenate(parts[0:4], axis=1)


def _run_b1(inp, x1, prf):
    pr1 = _prep_b1(inp)
    res_pre = _run_b1pre(x1, pr1)
    xn1 = [res_pre[c]["xn1"] for c in range(NC)]
    val = [res_pre[c]["val_o"] for c in range(NC)]
    q_b0 = np.concatenate([res_pre[c]["q_o"] for c in range(4)], axis=1)
    k_b0 = np.concatenate([res_pre[c]["k_o"] for c in range(4)], axis=1)
    v_b0 = np.concatenate([res_pre[c]["v_o"] for c in range(4)], axis=1)
    ql = np.zeros((2, 8, 64, 256), np.float32)
    kl = np.zeros((2, 8, 64, 256), np.float32)
    for c in range(NC):
        b, q = c // 4, c % 4
        lm = res_pre[c]["lm_o"]
        ql[b, :, :, q * 64:(q + 1) * 64] = lm[:, 0:64].reshape(8, 64, 64)
        kl[b, :, :, q * 64:(q + 1) * 64] = lm[:, 64:128].reshape(8, 64, 64)

    nc_mid = build_b1mid()
    maps = []
    for c in range(NC):
        hsel = c
        pairs = [(0, hsel)] + [(b, h) for b in range(2) for h in range(8) if (b, h) != (0, hsel)]
        lm_in = np.zeros((128, 8 * 512), np.float32)
        for p, (b, h) in enumerate(pairs):
            tile, bh = p // 2, (p % 2) * 64
            lm_in[bh:bh + 64, tile * 512:tile * 512 + 256] = ql[b, h]
            lm_in[bh:bh + 64, tile * 512 + 256:tile * 512 + 512] = kl[b, h]
        maps.append({"lm_in": np.ascontiguousarray(lm_in),
                     "k_head": np.ascontiguousarray(k_b0[hsel * 64:(hsel + 1) * 64, :]),
                     "v_head": np.ascontiguousarray(v_b0[hsel * 64:(hsel + 1) * 64, :].T)})
    r = run_bass_kernel_spmd(nc_mid, maps, core_ids=list(range(NC)))
    z2 = [r.results[c]["z2_o"] for c in range(NC)]

    nc_post = build_b1post()
    z2_in = np.concatenate(z2, axis=1)
    cw = inp["L1_attn_conv_w"][:, 0, :, 0]
    cw_in = np.zeros((128, 4 * 33), np.float32)
    for g in range(4):
        cw_in[0:64, g * 33:(g + 1) * 33] = cw[2 * g][None, :]
        cw_in[64:128, g * 33:(g + 1) * 33] = cw[2 * g + 1][None, :]
    outw = inp["L1_attn_out_w"].reshape(4, 128, 8, 128).transpose(2, 0, 1, 3)
    outb = inp["L1_attn_out_b"].reshape(8, 128).T
    wq_in = inp["L1_attn_wq"].reshape(8, 128, 256)
    wk_in = inp["L1_attn_wk"].reshape(8, 128, 256)
    maps = []
    for c in range(NC):
        b, q = c // 4, c % 4
        t0 = q * TC
        lo, hi = t0 - 16, t0 + TC + 16
        vsl = v_b0[:, max(lo, 0):min(hi, T)]
        if lo < 0:
            vsl = np.concatenate([np.zeros((512, -lo), np.float32), vsl], 1)
        if hi > T:
            vsl = np.concatenate([vsl, np.zeros((512, hi - T), np.float32)], 1)
        kl_f = np.zeros((128, 4 * 256), np.float32)
        klb0 = kl[0].reshape(512, 256)
        for g in range(4):
            kl_f[:, g * 256:(g + 1) * 256] = klb0[g * 128:(g + 1) * 128, :]
        maps.append({"z2_in": np.ascontiguousarray(z2_in),
                     "q_in": np.ascontiguousarray(q_b0[:, t0:t0 + TC]),
                     "kl_in": np.ascontiguousarray(kl_f),
                     "v_in": np.ascontiguousarray(vsl),
                     "cw_in": np.ascontiguousarray(cw_in),
                     "outw": np.ascontiguousarray(outw), "outb": np.ascontiguousarray(outb),
                     "xn1T": xn1[c], "wq_in": np.ascontiguousarray(wq_in), "wk_in": np.ascontiguousarray(wk_in)})
    r = run_bass_kernel_spmd(nc_post, maps, core_ids=list(range(NC)))
    enc = [r.results[c]["enc_o"] for c in range(NC)]
    qe_all = np.concatenate([r.results[c]["qe_o"] for c in range(4)], axis=0)
    ke_all = np.concatenate([r.results[c]["ke_o"] for c in range(4)], axis=0)

    nc_edge = build_b1edge()
    rows = inp["L1_attn_adj_rows"].astype(np.uint32)
    cols_ = inp["L1_attn_adj_cols"].astype(np.uint32)
    vals_ = inp["L1_attn_adj_vals"].astype(np.float32)
    maps = []
    for c in range(NC):
        sl = slice(c * NEDGE, (c + 1) * NEDGE)
        maps.append({"qe_all": np.ascontiguousarray(qe_all), "ke_all": np.ascontiguousarray(ke_all),
                     "ridx": np.ascontiguousarray(rows[sl].reshape(NG, 128).T),
                     "cidx": np.ascontiguousarray(cols_[sl].reshape(NG, 128).T),
                     "vals": np.ascontiguousarray(vals_[sl].reshape(NG, 128).T)})
    r = run_bass_kernel_spmd(nc_edge, maps, core_ids=list(range(NC)))
    aparts = [r.results[c]["apart"].reshape(T) for c in range(NC)]
    a_all = np.stack(aparts)

    nc_mix = build_b1mix()
    maps = []
    for c in range(NC):
        q = c % 4
        t0 = q * TC
        maps.append({"a_all": np.ascontiguousarray(a_all),
                     "a_loc": np.ascontiguousarray(a_all[:, t0:t0 + TC]),
                     "valT": val[c], "encT": enc[c]})
    r = run_bass_kernel_spmd(nc_mix, maps, core_ids=list(range(NC)))
    s1 = [r.results[c]["s1_o"] for c in range(NC)]

    xff = [np.ascontiguousarray(xn1[c]) for c in range(NC)]
    y6 = _run_ffn(xff, prf)

    nc_fin = build_final()
    maps = [{"x1T": x1[c], "s1T": s1[c], "y6T": y6[c]} for c in range(NC)]
    r = run_bass_kernel_spmd(nc_fin, maps, core_ids=list(range(NC)))
    return [r.results[c]["out"] for c in range(NC)]


def kernel(**inp):
    pr = _prep_common(inp)
    x1 = _run_b0(inp, pr)
    x1 = [np.ascontiguousarray(v) for v in x1]
    if _DEV_STAGE == "b0":
        return x1
    prf = _prep_ffn(inp)
    outs = _run_b1(inp, x1, prf)
    full = np.zeros((2, T, D), np.float32)
    for c in range(NC):
        b, q = c // 4, c % 4
        full[b, q * TC:(q + 1) * TC, :] = outs[c].T
    return full


# version 03
# speedup vs baseline: 1.0019x; 1.0019x over previous
import numpy as np
import concourse.bass as bass
import concourse.mybir as mybir
from concourse.bass_utils import run_bass_kernel_spmd
from concourse.bass_core_rust import IndirectOffsetOnAxis

F32 = mybir.dt.float32
BF16 = mybir.dt.bfloat16
U32 = mybir.dt.uint32
I32 = mybir.dt.int32
U8 = mybir.dt.uint8
AF = mybir.ActivationFunctionType
ALU = mybir.AluOpType
AX = mybir.AxisListType

NC = 8
B = 2
T = 8192
D = 1024
TC = 2048
TX = TC + 2
NEG = -3.0e38

_DEV_STAGE = None
_DEBUG_B0 = False
_DEBUG_FFN = False

import math
_EPS0 = math.exp(-1.0 / 12.0)
EPS_R = _EPS0 / 2.0
EPS_K = _EPS0 + 0.3 / 11.0
EPS_V = _EPS0

ENGINES = ("pe", "act", "dve", "pool", "sp")


class Em:
    def __init__(self, nc):
        self.nc = nc
        self.ops = []
        self.cnt = {}
        for e in ENGINES:
            self.cnt[e] = 0
            self.cnt["q" + e] = 0
        self.cnt["cc"] = 0

    def all_tokens(self):
        return [(s, v) for s, v in self.cnt.items() if v > 0]

    def op(self, eng, fn, deps="all"):
        waits = self.all_tokens() if deps == "all" else list(deps)
        self.cnt[eng] += 1
        tok = (eng, self.cnt[eng])
        self.ops.append((eng, fn, [(eng, 1)], waits))
        return tok

    def dma(self, eng, fn, deps="all"):
        waits = self.all_tokens() if deps == "all" else list(deps)
        q = "q" + eng
        self.cnt[q] += 16
        tok = (q, self.cnt[q])
        self.ops.append((eng, fn, [(q, 16)], waits))
        return tok

    def cc(self, fn, deps="all"):
        waits = self.all_tokens() if deps == "all" else list(deps)
        self.cnt["cc"] += 1
        tok = ("cc", self.cnt["cc"])
        self.ops.append(("pool", fn, [("cc", 1)], waits))
        return tok

    def build(self):
        nc = self.nc
        import contextlib
        stack = contextlib.ExitStack()
        sem = {}
        for s in self.cnt:
            if self.cnt[s] > 0:
                sem[s] = stack.enter_context(nc.semaphore(name="sem_" + s))
        block = stack.enter_context(nc.Block())
        last = {e: {} for e in ENGINES}

        BATCH = ("pe",)
        n_ops = len(self.ops)
        run_inc = [0] * n_ops
        pend = {}
        last_comp_idx = {}
        for idx, (eng, fn, incs, waits) in enumerate(self.ops):
            if eng not in BATCH:
                continue_flag = False
            is_comp = any(s == eng for (s, _a) in incs)
            if is_comp and eng in BATCH:
                pend[eng] = pend.get(eng, 0) + 1
                last_comp_idx[eng] = idx
            nxt = self.ops[idx + 1] if idx + 1 < n_ops else None
            if eng in BATCH and pend.get(eng, 0) and (nxt is None or nxt[0] != eng):
                run_inc[last_comp_idx[eng]] = pend[eng]
                pend[eng] = 0
        for e2 in list(pend):
            if pend[e2]:
                run_inc[last_comp_idx[e2]] = pend[e2]

        def mk(ename):
            def section(eng_obj):
                for idx, (eng, fn, incs, waits) in enumerate(self.ops):
                    if eng != ename:
                        continue
                    for (s, v) in waits:
                        if s == ename and ename in BATCH:
                            continue
                        if s in sem and last[ename].get(s, 0) < v:
                            eng_obj.wait_ge(sem[s], v)
                            last[ename][s] = v
                    if fn is not None:
                        inst = fn(eng_obj)
                        for (s, amt) in incs:
                            if s == eng and amt == 1 and eng in BATCH:
                                if run_inc[idx]:
                                    inst.then_inc(sem[s], run_inc[idx])
                            else:
                                inst.then_inc(sem[s], amt)
                for s, v in self.cnt.items():
                    if v > 0 and (s.startswith("q") or s == "cc") and s in sem and last[ename].get(s, 0) < v:
                        eng_obj.wait_ge(sem[s], v)
                        last[ename][s] = v
            return section

        secs = {"pe": "tensor", "act": "scalar", "dve": "vector",
                "pool": "gpsimd", "sp": "sync"}
        for ename in ENGINES:
            if any(o[0] == ename for o in self.ops):
                getattr(block, secs[ename])(mk(ename))
        stack.close()
        return nc


def _chunks(n, size=512):
    return [(i, min(size, n - i)) for i in range(0, n, size)]


class Ctx:

    def __init__(self, nc):
        import contextlib
        self.nc = nc
        self.em = Em(nc)
        self.st = contextlib.ExitStack()
        self._ones = None
        self._ident = None

    def sb(self, name, shape, dt=F32):
        return self.st.enter_context(self.nc.sbuf_tensor(name, shape, dt))

    def ps(self, name, shape, dt=F32):
        return self.st.enter_context(self.nc.psum_tensor(name, shape, dt))

    def ones_col(self):
        if self._ones is None:
            self._ones = self.sb("g_ones", [128, 128], F32)
            self.em.op('pool', lambda e: e.memset(self._ones[:], 1.0))
        return self._ones

    def ident(self, dt=F32):
        if self._ident is None:
            iot = self.sb("g_iot", [128, 128], F32)
            pcol = self.sb("g_pcol", [128, 1], F32)
            idf = self.sb("g_idf", [128, 128], F32)
            idb = self.sb("g_idb", [128, 128], BF16)
            self.em.op('pool', lambda e: e.iota(iot[:], [[1, 128]], base=0, channel_multiplier=0, allow_small_or_imprecise_dtypes=True))
            self.em.op('pool', lambda e: e.iota(pcol[:], [[0, 1]], base=0, channel_multiplier=1, allow_small_or_imprecise_dtypes=True))
            self.em.op('dve', lambda e: e.tensor_scalar(out=idf[:], in0=iot[:], scalar1=pcol[:], scalar2=None, op0=ALU.is_equal))
            self.em.op('dve', lambda e: e.tensor_copy(out=idb[:], in_=idf[:]))
            self._ident = {F32: idf, BF16: idb}
        return self._ident[dt]



def emit_ln(cx, xdram, W, xn_sb, spill=None, name="ln", shifts=None, g_cols=None, b_cols=None):
    em, nc = cx.em, cx.nc
    ones = cx.ones_col()
    import contextlib
    with contextlib.ExitStack() as st:
        xc = st.enter_context(nc.sbuf_tensor(name + "_xc", [128, 8 * 512], F32))
        sq = st.enter_context(nc.sbuf_tensor(name + "_sq", [128, 512], F32))
        p1 = st.enter_context(nc.psum_tensor(name + "_p1", [1, 512], F32))
        p2 = st.enter_context(nc.psum_tensor(name + "_p2", [1, 512], F32))
        r1 = st.enter_context(nc.sbuf_tensor(name + "_r1", [1, 512], F32))
        r2 = st.enter_context(nc.sbuf_tensor(name + "_r2", [1, 512], F32))
        bc1 = st.enter_context(nc.psum_tensor(name + "_bc1", [128, 512], F32))
        bc2 = st.enter_context(nc.psum_tensor(name + "_bc2", [128, 512], F32))
        tmp = st.enter_context(nc.sbuf_tensor(name + "_tmp", [128, 512], F32))
        for (off, w) in _chunks(W, 512):
            for i in range(8):
                sh = 0 if shifts is None else shifts[i]
                em.dma('sp', lambda e, i=i, off=off, w=w, sh=sh: e.dma_start(
                    out=xc[:, i * 512:i * 512 + w], in_=xdram[i * 128:(i + 1) * 128, off + sh:off + sh + w]))
            for i in range(8):
                em.op('pe', lambda e, i=i, w=w: e.matmul(p1[0:1, 0:w], ones[:, 0:1], xc[:, i * 512:i * 512 + w], start=(i == 0), stop=(i == 7)))
            for i in range(8):
                em.op('act', lambda e, i=i, w=w: e.activation(out=sq[:, 0:w], in_=xc[:, i * 512:i * 512 + w], func=AF.Square))
                em.op('pe', lambda e, i=i, w=w: e.matmul(p2[0:1, 0:w], ones[:, 0:1], sq[:, 0:w], start=(i == 0), stop=(i == 7)))
            em.op('dve', lambda e, w=w: e.tensor_scalar(out=r1[0:1, 0:w], in0=p1[0:1, 0:w], scalar1=1.0 / D, scalar2=None, op0=ALU.mult))
            em.op('act', lambda e, w=w: e.activation(out=r2[0:1, 0:w], in_=r1[0:1, 0:w], func=AF.Square))
            em.op('dve', lambda e, w=w: e.scalar_tensor_tensor(out=r2[0:1, 0:w], in0=p2[0:1, 0:w], scalar=1.0 / D, in1=r2[0:1, 0:w], op0=ALU.mult, op1=ALU.subtract))
            em.op('dve', lambda e, w=w: e.tensor_scalar(out=r2[0:1, 0:w], in0=r2[0:1, 0:w], scalar1=1e-5, scalar2=None, op0=ALU.add))
            em.op('act', lambda e, w=w: e.activation(out=r2[0:1, 0:w], in_=r2[0:1, 0:w], func=AF.Ln))
            em.op('act', lambda e, w=w: e.activation(out=r2[0:1, 0:w], in_=r2[0:1, 0:w], func=AF.Exp, scale=-0.5))
            em.op('dve', lambda e, w=w: e.scalar_tensor_tensor(out=r1[0:1, 0:w], in0=r1[0:1, 0:w], scalar=-1.0, in1=r2[0:1, 0:w], op0=ALU.mult, op1=ALU.mult))
            em.op('pe', lambda e, w=w: e.matmul(bc1[:, 0:w], ones[0:1, :], r2[0:1, 0:w], start=True, stop=True))
            em.op('pe', lambda e, w=w: e.matmul(bc2[:, 0:w], ones[0:1, :], r1[0:1, 0:w], start=True, stop=True))
            for i in range(8):
                em.op('dve', lambda e, i=i, w=w: e.tensor_tensor(out=tmp[:, 0:w], in0=xc[:, i * 512:i * 512 + w], in1=bc1[:, 0:w], op=ALU.mult))
                em.op('dve', lambda e, i=i, off=off, w=w: e.tensor_tensor(out=tmp[:, 0:w], in0=tmp[:, 0:w], in1=bc2[:, 0:w], op=ALU.add))
                if g_cols is not None:
                    em.op('dve', lambda e, i=i, w=w: e.tensor_scalar(out=tmp[:, 0:w], in0=tmp[:, 0:w], scalar1=g_cols[:, i:i + 1], scalar2=b_cols[:, i:i + 1], op0=ALU.mult, op1=ALU.add))
                em.op('dve', lambda e, i=i, off=off, w=w: e.tensor_copy(out=xn_sb[:, i * W + off:i * W + off + w], in_=tmp[:, 0:w]))
                if spill is not None:
                    em.dma('sp', lambda e, i=i, off=off, w=w: e.dma_start(out=spill[i * 128:(i + 1) * 128, off:off + w], in_=tmp[:, 0:w]))



def _emit_rwkv_tile(cx, i, xn, wT_dram, cols, trem_sb, ccol_e, ccol_v, contrib, tot_sb, with_carry, tag=""):
    em, nc = cx.em, cx.nc
    import contextlib
    with contextlib.ExitStack() as st:
        sb = lambda n, s, dt=F32: st.enter_context(nc.sbuf_tensor(n + tag + str(i), s, dt))
        wtile = sb("rw_w", [128, 8 * 3 * 128], BF16)
        need_r = contrib is not None
        kinds = (("r", 0), ("k", 1), ("v", 2)) if need_r else (("k", 1), ("v", 2))
        for (nm, j) in kinds:
            em.dma('pool', lambda e, nm=nm, j=j: e.dma_start(
                out=wtile.ap().rearrange("p (a k f) -> p a k f", a=8, k=3)[:, :, j, :],
                in_=wT_dram[nm][i].rearrange("a p f -> p a f")))
        yr = sb("yr", [128, TX]) if need_r else None
        yk = sb("yk", [128, TX]); yv = sb("yv", [128, TX])
        pp = st.enter_context(nc.psum_tensor("rw_pp" + tag + str(i), [128, 512], F32))
        for (dst, j) in (((yr, 0),) if need_r else ()) + ((yk, 1), (yv, 2)):
            for (off, w) in _chunks(TX, 512):
                for dt_ in range(8):
                    em.op('pe', lambda e, dst=dst, j=j, off=off, w=w, dt_=dt_: e.matmul(
                        pp[:, 0:w], wtile[:, dt_ * 384 + j * 128: dt_ * 384 + j * 128 + 128],
                        xn[:, dt_ * TX + off: dt_ * TX + off + w], start=(dt_ == 0), stop=(dt_ == 7)))
                em.op('act', lambda e, dst=dst, off=off, w=w: e.activation(out=dst[:, off:off + w], in_=pp[:, 0:w], func=AF.Copy))
        kf = sb("kf", [128, TC]); ee = sb("ee", [128, TC]); vv = sb("vv", [128, TC])
        ev = sb("ev", [128, TC])
        eu = sb("eu", [128, TC]) if with_carry else None
        em.op('dve', lambda e: e.scalar_tensor_tensor(out=kf[:], in0=yk[:, 0:TC], scalar=EPS_K / (1.0 - EPS_K), in1=yk[:, 2:TX], op0=ALU.mult, op1=ALU.add))
        em.op('dve', lambda e: e.tensor_scalar(out=kf[:], in0=kf[:], scalar1=(1.0 - EPS_K), scalar2=cols[:, 1:2], op0=ALU.mult, op1=ALU.add))
        em.op('dve', lambda e: e.scalar_tensor_tensor(out=vv[:], in0=yv[:, 0:TC], scalar=EPS_V / (1.0 - EPS_V), in1=yv[:, 2:TX], op0=ALU.mult, op1=ALU.add))
        em.op('dve', lambda e: e.tensor_scalar(out=vv[:], in0=vv[:], scalar1=(1.0 - EPS_V), scalar2=cols[:, 2:3], op0=ALU.mult, op1=ALU.add))
        em.op('dve', lambda e: e.tensor_scalar(out=ee[:], in0=trem_sb[:], scalar1=cols[:, 5:6], scalar2=None, op0=ALU.mult))
        em.op('dve', lambda e: e.tensor_tensor(out=ee[:], in0=kf[:], in1=ee[:], op=ALU.add))
        em.op('dve', lambda e: e.tensor_scalar(out=ee[:], in0=ee[:], scalar1=10.0, scalar2=-20.0, op0=ALU.min, op1=ALU.max))
        em.op('act', lambda e: e.activation(out=ee[:], in_=ee[:], func=AF.Exp))
        em.op('dve', lambda e: e.tensor_tensor(out=ev[:], in0=ee[:], in1=vv[:], op=ALU.mult))
        if not with_carry:
            scn = sb("scn", [128, TC])
            em.op('dve', lambda e: e.tensor_tensor_scan(out=scn[:], data0=ee[:], data1=ee[:], initial=0.0, op0=ALU.add, op1=ALU.bypass))
            em.op('dve', lambda e: e.tensor_copy(out=tot_sb[:, 2 * i:2 * i + 1], in_=scn[:, TC - 1:TC]))
            em.op('dve', lambda e: e.tensor_tensor_scan(out=scn[:], data0=ev[:], data1=ev[:], initial=0.0, op0=ALU.add, op1=ALU.bypass))
            em.op('dve', lambda e: e.tensor_copy(out=tot_sb[:, 2 * i + 1:2 * i + 2], in_=scn[:, TC - 1:TC]))
            return
        em.op('dve', lambda e: e.tensor_scalar(out=eu[:], in0=kf[:], scalar1=cols[:, 4:5], scalar2=10.0, op0=ALU.add, op1=ALU.min))
        em.op('dve', lambda e: e.tensor_scalar(out=eu[:], in0=eu[:], scalar1=-20.0, scalar2=None, op0=ALU.max))
        em.op('act', lambda e: e.activation(out=eu[:], in_=eu[:], func=AF.Exp))
        sA = sb("sA", [128, TC + 1]); sB = sb("sB", [128, TC + 1])
        em.op('dve', lambda e: e.tensor_copy(out=sB[:, 0:1], in_=ccol_e[:, i:i + 1]))
        em.op('dve', lambda e: e.tensor_tensor_scan(out=sB[:, 1:TC + 1], data0=ee[:], data1=ee[:], initial=ccol_e[:, i:i + 1], op0=ALU.add, op1=ALU.bypass))
        em.op('dve', lambda e: e.tensor_copy(out=sA[:, 0:1], in_=ccol_v[:, i:i + 1]))
        em.op('dve', lambda e: e.tensor_tensor_scan(out=sA[:, 1:TC + 1], data0=ev[:], data1=ev[:], initial=ccol_v[:, i:i + 1], op0=ALU.add, op1=ALU.bypass))
        em.op('dve', lambda e: e.tensor_tensor(out=ee[:], in0=sB[:, 0:TC], in1=eu[:], op=ALU.add))
        em.op('dve', lambda e: e.tensor_tensor(out=eu[:], in0=eu[:], in1=vv[:], op=ALU.mult))
        em.op('dve', lambda e: e.tensor_tensor(out=eu[:], in0=sA[:, 0:TC], in1=eu[:], op=ALU.add))
        em.op('act', lambda e: e.activation(out=ee[:], in_=ee[:], func=AF.Ln))
        em.op('act', lambda e: e.activation(out=ee[:], in_=ee[:], func=AF.Exp, scale=-1.0))
        em.op('dve', lambda e: e.tensor_tensor(out=eu[:], in0=eu[:], in1=ee[:], op=ALU.mult))
        rr = kf
        em.op('dve', lambda e: e.scalar_tensor_tensor(out=rr[:], in0=yr[:, 0:TC], scalar=EPS_R / (1.0 - EPS_R), in1=yr[:, 2:TX], op0=ALU.mult, op1=ALU.add))
        em.op('act', lambda e: e.activation(out=rr[:], in_=rr[:], func=AF.Sigmoid, scale=(1.0 - EPS_R), bias=cols[:, 0:1]))
        em.op('dve', lambda e, i=i: e.tensor_tensor(out=contrib[:, i * TC:(i + 1) * TC], in0=rr[:], in1=eu[:], op=ALU.mult))


def build_b0scan():
    nc = bass.Bass("TRN2", target_bir_lowering=False)
    cx = Ctx(nc)
    em = cx.em
    ap = lambda n, s, dt=F32, kind="ExternalInput": nc.dram_tensor(n, s, dt, kind=kind).ap()
    xT = ap("xT", [D, TX])
    WkTt = ap("WkTt", [8, 8, 128, 128])
    WvTt = ap("WvTt", [8, 8, 128, 128])
    colsd = ap("colsd", [128, 8 * 8])
    trem = ap("trem", [128, TC])
    totals = ap("totals", [128, 16], kind="ExternalOutput")

    xn = cx.sb("xn", [128, 8 * TX], BF16)
    emit_ln(cx, xT, TX, xn)
    cols_all = _sb0("cols_all", [128, 64])
    em.dma('sp', lambda e: e.dma_start(out=cols_all[:], in_=colsd[:, :]))
    trem_sb = _sb0("trem_sb", [128, TC])
    em.dma('sp', lambda e: e.dma_start(out=trem_sb[:], in_=trem[:, :]))
    for i in range(8):
        em.op('act', lambda e, i=i: e.activation(out=cols_all[:, i * 8 + 5:i * 8 + 6], in_=cols_all[:, i * 8 + 3:i * 8 + 4], func=AF.Relu))
        em.op('dve', lambda e, i=i: e.tensor_scalar(out=cols_all[:, i * 8 + 5:i * 8 + 6], in0=cols_all[:, i * 8 + 5:i * 8 + 6], scalar1=-1.0, scalar2=None, op0=ALU.mult))
    tot_sb = _sb0("tot_sb", [128, 16])
    wT = {"k": WkTt, "v": WvTt}
    for i in range(8):
        _emit_rwkv_tile(cx, i, xn, wT, cols_all[:, i * 8:(i + 1) * 8], trem_sb, None, None, None, tot_sb, with_carry=False)
    em.dma('sp', lambda e: e.dma_start(out=totals[:, :], in_=tot_sb[:]))
    em.build()
    cx.st.close()
    return nc


def build_b0main():
    nc = bass.Bass("TRN2", target_bir_lowering=False)
    cx = Ctx(nc)
    em = cx.em
    ap = lambda n, s, dt=F32, kind="ExternalInput": nc.dram_tensor(n, s, dt, kind=kind).ap()
    xT = ap("xT", [D, TX])
    WrTt = ap("WrTt", [8, 8, 128, 128]); WkTt = ap("WkTt", [8, 8, 128, 128])
    WvTt = ap("WvTt", [8, 8, 128, 128]); WoTt = ap("WoTt", [8, 8, 128, 128])
    colsd = ap("colsd", [128, 64])
    trem = ap("trem", [128, TC])
    carry_e = ap("carry_e", [128, 64])
    carry_v = ap("carry_v", [128, 64])
    routerTt = ap("routerTt", [128, 64])
    sgTt = ap("sgTt", [16, 8, 128, 128]); suTt = ap("suTt", [16, 8, 128, 128])
    sdT = ap("sdT", [2 * D, D])
    egTt = ap("egTt", [8, 16, 8, 128, 128]); euTt = ap("euTt", [8, 16, 8, 128, 128])
    edT = ap("edT", [8, 2 * D, D])
    x1_out = ap("x1", [D, TC], kind="ExternalOutput")
    if _DEBUG_B0:
        s_dbg = ap("s_dbg", [D, TC], kind="ExternalOutput")
        xn_dbg = ap("xn_dbg", [D, TX], kind="ExternalOutput")
        g_dbg = ap("g_dbg", [128, 16 * 1024], kind="ExternalOutput")
        comb_dbg = ap("comb_dbg", [128, 128], kind="ExternalOutput")

    s_spill = nc.dram_tensor("s_spill", [D, TC], F32, kind="Internal").ap()
    xn_spill = nc.dram_tensor("xn_spill", [D, TX], F32, kind="Internal").ap()

    xn = cx.sb("xn", [128, 8 * TX], BF16)
    emit_ln(cx, xT, TX, xn, spill=xn_spill)

    cols_all = _sb0("cols_all", [128, 64])
    em.dma('sp', lambda e: e.dma_start(out=cols_all[:], in_=colsd[:, :]))
    trem_sb = _sb0("trem_sb", [128, TC])
    em.dma('sp', lambda e: e.dma_start(out=trem_sb[:], in_=trem[:, :]))
    for i in range(8):
        em.op('act', lambda e, i=i: e.activation(out=cols_all[:, i * 8 + 5:i * 8 + 6], in_=cols_all[:, i * 8 + 3:i * 8 + 4], func=AF.Relu))
        em.op('dve', lambda e, i=i: e.tensor_scalar(out=cols_all[:, i * 8 + 5:i * 8 + 6], in0=cols_all[:, i * 8 + 5:i * 8 + 6], scalar1=-1.0, scalar2=None, op0=ALU.mult))
    ccol_e = cx.sb("ccol_e", [128, 8]); ccol_v = cx.sb("ccol_v", [128, 8])
    car_e = cx.sb("car_e", [128, 64]); car_v = cx.sb("car_v", [128, 64])
    em.dma('sp', lambda e: e.dma_start(out=car_e[:], in_=carry_e[:, :]))
    em.dma('sp', lambda e: e.dma_start(out=car_v[:], in_=carry_v[:, :]))
    for i in range(8):
        em.op('dve', lambda e, i=i: e.reduce_sum(out=ccol_e[:, i:i + 1], in_=car_e[:, i * 8:(i + 1) * 8], axis=AX.X))
        em.op('dve', lambda e, i=i: e.reduce_sum(out=ccol_v[:, i:i + 1], in_=car_v[:, i * 8:(i + 1) * 8], axis=AX.X))

    contrib = cx.sb("contrib", [128, 8 * TC], BF16)
    wT = {"r": WrTt, "k": WkTt, "v": WvTt}
    for i in range(8):
        _emit_rwkv_tile(cx, i, xn, wT, cols_all[:, i * 8:(i + 1) * 8], trem_sb, ccol_e, ccol_v, contrib, None, with_carry=True)

    import contextlib
    with contextlib.ExitStack() as st:
        wo_sb = st.enter_context(nc.sbuf_tensor("wo_sb", [128, 8 * 128], BF16))
        s_f = st.enter_context(nc.sbuf_tensor("s_f", [128, TC], F32))
        pp = st.enter_context(nc.psum_tensor("ps_s", [128, 512], F32))
        for ot in range(8):
            em.dma('pool', lambda e, ot=ot: e.dma_start(out=wo_sb.ap().rearrange("p (a f) -> p a f", a=8), in_=WoTt[ot].rearrange("a p f -> p a f")))
            for (off, w) in _chunks(TC, 512):
                for dt_ in range(8):
                    em.op('pe', lambda e, ot=ot, off=off, w=w, dt_=dt_: e.matmul(
                        pp[:, 0:w], wo_sb[:, dt_ * 128:(dt_ + 1) * 128],
                        contrib[:, dt_ * TC + off: dt_ * TC + off + w], start=(dt_ == 0), stop=(dt_ == 7)))
                em.op('dve', lambda e, ot=ot, off=off, w=w: e.tensor_scalar(out=s_f[:, off:off + w], in0=pp[:, 0:w], scalar1=cols_all[:, ot * 8 + 6:ot * 8 + 7], scalar2=None, op0=ALU.add))
            em.dma('sp', lambda e, ot=ot: e.dma_start(out=s_spill[ot * 128:(ot + 1) * 128, :], in_=s_f[:]))

    comb = _sb0("comb", [128, 128])
    with contextlib.ExitStack() as st:
        rt_sb = st.enter_context(nc.sbuf_tensor("rt_sb", [128, 64], F32))
        em.dma('sp', lambda e: e.dma_start(out=rt_sb[:], in_=routerTt[:, :]))
        xf_t = st.enter_context(nc.sbuf_tensor("xf_t", [128, 8 * 128], F32))
        pl = st.enter_context(nc.psum_tensor("pl", [128, 512], F32))
        lg = st.enter_context(nc.sbuf_tensor("lg", [128, 8], F32))
        elg = st.enter_context(nc.sbuf_tensor("elg", [128, 8], F32))
        mcol = st.enter_context(nc.sbuf_tensor("mcol", [128, 5], F32))
        for tt in range(16):
            for i in range(8):
                em.dma('sp', lambda e, tt=tt, i=i: e.dma_start(out=xf_t[:, i * 128:(i + 1) * 128], in_=xn_spill[i * 128:(i + 1) * 128, 1 + tt * 128:1 + (tt + 1) * 128]))
            for i in range(8):
                em.op('pe', lambda e, i=i: e.matmul(pl[:, 0:8], xf_t[:, i * 128:(i + 1) * 128], rt_sb[:, i * 8:(i + 1) * 8], start=(i == 0), stop=(i == 7)))
            em.op('dve', lambda e: e.reduce_max(out=mcol[:, 0:1], in_=pl[:, 0:8], axis=AX.X, negate=True))
            em.op('act', lambda e: e.activation(out=elg[:], in_=pl[:, 0:8], func=AF.Exp, bias=mcol[:, 0:1]))
            em.op('dve', lambda e: e.reduce_max(out=mcol[:, 4:5], in_=elg[:], axis=AX.X))
            em.op('dve', lambda e: e.tensor_scalar(out=lg[:], in0=elg[:], scalar1=mcol[:, 4:5], scalar2=None, op0=ALU.is_ge))
            em.op('dve', lambda e: e.scalar_tensor_tensor(out=lg[:], in0=lg[:], scalar=-2.0, in1=elg[:], op0=ALU.mult, op1=ALU.add))
            em.op('dve', lambda e: e.reduce_max(out=mcol[:, 1:2], in_=lg[:], axis=AX.X))
            em.op('dve', lambda e: e.tensor_scalar(out=lg[:], in0=elg[:], scalar1=mcol[:, 1:2], scalar2=None, op0=ALU.is_ge))
            em.op('dve', lambda e: e.scalar_tensor_tensor(out=lg[:], in0=elg[:], scalar=1.0, in1=lg[:], op0=ALU.bypass, op1=ALU.mult, accum_out=mcol[:, 2:3]))
            em.op('dve', lambda e: e.reciprocal(out=mcol[:, 3:4], in_=mcol[:, 2:3]))
            em.op('dve', lambda e, tt=tt: e.tensor_scalar(out=comb[:, tt * 8:(tt + 1) * 8], in0=lg[:], scalar1=mcol[:, 3:4], scalar2=None, op0=ALU.mult))

    for h in range(4):
        em.op('pool', lambda e, h=h: e.memset(acc[:, h * 4096:(h + 1) * 4096], 0.0))

    with contextlib.ExitStack() as st:
        wg = st.enter_context(nc.sbuf_tensor("wg", [128, 8 * 256], BF16))
        ed_sb = st.enter_context(nc.sbuf_tensor("ed_sb", [128, 16 * 1024], BF16))
        hb = st.enter_context(nc.sbuf_tensor("hb", [128, 16 * 1024], BF16))
        pg = st.enter_context(nc.psum_tensor("pg", [128, 512], F32))
        pu = st.enter_context(nc.psum_tensor("pu", [128, 512], F32))
        py = st.enter_context(nc.psum_tensor("py", [128, 512], F32))
        sg_t = st.enter_context(nc.sbuf_tensor("sg_t", [128, 512], BF16))

        def expert(eidx):
            if eidx == 8:
                g_dram, u_dram, d_dram = sgTt, suTt, sdT
            else:
                g_dram, u_dram, d_dram = egTt[eidx], euTt[eidx], edT[eidx]
            for i in range(16):
                em.dma('pool', lambda e, i=i, d_dram=d_dram: e.dma_start(out=ed_sb[:, i * 1024:(i + 1) * 1024], in_=d_dram[i * 128:(i + 1) * 128, :]))
            for half in range(2):
                for ht in range(16):
                    em.dma('pool', lambda e, ht=ht, g_dram=g_dram: e.dma_start(
                        out=wg.ap().rearrange("p (a two f) -> p a two f", a=8, two=2)[:, :, 0, :],
                        in_=g_dram[ht].rearrange("a p f -> p a f")))
                    em.dma('pool', lambda e, ht=ht, u_dram=u_dram: e.dma_start(
                        out=wg.ap().rearrange("p (a two f) -> p a two f", a=8, two=2)[:, :, 1, :],
                        in_=u_dram[ht].rearrange("a p f -> p a f")))
                    for cch in range(2):
                        off = 1 + half * 1024 + cch * 512
                        for i in range(8):
                            em.op('pe', lambda e, i=i, off=off: e.matmul(pg[:], wg[:, i * 256:i * 256 + 128], xn[:, i * TX + off:i * TX + off + 512], start=(i == 0), stop=(i == 7)))
                        for i in range(8):
                            em.op('pe', lambda e, i=i, off=off: e.matmul(pu[:], wg[:, i * 256 + 128:i * 256 + 256], xn[:, i * TX + off:i * TX + off + 512], start=(i == 0), stop=(i == 7)))
                        em.op('act', lambda e: e.activation(out=sg_t[:], in_=pg[:], func=AF.Silu))
                        em.op('dve', lambda e, ht=ht, cch=cch: e.tensor_tensor(out=hb[:, ht * 1024 + cch * 512: ht * 1024 + (cch + 1) * 512], in0=pu[:], in1=sg_t[:], op=ALU.mult))
                for ts in range(8):
                    tt_g = half * 8 + ts
                    for och in range(2):
                        for h2 in range(16):
                            em.op('pe', lambda e, ts=ts, och=och, h2=h2: e.matmul(
                                py[:], hb[:, h2 * 1024 + ts * 128: h2 * 1024 + (ts + 1) * 128],
                                ed_sb[:, h2 * 1024 + och * 512: h2 * 1024 + (och + 1) * 512],
                                start=(h2 == 0), stop=(h2 == 15)))
                        if eidx == 8:
                            em.op('dve', lambda e, tt_g=tt_g, och=och: e.tensor_tensor(
                                out=acc[:, tt_g * 1024 + och * 512: tt_g * 1024 + (och + 1) * 512],
                                in0=acc[:, tt_g * 1024 + och * 512: tt_g * 1024 + (och + 1) * 512],
                                in1=py[:], op=ALU.add))
                        else:
                            em.op('dve', lambda e, tt_g=tt_g, och=och, eidx=eidx: e.scalar_tensor_tensor(
                                out=acc[:, tt_g * 1024 + och * 512: tt_g * 1024 + (och + 1) * 512],
                                in0=py[:], scalar=comb[:, tt_g * 8 + eidx: tt_g * 8 + eidx + 1],
                                in1=acc[:, tt_g * 1024 + och * 512: tt_g * 1024 + (och + 1) * 512],
                                op0=ALU.mult, op1=ALU.add))

        for eidx in range(9):
            expert(eidx)

    if _DEBUG_B0:
        em.dma('sp', lambda e: e.dma_start(out=g_dbg[:, :], in_=acc[:]))
        em.dma('sp', lambda e: e.dma_start(out=comb_dbg[:, :], in_=comb[:]))
        import contextlib as _ctl2
        with _ctl2.ExitStack() as st_d:
            tmpd = st_d.enter_context(nc.sbuf_tensor("tmpd", [128, TC], F32))
            for ot in range(8):
                em.dma('sp', lambda e, ot=ot: e.dma_start(out=tmpd[:], in_=s_spill[ot * 128:(ot + 1) * 128, :]))
                em.dma('sp', lambda e, ot=ot: e.dma_start(out=s_dbg[ot * 128:(ot + 1) * 128, :], in_=tmpd[:]))
                em.dma('sp', lambda e, ot=ot: e.dma_start(out=tmpd[:], in_=xn_spill[ot * 128:(ot + 1) * 128, 0:TC]))
                em.dma('sp', lambda e, ot=ot: e.dma_start(out=xn_dbg[ot * 128:(ot + 1) * 128, 0:TC], in_=tmpd[:]))
    with contextlib.ExitStack() as st:
        identf = cx.ident(F32)
        ptr = st.enter_context(nc.psum_tensor("ptr", [128, 128], F32))
        gfm = st.enter_context(nc.sbuf_tensor("gfm", [128, TC], F32))
        sfm = st.enter_context(nc.sbuf_tensor("sfm", [128, TC], F32))
        xfm = st.enter_context(nc.sbuf_tensor("xfm", [128, TC], F32))
        for ot in range(8):
            for tt in range(16):
                em.op('pe', lambda e, ot=ot, tt=tt: e.transpose(ptr[:], acc[:, tt * 1024 + ot * 128:tt * 1024 + (ot + 1) * 128], identf[:]))
                em.op('act', lambda e, tt=tt: e.activation(out=gfm[:, tt * 128:(tt + 1) * 128], in_=ptr[:], func=AF.Silu))
            em.dma('sp', lambda e, ot=ot: e.dma_start(out=sfm[:], in_=s_spill[ot * 128:(ot + 1) * 128, :]))
            em.dma('sp', lambda e, ot=ot: e.dma_start(out=xfm[:], in_=xT[ot * 128:(ot + 1) * 128, 1:TC + 1]))
            em.op('dve', lambda e: e.tensor_tensor(out=gfm[:], in0=gfm[:], in1=sfm[:], op=ALU.mult))
            em.op('dve', lambda e: e.tensor_tensor(out=gfm[:], in0=gfm[:], in1=xfm[:], op=ALU.add))
            em.dma('sp', lambda e, ot=ot: e.dma_start(out=x1_out[ot * 128:(ot + 1) * 128, :], in_=gfm[:]))

    em.build()
    cx.st.close()
    return nc



def _tile_wT(w):
    wT = np.ascontiguousarray(w.T)
    t = wT.reshape(8, 128, 8, 128).transpose(2, 0, 1, 3)
    return np.ascontiguousarray(t)


def _tile_upproj(wT, nh):
    t = wT.reshape(8, 128, nh, 128).transpose(2, 0, 1, 3)
    return np.ascontiguousarray(t)


def _prep_common(inp):
    x = inp["L0_attn_x"]
    pr = {}
    xT = []
    for c in range(NC):
        b, q = c // 4, c % 4
        t0 = q * TC
        lo, hi = t0 - 1, t0 + TC + 1
        sl = x[b, max(lo, 0):min(hi, T)]
        if lo < 0:
            sl = np.concatenate([np.zeros((1, D), np.float32), sl], 0)
        if hi > T:
            sl = np.concatenate([sl, np.zeros((1, D), np.float32)], 0)
        xT.append(np.ascontiguousarray(sl.T))
    pr["xT"] = xT
    trem = []
    for c in range(NC):
        q = c % 4
        t0 = q * TC
        row = (T - (np.arange(t0, t0 + TC) + 1)).astype(np.float32)
        trem.append(np.ascontiguousarray(np.broadcast_to(row[None, :], (128, TC)).copy()))
    pr["trem"] = trem
    for nm, key in (("WrTt", "Wr"), ("WkTt", "Wk"), ("WvTt", "Wv"), ("WoTt", "Wo")):
        pr[nm] = _tile_wT(inp["L0_attn_" + key])
    cols = np.zeros((128, 64), np.float32)
    for i in range(8):
        sl = slice(i * 128, (i + 1) * 128)
        cols[:, i * 8 + 0] = inp["L0_attn_br"][sl]
        cols[:, i * 8 + 1] = inp["L0_attn_bk"][sl]
        cols[:, i * 8 + 2] = inp["L0_attn_bv"][sl]
        cols[:, i * 8 + 3] = inp["L0_attn_time_decay"][sl]
        cols[:, i * 8 + 4] = inp["L0_attn_U"][sl]
        cols[:, i * 8 + 6] = inp["L0_attn_bo"][sl]
    pr["colsd"] = np.ascontiguousarray(cols)
    rt = inp["L0_ffn_router_w"].T
    pr["routerTt"] = np.ascontiguousarray(rt.reshape(8, 128, 8).reshape(8, 128, 8).transpose(1, 0, 2).reshape(128, 64))
    pr["sgTt"] = _tile_upproj(np.ascontiguousarray(inp["L0_ffn_sg"].T), 16)
    pr["suTt"] = _tile_upproj(np.ascontiguousarray(inp["L0_ffn_su"].T), 16)
    pr["sdT"] = np.ascontiguousarray(inp["L0_ffn_sd"].T)
    pr["egTt"] = np.ascontiguousarray(np.stack([_tile_upproj(np.ascontiguousarray(inp["L0_ffn_eg"][e].T), 16) for e in range(8)]))
    pr["euTt"] = np.ascontiguousarray(np.stack([_tile_upproj(np.ascontiguousarray(inp["L0_ffn_eu"][e].T), 16) for e in range(8)]))
    pr["edT"] = np.ascontiguousarray(inp["L0_ffn_ed"].transpose(0, 2, 1))
    return pr


def _run_b0(inp, pr):
    nc1 = build_b0scan()
    maps1 = []
    for c in range(NC):
        maps1.append({"xT": pr["xT"][c], "WkTt": pr["WkTt"], "WvTt": pr["WvTt"],
                      "colsd": pr["colsd"], "trem": pr["trem"][c]})
    r1 = run_bass_kernel_spmd(nc1, maps1, core_ids=list(range(NC)))
    totals = [r1.results[c]["totals"] for c in range(NC)]

    carry_e, carry_v = [], []
    for c in range(NC):
        b, q = c // 4, c % 4
        ae = np.zeros((128, 64), np.float32)
        av = np.zeros((128, 64), np.float32)
        for qq in range(q):
            tt = totals[b * 4 + qq]
            for i in range(8):
                ae[:, i * 8 + qq] = tt[:, 2 * i]
                av[:, i * 8 + qq] = tt[:, 2 * i + 1]
        carry_e.append(np.ascontiguousarray(ae))
        carry_v.append(np.ascontiguousarray(av))

    nc2 = build_b0main()
    maps2 = []
    for c in range(NC):
        m = {"xT": pr["xT"][c], "trem": pr["trem"][c], "carry_e": carry_e[c], "carry_v": carry_v[c]}
        for nm in ("WrTt", "WkTt", "WvTt", "WoTt", "colsd", "routerTt", "sgTt", "suTt", "sdT", "egTt", "euTt", "edT"):
            m[nm] = pr[nm]
        maps2.append(m)
    r2 = run_bass_kernel_spmd(nc2, maps2, core_ids=list(range(NC)))
    x1 = [r2.results[c]["x1"] for c in range(NC)]
    return x1





def emit_ln_sb(cx, src_cb, W, dst_cb, name, g_cols=None, b_cols=None, f32_cb=None):
    em, nc = cx.em, cx.nc
    ones = cx.ones_col()
    import contextlib
    with contextlib.ExitStack() as st:
        sq = st.enter_context(nc.sbuf_tensor(name + "_sq", [128, 512], F32))
        p1 = st.enter_context(nc.psum_tensor(name + "_p1", [1, 512], F32))
        p2 = st.enter_context(nc.psum_tensor(name + "_p2", [1, 512], F32))
        r1 = st.enter_context(nc.sbuf_tensor(name + "_r1", [1, 512], F32))
        r2 = st.enter_context(nc.sbuf_tensor(name + "_r2", [1, 512], F32))
        bc1 = st.enter_context(nc.psum_tensor(name + "_bc1", [128, 512], F32))
        bc2 = st.enter_context(nc.psum_tensor(name + "_bc2", [128, 512], F32))
        tmp = st.enter_context(nc.sbuf_tensor(name + "_tmp", [128, 512], F32))
        for (off, w) in _chunks(W, 512):
            for i in range(8):
                em.op('pe', lambda e, i=i, off=off, w=w: e.matmul(p1[0:1, 0:w], ones[:, 0:1], src_cb(i, off, w), start=(i == 0), stop=(i == 7)))
            for i in range(8):
                em.op('act', lambda e, i=i, off=off, w=w: e.activation(out=sq[:, 0:w], in_=src_cb(i, off, w), func=AF.Square))
                em.op('pe', lambda e, i=i, w=w: e.matmul(p2[0:1, 0:w], ones[:, 0:1], sq[:, 0:w], start=(i == 0), stop=(i == 7)))
            em.op('dve', lambda e, w=w: e.tensor_scalar(out=r1[0:1, 0:w], in0=p1[0:1, 0:w], scalar1=1.0 / D, scalar2=None, op0=ALU.mult))
            em.op('act', lambda e, w=w: e.activation(out=r2[0:1, 0:w], in_=r1[0:1, 0:w], func=AF.Square))
            em.op('dve', lambda e, w=w: e.scalar_tensor_tensor(out=r2[0:1, 0:w], in0=p2[0:1, 0:w], scalar=1.0 / D, in1=r2[0:1, 0:w], op0=ALU.mult, op1=ALU.subtract))
            em.op('dve', lambda e, w=w: e.tensor_scalar(out=r2[0:1, 0:w], in0=r2[0:1, 0:w], scalar1=1e-5, scalar2=None, op0=ALU.add))
            em.op('act', lambda e, w=w: e.activation(out=r2[0:1, 0:w], in_=r2[0:1, 0:w], func=AF.Ln))
            em.op('act', lambda e, w=w: e.activation(out=r2[0:1, 0:w], in_=r2[0:1, 0:w], func=AF.Exp, scale=-0.5))
            em.op('dve', lambda e, w=w: e.scalar_tensor_tensor(out=r1[0:1, 0:w], in0=r1[0:1, 0:w], scalar=-1.0, in1=r2[0:1, 0:w], op0=ALU.mult, op1=ALU.mult))
            em.op('pe', lambda e, w=w: e.matmul(bc1[:, 0:w], ones[0:1, :], r2[0:1, 0:w], start=True, stop=True))
            em.op('pe', lambda e, w=w: e.matmul(bc2[:, 0:w], ones[0:1, :], r1[0:1, 0:w], start=True, stop=True))
            for i in range(8):
                em.op('dve', lambda e, i=i, off=off, w=w: e.tensor_tensor(out=tmp[:, 0:w], in0=src_cb(i, off, w), in1=bc1[:, 0:w], op=ALU.mult))
                em.op('dve', lambda e, i=i, off=off, w=w: e.tensor_tensor(out=tmp[:, 0:w], in0=tmp[:, 0:w], in1=bc2[:, 0:w], op=ALU.add))
                if g_cols is not None:
                    em.op('dve', lambda e, i=i, off=off, w=w: e.tensor_scalar(out=tmp[:, 0:w], in0=tmp[:, 0:w], scalar1=g_cols[:, i:i + 1], scalar2=b_cols[:, i:i + 1], op0=ALU.mult, op1=ALU.add))
                if f32_cb is not None:
                    em.op('dve', lambda e, i=i, off=off, w=w: e.tensor_copy(out=f32_cb(i, off, w), in_=tmp[:, 0:w]))
                em.op('dve', lambda e, i=i, off=off, w=w: e.tensor_copy(out=dst_cb(i, off, w), in_=tmp[:, 0:w]))



def build_b1pre():
    nc = bass.Bass("TRN2", target_bir_lowering=False)
    cx = Ctx(nc)
    em = cx.em
    ap = lambda n, s, dt=F32, kind="ExternalInput": nc.dram_tensor(n, s, dt, kind=kind).ap()
    x1T = ap("x1T", [D, TC])
    qkvw = ap("qkvw", [12, 8, 128, 128])
    wvw = ap("wvw", [8, 8, 128, 128])
    wvb = ap("wvb", [128, 8])
    xn1_out = ap("xn1", [D, TC], kind="ExternalOutput")
    q_out = ap("q_o", [512, TC], kind="ExternalOutput")
    k_out = ap("k_o", [512, TC], kind="ExternalOutput")
    v_out = ap("v_o", [512, TC], kind="ExternalOutput")
    lm_out = ap("lm_o", [512, 128], kind="ExternalOutput")
    val_out = ap("val_o", [D, TC], kind="ExternalOutput")

    xb = cx.sb("xb", [128, 8 * TC], BF16)
    xn1_sp = nc.dram_tensor("xn1_sp", [D, TC], F32, kind="Internal").ap()
    emit_ln(cx, x1T, TC, xb, spill=xn1_sp)
    import contextlib
    with contextlib.ExitStack() as st:
        tbuf = st.enter_context(nc.sbuf_tensor("tbuf", [128, TC], F32))
        for i in range(8):
            em.dma('sp', lambda e, i=i: e.dma_start(out=tbuf[:], in_=xn1_sp[i * 128:(i + 1) * 128, :]))
            em.dma('sp', lambda e, i=i: e.dma_start(out=xn1_out[i * 128:(i + 1) * 128, :], in_=tbuf[:]))

    with contextlib.ExitStack() as st:
        wt = st.enter_context(nc.sbuf_tensor("wt", [128, 8 * 128], BF16))
        pp = st.enter_context(nc.psum_tensor("ppq", [128, 512], F32))
        ob = st.enter_context(nc.sbuf_tensor("ob", [128, TC], F32))
        red = st.enter_context(nc.sbuf_tensor("red", [128, 64], F32))
        SC = 0.125
        for it in range(12):
            em.dma('pool', lambda e, it=it: e.dma_start(out=wt.ap().rearrange("p (a f) -> p a f", a=8), in_=qkvw[it].rearrange("a p f -> p a f")))
            for (off, w) in _chunks(TC, 512):
                for dt_ in range(8):
                    em.op('pe', lambda e, it=it, off=off, w=w, dt_=dt_: e.matmul(
                        pp[:, 0:w], wt[:, dt_ * 128:(dt_ + 1) * 128], xb[:, dt_ * TC + off:dt_ * TC + off + w],
                        start=(dt_ == 0), stop=(dt_ == 7)))
                em.op('act', lambda e, it=it, off=off, w=w: e.activation(out=ob[:, off:off + w], in_=pp[:, 0:w], func=AF.Copy, scale=(SC if it < 4 else 1.0)))
            dst = q_out if it < 4 else (k_out if it < 8 else v_out)
            em.dma('sp', lambda e, it=it, dst=dst: e.dma_start(out=dst[(it % 4) * 128:(it % 4) * 128 + 128, :], in_=ob[:]))
            if it < 8:
                em.op('dve', lambda e: e.tensor_reduce(out=red[:], in_=ob.ap().rearrange("p (g s) -> p g s", s=32), axis=AX.X, op=ALU.add))
                em.op('dve', lambda e: e.tensor_scalar(out=red[:], in0=red[:], scalar1=1.0 / 32, scalar2=None, op0=ALU.mult))
                em.dma('sp', lambda e, it=it: e.dma_start(out=lm_out[(it % 4) * 128:(it % 4) * 128 + 128, (0 if it < 4 else 64):(64 if it < 4 else 128)], in_=red[:]))
        wvb_sb = st.enter_context(nc.sbuf_tensor("wvb_sb", [128, 8], F32))
        em.dma('sp', lambda e: e.dma_start(out=wvb_sb[:], in_=wvb[:, :]))
        for ot in range(8):
            em.dma('pool', lambda e, ot=ot: e.dma_start(out=wt.ap().rearrange("p (a f) -> p a f", a=8), in_=wvw[ot].rearrange("a p f -> p a f")))
            for (off, w) in _chunks(TC, 512):
                for dt_ in range(8):
                    em.op('pe', lambda e, ot=ot, off=off, w=w, dt_=dt_: e.matmul(
                        pp[:, 0:w], wt[:, dt_ * 128:(dt_ + 1) * 128], xb[:, dt_ * TC + off:dt_ * TC + off + w],
                        start=(dt_ == 0), stop=(dt_ == 7)))
                em.op('dve', lambda e, ot=ot, off=off, w=w: e.tensor_scalar(out=ob[:, off:off + w], in0=pp[:, 0:w], scalar1=wvb_sb[:, ot:ot + 1], scalar2=None, op0=ALU.add))
            em.dma('sp', lambda e, ot=ot: e.dma_start(out=val_out[ot * 128:(ot + 1) * 128, :], in_=ob[:]))
    em.build()
    cx.st.close()
    return nc


def _prep_b1(inp):
    pr = {}
    qkvw = inp["L1_attn_qkv_w"]
    pr["qkvw"] = np.ascontiguousarray(qkvw.reshape(8, 128, 12, 128).transpose(2, 0, 1, 3))
    wvw = inp["L1_attn_wv_w"]
    pr["wvw"] = np.ascontiguousarray(wvw.reshape(8, 128, 8, 128).transpose(2, 0, 1, 3))
    pr["wvb"] = np.ascontiguousarray(inp["L1_attn_wv_b"].reshape(8, 128).T)
    return pr


def _run_b1pre(x1, pr1):
    nc = build_b1pre()
    maps = []
    for c in range(NC):
        maps.append({"x1T": x1[c], "qkvw": pr1["qkvw"], "wvw": pr1["wvw"], "wvb": pr1["wvb"]})
    r = run_bass_kernel_spmd(nc, maps, core_ids=list(range(NC)))
    return [r.results[c] for c in range(NC)]



TE = 2304


def build_ffn_attn():
    nc = bass.Bass("TRN2", target_bir_lowering=False)
    cx = Ctx(nc)
    em = cx.em
    import contextlib
    ap = lambda n, s, dt=F32, kind="ExternalInput": nc.dram_tensor(n, s, dt, kind=kind).ap()
    xT = ap("xT", [D, 2305])
    wqk = ap("wqk", [16, 8, 128, 128])
    wv = ap("wv", [8, 128, D])
    wo = ap("wo", [8, 8, 128, 128])
    qscol = ap("qscol", [128, 2])
    tabq = ap("tabq", [128, 2 * 2048])
    tabk = ap("tabk", [128, 2 * TE])
    pswap = ap("pswap", [128, 128])
    masks = ap("masks", [128, 8 * 128])
    xout = ap("xout", [D, TC], kind="ExternalOutput")

    ones = cx.ones_col()
    identf0 = cx.ident(F32)
    bd = cx.sb("bd", [128, 4])
    bdT = cx.sb("bdT", [2, 128])
    em.op('pool', lambda e: e.memset(bd[:], 0.0))
    em.op('pool', lambda e: e.memset(bd[0:64, 0:1], 1.0))
    em.op('pool', lambda e: e.memset(bd[64:128, 1:2], 1.0))
    with nc.psum_tensor("pbdT", [2, 128], F32) as pbdT:
        em.op('pe', lambda e: e.transpose(pbdT[0:2, 0:128], bd[:, 0:2], identf0[:]))
        em.op('dve', lambda e: e.tensor_copy(out=bdT[:], in_=pbdT[0:2, 0:128]))

    mask_sb = cx.sb("mask_sb", [128, 8 * 128], F32)
    em.dma('sp', lambda e: e.dma_start(out=mask_sb[:], in_=masks[:, :]))
    qs_sb = cx.sb("qs_sb", [128, 2], F32)
    em.dma('sp', lambda e: e.dma_start(out=qs_sb[:], in_=qscol[:, :]))
    psw = cx.sb("psw", [128, 128], F32)
    em.dma('sp', lambda e: e.dma_start(out=psw[:], in_=pswap[:, :]))
    tq = cx.sb("tq", [128, 2 * 2048], BF16)
    tk = cx.sb("tk", [128, 2 * TE], BF16)
    em.dma('pool', lambda e: e.dma_start(out=tq[:], in_=tabq[:, :]))
    em.dma('pool', lambda e: e.dma_start(out=tk[:], in_=tabk[:, :]))
    c30 = cx.sb("c30", [128, 1])
    em.op('pool', lambda e: e.memset(c30[:], 1e-30))
    ones_bf = cx.sb("ones_bf", [128, 1], BF16)
    em.op('pool', lambda e: e.memset(ones_bf[:], 1.0))

    wv_sb = cx.sb("wv_sb", [128, 8 * 1024], BF16)
    for i in range(8):
        em.dma('pool', lambda e, i=i: e.dma_start(out=wv_sb[:, i * 1024:(i + 1) * 1024], in_=wv[i]))
    qf = cx.sb("qf", [128, 8 * 2048], BF16)
    kf = cx.sb("kf", [128, 8 * TE], BF16)
    vbuf = cx.sb("vbuf", [128, 18 * 16 * 64], BF16)

    with contextlib.ExitStack() as st:
        sb = lambda n, s, dt=F32: st.enter_context(nc.sbuf_tensor(n, s, dt))
        xb = sb("xb", [128, 8 * TE], BF16)
        for i in range(8):
            sh = 1 if i < 4 else 0
            em.dma('pool', lambda e, i=i, sh=sh: e.dma_start(out=xb[:, i * TE:(i + 1) * TE], in_=xT[i * 128:(i + 1) * 128, sh:sh + TE]))
        wt = sb("wt", [128, 8 * 128], BF16)
        pp = st.enter_context(nc.psum_tensor("ppq", [128, 512], F32))
        psp = st.enter_context(nc.psum_tensor("psp", [128, 512], F32))
        pnrm = st.enter_context(nc.psum_tensor("pnrm", [2, 512], F32))
        pbc = st.enter_context(nc.psum_tensor("pbc", [128, 512], F32))
        qraw = sb("qraw", [128, 512]); sqb = sb("sqb", [128, 512])
        nrm = sb("nrm", [2, 512]); t1b = sb("t1b", [128, 512]); t2b = sb("t2b", [128, 512])
        for it in range(16):
            em.dma('pool', lambda e, it=it: e.dma_start(out=wt.ap().rearrange("p (a f) -> p a f", a=8), in_=wqk[it].rearrange("a p f -> p a f")))
            rng = 2048 if it < 8 else TE
            base_sh = 256 if it < 8 else 0
            tab = tq if it < 8 else tk
            tw = 2048 if it < 8 else TE
            qsc = qs_sb[:, 0:1] if it < 8 else qs_sb[:, 1:2]
            dstbuf = qf if it < 8 else kf
            for (off, w) in _chunks(rng, 512):
                for dt_ in range(8):
                    em.op('pe', lambda e, it=it, off=off, w=w, dt_=dt_, base_sh=base_sh: e.matmul(
                        pp[:, 0:w], wt[:, dt_ * 128:(dt_ + 1) * 128], xb[:, dt_ * TE + base_sh + off:dt_ * TE + base_sh + off + w],
                        start=(dt_ == 0), stop=(dt_ == 7)))
                em.op('dve', lambda e, w=w: e.tensor_copy(out=qraw[:, 0:w], in_=pp[:, 0:w]))
                em.op('act', lambda e, w=w: e.activation(out=sqb[:, 0:w], in_=qraw[:, 0:w], func=AF.Square))
                em.op('pe', lambda e, w=w: e.matmul(pnrm[0:2, 0:w], bd[:, 0:2], sqb[:, 0:w], start=True, stop=True))
                em.op('act', lambda e, w=w: e.activation(out=nrm[0:2, 0:w], in_=pnrm[0:2, 0:w], func=AF.Ln, bias=c30[0:2, 0:1]))
                em.op('act', lambda e, w=w: e.activation(out=nrm[0:2, 0:w], in_=nrm[0:2, 0:w], func=AF.Exp, scale=0.5))
                em.op('dve', lambda e, w=w: e.tensor_scalar(out=nrm[0:2, 0:w], in0=nrm[0:2, 0:w], scalar1=1e-12, scalar2=None, op0=ALU.max))
                em.op('dve', lambda e, w=w: e.reciprocal(out=nrm[0:2, 0:w], in_=nrm[0:2, 0:w]))
                em.op('pe', lambda e, w=w: e.matmul(pbc[:, 0:w], bdT[0:2, :], nrm[0:2, 0:w], start=True, stop=True))
                em.op('dve', lambda e, w=w, qsc=qsc: e.scalar_tensor_tensor(out=qraw[:, 0:w], in0=qraw[:, 0:w], scalar=qsc, in1=pbc[:, 0:w], op0=ALU.mult, op1=ALU.mult))
                em.op('pe', lambda e, w=w: e.matmul(psp[:, 0:w], psw[:], qraw[:, 0:w], start=True, stop=True))
                em.op('dve', lambda e, off=off, w=w, tab=tab: e.tensor_tensor(out=t1b[:, 0:w], in0=qraw[:, 0:w], in1=tab[:, off:off + w], op=ALU.mult))
                em.op('dve', lambda e, off=off, w=w, tab=tab, tw=tw: e.tensor_tensor(out=t2b[:, 0:w], in0=psp[:, 0:w], in1=tab[:, tw + off:tw + off + w], op=ALU.mult))
                em.op('dve', lambda e, it=it, off=off, w=w, dstbuf=dstbuf, rng=rng: e.tensor_tensor(
                    out=dstbuf[:, (it % 8) * rng + off:(it % 8) * rng + off + w], in0=t1b[:, 0:w], in1=t2b[:, 0:w], op=ALU.add))
        for tt in range(18):
            for och in range(2):
                for dt_ in range(8):
                    em.op('pe', lambda e, tt=tt, och=och, dt_=dt_: e.matmul(
                        pp[:], xb[:, dt_ * TE + tt * 128:dt_ * TE + (tt + 1) * 128],
                        wv_sb[:, dt_ * 1024 + och * 512:dt_ * 1024 + (och + 1) * 512], start=(dt_ == 0), stop=(dt_ == 7)))
                for hh in range(8):
                    h = och * 8 + hh
                    em.op('dve', lambda e, tt=tt, h=h, hh=hh: e.tensor_copy(
                        out=vbuf[:, (tt * 16 + h) * 64:(tt * 16 + h) * 64 + 64], in_=pp[:, hh * 64:(hh + 1) * 64]))

    avT = cx.sb("avT", [128, 8 * 2048], BF16)
    with contextlib.ExitStack() as st:
        sb = lambda n, s, dt=F32: st.enter_context(nc.sbuf_tensor(n, s, dt))
        expT = sb("expT", [128, 4 * 256], BF16)
        exin = sb("exin", [128, 128])
        mcol = sb("mcol", [128, 2])
        mrow_s = sb("mrow_s", [1, 256])
        zbcs = sb("zbcs", [128, 2048])
        zrow = sb("zrow", [128, 2048])
        ptm = st.enter_context(nc.psum_tensor("ptm", [128, 512], F32))
        pst = st.enter_context(nc.psum_tensor("pst", [128, 256], F32))
        pav = st.enter_context(nc.psum_tensor("pav", [128, 2048], F32))
        pz = st.enter_context(nc.psum_tensor("pz", [1, 256], F32))
        pzb = st.enter_context(nc.psum_tensor("pzb", [128, 512], F32))
        identf = cx.ident(F32)
        BLK = {(0, 0): 0, (1, 0): 1, (2, 0): 2, (1, 1): 3, (2, 1): 4, (3, 1): 5}
        STD = {(0, 0): 6, (1, 0): None, (2, 0): 7, (1, 1): 6, (2, 1): None, (3, 1): 7}
        SEQ = [(0, 0, 128), (1, 0, 256), (2, 0, 256), (3, 128, 256)]
        for h in range(16):
            tile, bh = h // 2, (h % 2) * 64
            for w_ in range(8):
                for it_ in range(2):
                    em.op('pe', lambda e, tile=tile, bh=bh, w_=w_, it_=it_: e.matmul(
                        ptm[:, :],
                        qf[bh:bh + 64, tile * 2048 + w_ * 256 + it_ * 128: tile * 2048 + w_ * 256 + it_ * 128 + 128],
                        kf[bh:bh + 64, tile * TE + w_ * 256: tile * TE + w_ * 256 + 512], start=True, stop=True))
                    em.op('dve', lambda e, it_=it_: e.reduce_max(out=mcol[:, it_:it_ + 1], in_=ptm[:], axis=AX.X, negate=True))
                    em.op('pe', lambda e, it_=it_: e.transpose(pz[0:1, 0:128], mcol[:, it_:it_ + 1], identf[:]))
                    em.op('dve', lambda e, it_=it_: e.tensor_copy(out=mrow_s[0:1, it_ * 128:(it_ + 1) * 128], in_=pz[0:1, 0:128]))
                for (jt, it_), mstd in STD.items():
                    midx = BLK[(jt, it_)] if w_ == 0 else mstd
                    em.op('pe', lambda e, jt=jt, it_=it_, w_=w_, tile=tile, bh=bh: e.matmul(
                        pst[:, 0:128], kf[bh:bh + 64, tile * TE + w_ * 256 + jt * 128: tile * TE + w_ * 256 + (jt + 1) * 128],
                        qf[bh:bh + 64, tile * 2048 + w_ * 256 + it_ * 128: tile * 2048 + w_ * 256 + (it_ + 1) * 128],
                        start=True, stop=False))
                    em.op('pe', lambda e, jt=jt, it_=it_: e.matmul(
                        pst[:, 0:128], ones[0:1, 0:128], mrow_s[0:1, it_ * 128:(it_ + 1) * 128], start=False, stop=True))
                    if midx is not None:
                        em.op('dve', lambda e, midx=midx: e.tensor_tensor(out=exin[:], in0=pst[:, 0:128], in1=mask_sb[:, midx * 128:(midx + 1) * 128], op=ALU.add))
                        em.op('act', lambda e, jt=jt, it_=it_: e.activation(out=expT[:, jt * 256 + it_ * 128: jt * 256 + (it_ + 1) * 128], in_=exin[:], func=AF.Exp))
                    else:
                        em.op('act', lambda e, jt=jt, it_=it_: e.activation(out=expT[:, jt * 256 + it_ * 128: jt * 256 + (it_ + 1) * 128], in_=pst[:, 0:128], func=AF.Exp))
                for si, (jt, i0, i1) in enumerate(SEQ):
                    vt = 2 * w_ + jt
                    em.op('pe', lambda e, jt=jt, i0=i0, i1=i1, vt=vt, h=h, w_=w_, si=si, bh=bh: e.matmul(
                        pav[bh:bh + 64, w_ * 256 + i0: w_ * 256 + i1],
                        vbuf[:, (vt * 16 + h) * 64:(vt * 16 + h) * 64 + 64],
                        expT[:, jt * 256 + i0: jt * 256 + i1], start=(si == 0), stop=(si == 3)))
                    em.op('pe', lambda e, jt=jt, i0=i0, i1=i1, w_=w_, si=si: e.matmul(
                        pz[0:1, i0:i1], ones_bf[:, 0:1], expT[:, jt * 256 + i0: jt * 256 + i1], start=(si == 0), stop=(si == 3)))
                em.op('dve', lambda e, w_=w_: e.tensor_copy(out=zrow[0:1, w_ * 256:(w_ + 1) * 256], in_=pz[0:1, 0:256]))
            em.op('dve', lambda e: e.tensor_scalar(out=zrow[0:1, :], in0=zrow[0:1, :], scalar1=1e-38, scalar2=None, op0=ALU.add))
            em.op('dve', lambda e: e.reciprocal(out=zrow[0:1, :], in_=zrow[0:1, :]))
            for (off, w) in _chunks(2048, 512):
                em.op('pe', lambda e, off=off, w=w, bh=bh: e.matmul(pzb[bh:bh + 64, 0:w], ones[0:1, 0:64], zrow[0:1, off:off + w], start=True, stop=True))
                em.op('dve', lambda e, off=off, w=w, bh=bh: e.tensor_copy(out=zbcs[bh:bh + 64, 0:w], in_=pzb[bh:bh + 64, 0:w]))
                em.op('dve', lambda e, off=off, w=w, tile=tile, bh=bh: e.tensor_tensor(
                    out=avT[bh:bh + 64, tile * 2048 + off: tile * 2048 + off + w], in0=pav[bh:bh + 64, off:off + w], in1=zbcs[bh:bh + 64, 0:w], op=ALU.mult))

    with contextlib.ExitStack() as st:
        sb = lambda n, s, dt=F32: st.enter_context(nc.sbuf_tensor(n, s, dt))
        wt2 = sb("wt2", [128, 8 * 128], BF16)
        xfr = sb("xfr", [128, 512])
        ob2 = sb("ob2", [128, 512])
        pow_ = st.enter_context(nc.psum_tensor("pow", [128, 512], F32))
        for ot in range(8):
            em.dma('pool', lambda e, ot=ot: e.dma_start(out=wt2.ap().rearrange("p (a f) -> p a f", a=8), in_=wo[ot].rearrange("a p f -> p a f")))
            for (off, w) in _chunks(TC, 512):
                for it in range(8):
                    em.op('pe', lambda e, ot=ot, off=off, w=w, it=it: e.matmul(
                        pow_[:, 0:w], wt2[:, it * 128:(it + 1) * 128], avT[:, it * 2048 + off:it * 2048 + off + w],
                        start=(it == 0), stop=(it == 7)))
                em.dma('sp', lambda e, ot=ot, off=off, w=w: e.dma_start(out=xfr[:, 0:w], in_=xT[ot * 128:(ot + 1) * 128, 257 + off:257 + off + w]))
                em.op('dve', lambda e, off=off, w=w: e.tensor_tensor(out=ob2[:, 0:w], in0=pow_[:, 0:w], in1=xfr[:, 0:w], op=ALU.add))
                em.dma('sp', lambda e, ot=ot, off=off, w=w: e.dma_start(out=xout[ot * 128:(ot + 1) * 128, off:off + w], in_=ob2[:, 0:w]))
    em.build()
    cx.st.close()
    return nc



def build_ffn_ffn():
    nc = bass.Bass("TRN2", target_bir_lowering=False)
    cx = Ctx(nc)
    em = cx.em
    import contextlib
    ap = lambda n, s, dt=F32, kind="ExternalInput": nc.dram_tensor(n, s, dt, kind=kind).ap()
    xT = ap("xT", [D, TC + 1])
    w1a = ap("w1a", [22, 8, 128, 128])
    w1g = ap("w1g", [22, 8, 128, 128])
    w2t = ap("w2t", [22, 128, D])
    lncols = ap("lncols", [128, 16])
    bcols1 = ap("bcols1", [128, 44])
    bcols2 = ap("bcols2", [128, 8])
    xout = ap("xout", [D, TC], kind="ExternalOutput")
    if _DEBUG_FFN:
        xn_dbg = ap("xn_dbg", [D, TC], kind="ExternalOutput")
        hb_dbg = ap("hb_dbg", [128, 22 * 512], kind="ExternalOutput")

    lnc = cx.sb("lnc", [128, 16], F32)
    em.dma('sp', lambda e: e.dma_start(out=lnc[:], in_=lncols[:, :]))
    b1c = cx.sb("b1c", [128, 44], F32)
    em.dma('sp', lambda e: e.dma_start(out=b1c[:], in_=bcols1[:, :]))
    b2c = cx.sb("b2c", [128, 8], F32)
    em.dma('sp', lambda e: e.dma_start(out=b2c[:], in_=bcols2[:, :]))

    xnb = cx.sb("xnb", [128, 8 * TC], BF16)
    shifts = [1, 1, 1, 1, 0, 0, 0, 0]
    emit_ln(cx, xT, TC, xnb, name="lnf", shifts=shifts, g_cols=lnc[:, 0:8], b_cols=lnc[:, 8:16])
    if _DEBUG_FFN:
        import contextlib as _cl
        with _cl.ExitStack() as _st:
            _tb = _st.enter_context(nc.sbuf_tensor("_tb", [128, TC], F32))
            for i in range(8):
                em.op('dve', lambda e, i=i: e.tensor_copy(out=_tb[:], in_=xnb[:, i * TC:(i + 1) * TC]))
                em.dma('sp', lambda e, i=i: e.dma_start(out=xn_dbg[i * 128:(i + 1) * 128, :], in_=_tb[:]))

    w2_sb = cx.sb("w2_sb", [128, 22 * 1024], BF16)
    for ht in range(22):
        em.dma('pool', lambda e, ht=ht: e.dma_start(out=w2_sb[:, ht * 1024:(ht + 1) * 1024], in_=w2t[ht]))

    import contextlib
    with contextlib.ExitStack() as st:
        sb = lambda n, s, dt=F32: st.enter_context(nc.sbuf_tensor(n, s, dt))
        wta = sb("wta", [128, 8 * 128], BF16)
        wtg = sb("wtg", [128, 8 * 128], BF16)
        hbuf = sb("hbuf", [128, 22 * 512], BF16)
        gel = sb("gel", [128, 512])
        xfr = sb("xfr", [128, 512])
        ob = sb("ob", [128, 512])
        pa = st.enter_context(nc.psum_tensor("pa", [128, 512], F32))
        pg = st.enter_context(nc.psum_tensor("pg2", [128, 512], F32))
        py = st.enter_context(nc.psum_tensor("py2", [128, 512], F32))
        for (off, w) in _chunks(TC, 512):
            for ht in range(22):
                em.dma('pool', lambda e, ht=ht: e.dma_start(out=wta.ap().rearrange("p (a f) -> p a f", a=8), in_=w1a[ht].rearrange("a p f -> p a f")))
                em.dma('pool', lambda e, ht=ht: e.dma_start(out=wtg.ap().rearrange("p (a f) -> p a f", a=8), in_=w1g[ht].rearrange("a p f -> p a f")))
                for dt_ in range(8):
                    em.op('pe', lambda e, ht=ht, off=off, w=w, dt_=dt_: e.matmul(
                        pa[:, 0:w], wta[:, dt_ * 128:(dt_ + 1) * 128], xnb[:, dt_ * TC + off:dt_ * TC + off + w], start=(dt_ == 0), stop=(dt_ == 7)))
                for dt_ in range(8):
                    em.op('pe', lambda e, ht=ht, off=off, w=w, dt_=dt_: e.matmul(
                        pg[:, 0:w], wtg[:, dt_ * 128:(dt_ + 1) * 128], xnb[:, dt_ * TC + off:dt_ * TC + off + w], start=(dt_ == 0), stop=(dt_ == 7)))
                em.op('act', lambda e, ht=ht, w=w: e.activation(out=gel[:, 0:w], in_=pg[:, 0:w], func=AF.Gelu, bias=b1c[:, 22 + ht:23 + ht]))
                em.op('dve', lambda e, ht=ht, w=w: e.scalar_tensor_tensor(out=hbuf[:, ht * 512:ht * 512 + w], in0=pa[:, 0:w], scalar=b1c[:, ht:ht + 1], in1=gel[:, 0:w], op0=ALU.add, op1=ALU.mult))
            if _DEBUG_FFN and off == 0:
                em.dma('pool', lambda e: e.dma_start(out=hb_dbg[:, :], in_=hbuf[:]))
            for ot in range(8):
                for ht in range(22):
                    em.op('pe', lambda e, ot=ot, ht=ht, w=w: e.matmul(
                        py[:, 0:w], w2_sb[:, ht * 1024 + ot * 128: ht * 1024 + (ot + 1) * 128], hbuf[:, ht * 512:ht * 512 + w],
                        start=(ht == 0), stop=(ht == 21)))
                em.dma('sp', lambda e, ot=ot, off=off, w=w: e.dma_start(out=xfr[:, 0:w], in_=xT[ot * 128:(ot + 1) * 128, 1 + off:1 + off + w]))
                em.op('dve', lambda e, ot=ot, w=w: e.tensor_scalar(out=ob[:, 0:w], in0=py[:, 0:w], scalar1=b2c[:, ot:ot + 1], scalar2=None, op0=ALU.add))
                em.op('dve', lambda e, w=w: e.tensor_tensor(out=ob[:, 0:w], in0=ob[:, 0:w], in1=xfr[:, 0:w], op=ALU.add))
                em.dma('sp', lambda e, ot=ot, off=off, w=w: e.dma_start(out=xout[ot * 128:(ot + 1) * 128, off:off + w], in_=ob[:, 0:w]))
    em.build()
    cx.st.close()
    return nc


def _prep_ffn(inp):
    pr = {}
    Wqkv = inp["L1_ffn_Wqkv"]
    wqk, wv, wo, w1a, w1g, w2t = [], [], [], [], [], []
    for i in range(6):
        qk = Wqkv[i][:, 0:2048]
        wqk.append(qk.reshape(8, 128, 16, 128).transpose(2, 0, 1, 3))
        wv.append(np.ascontiguousarray(Wqkv[i][:, 2048:3072].reshape(8, 128, 1024)))
        wo.append(inp["L1_ffn_Wo"][i].reshape(8, 128, 8, 128).transpose(2, 0, 1, 3))
        W1 = inp["L1_ffn_W1"][i]
        a = np.zeros((1024, 2816), np.float32); g = np.zeros((1024, 2816), np.float32)
        a[:, 0:2730] = W1[:, 0:2730]; g[:, 0:2730] = W1[:, 2730:5460]
        w1a.append(a.reshape(8, 128, 22, 128).transpose(2, 0, 1, 3))
        w1g.append(g.reshape(8, 128, 22, 128).transpose(2, 0, 1, 3))
        W2 = inp["L1_ffn_W2"][i]
        W2p = np.zeros((2816, 1024), np.float32); W2p[0:2730] = W2
        w2t.append(W2p.reshape(22, 128, 1024))
    pr["wqk"] = np.ascontiguousarray(np.stack(wqk))
    pr["wv"] = np.ascontiguousarray(np.stack(wv))
    pr["wo"] = np.ascontiguousarray(np.stack(wo))
    pr["w1a"] = np.ascontiguousarray(np.stack(w1a))
    pr["w1g"] = np.ascontiguousarray(np.stack(w1g))
    pr["w2t"] = np.ascontiguousarray(np.stack(w2t))
    qsc = []
    for i in range(6):
        col = np.zeros((128, 2), np.float32)
        col[:, 0] = np.tile(inp["L1_ffn_q_scale"][i], 2) * 8.0
        col[:, 1] = np.tile(inp["L1_ffn_k_scale"][i], 2)
        qsc.append(col)
    pr["qscol"] = np.ascontiguousarray(np.stack(qsc))
    lnc, b1c, b2c = [], [], []
    for i in range(6):
        l = np.zeros((128, 16), np.float32)
        l[:, 0:8] = inp["L1_ffn_ln_g"][i].reshape(8, 128).T
        l[:, 8:16] = inp["L1_ffn_ln_b"][i].reshape(8, 128).T
        lnc.append(l)
        b1 = inp["L1_ffn_b1"][i]
        ba = np.zeros((2816,), np.float32); bg = np.zeros((2816,), np.float32)
        ba[0:2730] = b1[0:2730]; bg[0:2730] = b1[2730:5460]
        bb = np.zeros((128, 44), np.float32)
        bb[:, 0:22] = ba.reshape(22, 128).T
        bb[:, 22:44] = bg.reshape(22, 128).T
        b1c.append(bb)
        b2c.append(inp["L1_ffn_b2"][i].reshape(8, 128).T)
    pr["lncols"] = np.ascontiguousarray(np.stack(lnc))
    pr["bcols1"] = np.ascontiguousarray(np.stack(b1c))
    pr["bcols2"] = np.ascontiguousarray(np.stack(b2c))
    n = T
    half = np.arange(0, 64, 2, dtype=np.float32)
    inv_freq = 1.0 / 10000.0 ** (half / 64)
    t = np.arange(n, dtype=np.float32)
    freqs = t[:, None] * inv_freq[None, :]
    freqs = np.concatenate([freqs, freqs], -1)
    base_scale = (half + 0.4 * 64) / (1.4 * 64)
    power = (t - n // 2) / (256 // 2)
    scale = base_scale[None, :] ** power[:, None]
    scale = np.concatenate([scale, scale], -1)
    cos = np.cos(freqs); sin = np.sin(freqs)
    sign = np.concatenate([-np.ones(32, np.float32), np.ones(32, np.float32)])
    CS = (cos * scale).T
    SS = (sin * scale * sign[None, :]).T
    CI = (cos / scale).T
    SI = (sin / scale * sign[None, :]).T
    pr["rope"] = (CS, SS, CI, SI)
    NEGF = np.float32(-3.0e38)
    jr = np.arange(128)[:, None]; ir = np.arange(128)[None, :]
    MA = np.where(ir <= jr, 0.0, NEGF).astype(np.float32)
    MB = np.where(ir >= jr, 0.0, NEGF).astype(np.float32)
    Z0 = np.zeros((128, 128), np.float32)
    NEGT = np.full((128, 128), NEGF, np.float32)
    std6 = [MA, Z0, MB, MA, Z0, MB]
    m_std = np.concatenate(std6 + [MA, MB], axis=1)
    w0_6 = [NEGT, NEGT, MB, NEGT, Z0, MB]
    m_w0 = np.concatenate(w0_6 + [MA, MB], axis=1)
    pr["mask_std"] = np.ascontiguousarray(m_std)
    pr["mask_w0"] = np.ascontiguousarray(m_w0)
    pp_ = np.arange(128)
    pr["pswap"] = np.ascontiguousarray((pp_[None, :] == (pp_[:, None] ^ 32)).astype(np.float32))
    return pr


def _ffn_core_inputs(prf, c, li):
    b, q = c // 4, c % 4
    t0 = q * TC
    CS, SS, CI, SI = prf["rope"]
    qlo = t0
    tabq = np.concatenate([np.tile(CS[:, qlo:qlo + 2048], (2, 1)), np.tile(SS[:, qlo:qlo + 2048], (2, 1))], axis=1)
    klo = t0 - 256
    if klo >= 0:
        CIs, SIs = CI[:, klo:klo + TE], SI[:, klo:klo + TE]
    else:
        CIs = np.concatenate([np.zeros((64, -klo), np.float32), CI[:, 0:TE + klo]], 1)
        SIs = np.concatenate([np.zeros((64, -klo), np.float32), SI[:, 0:TE + klo]], 1)
    tabk = np.concatenate([np.tile(CIs, (2, 1)), np.tile(SIs, (2, 1))], axis=1)
    m = {
        "wqk": prf["wqk"][li], "wv": prf["wv"][li], "wo": prf["wo"][li],
        "qscol": prf["qscol"][li],
        "tabq": np.ascontiguousarray(tabq.astype(np.float32)),
        "tabk": np.ascontiguousarray(tabk.astype(np.float32)),
        "masks": prf["mask_w0"] if q == 0 else prf["mask_std"],
        "pswap": prf["pswap"],
    }
    return m


def _run_ffn(xcur, prf):
    nca = build_ffn_attn()
    ncf = build_ffn_ffn()
    for li in range(6):
        maps = []
        for c in range(NC):
            b, q = c // 4, c % 4
            if q == 0:
                halo = np.zeros((D, 257), np.float32)
            else:
                halo = xcur[c - 1][:, TC - 257:]
            xe = np.ascontiguousarray(np.concatenate([halo, xcur[c]], axis=1))
            m = _ffn_core_inputs(prf, c, li)
            m["xT"] = xe
            maps.append(m)
        r = run_bass_kernel_spmd(nca, maps, core_ids=list(range(NC)))
        xcur = [np.ascontiguousarray(r.results[c]["xout"]) for c in range(NC)]
        maps = []
        for c in range(NC):
            b, q = c // 4, c % 4
            halo = np.zeros((D, 1), np.float32) if q == 0 else xcur[c - 1][:, TC - 1:]
            xe = np.ascontiguousarray(np.concatenate([halo, xcur[c]], axis=1))
            maps.append({"xT": xe, "w1a": prf["w1a"][li], "w1g": prf["w1g"][li], "w2t": prf["w2t"][li],
                         "lncols": prf["lncols"][li], "bcols1": prf["bcols1"][li], "bcols2": prf["bcols2"][li]})
        r = run_bass_kernel_spmd(ncf, maps, core_ids=list(range(NC)))
        xcur = [np.ascontiguousarray(r.results[c]["xout"]) for c in range(NC)]
    return xcur




def build_b1mid():
    nc = bass.Bass("TRN2", target_bir_lowering=False)
    cx = Ctx(nc)
    em = cx.em
    import contextlib
    ap = lambda n, s, dt=F32, kind="ExternalInput": nc.dram_tensor(n, s, dt, kind=kind).ap()
    lm_in = ap("lm_in", [128, 8 * 512])
    k_head = ap("k_head", [64, T])
    v_head = ap("v_head", [T, 64])
    z2_o = ap("z2_o", [128, 128], kind="ExternalOutput")

    ones = cx.ones_col()
    identf = cx.ident(F32)
    ones_bf = cx.sb("ones_bf", [128, 1], BF16)
    em.op('pool', lambda e: e.memset(ones_bf[:], 1.0))

    lm_sb = _sb7("lm_sb", [128, 8 * 512], BF16)
    em.dma('pool', lambda e: e.dma_start(out=lm_sb[:], in_=lm_in[:, :]))
    k_sb = _sb7("k_sb", [128, T], BF16)
    em.dma('pool', lambda e: e.dma_start(out=k_sb[0:64, :], in_=k_head[:, :]))
    v_sb = _sb7("v_sb", [128, 64 * 64], BF16)
    em.dma('pool', lambda e: e.dma_start(out=v_sb.ap().rearrange("p (a f) -> p a f", a=64), in_=v_head.rearrange("(a p) f -> p a f", p=128)))

    a2buf = _sb7("a2buf", [128, 16 * 512], F32)
    rsbuf = _sb7("rsbuf", [128, 34], F32)
    csbuf = _sb7("csbuf", [1, 16 * 256], F32)
    sc = cx.sb("sc", [1, 8], F32)
    rsc_col = _sb7("rsc_col", [128, 1], F32)

    with contextlib.ExitStack() as st:
        sb = lambda n, s, dt=F32: st.enter_context(nc.sbuf_tensor(n, s, dt))
        pl2 = st.enter_context(nc.psum_tensor("pl2", [128, 256], F32))
        pcs = st.enter_context(nc.psum_tensor("pcs", [1, 256], F32))
        mc = sb("mc", [128, 2])
        for p in range(16):
            tile, bh = p // 2, (p % 2) * 64
            for it in range(2):
                em.op('pe', lambda e, tile=tile, bh=bh, it=it: e.matmul(
                    pl2[:, :], lm_sb[bh:bh + 64, tile * 512 + it * 128: tile * 512 + (it + 1) * 128],
                    lm_sb[bh:bh + 64, tile * 512 + 256: tile * 512 + 512], start=True, stop=True))
                em.op('dve', lambda e: e.reduce_max(out=mc[:, 0:1], in_=pl2[:], axis=AX.X, negate=True))
                em.op('act', lambda e, p=p, it=it: e.activation(out=a2buf[:, p * 512 + it * 256: p * 512 + (it + 1) * 256], in_=pl2[:], func=AF.Exp, bias=mc[:, 0:1]))
                em.op('dve', lambda e, p=p, it=it: e.reduce_sum(out=mc[:, 1:2], in_=a2buf[:, p * 512 + it * 256: p * 512 + (it + 1) * 256], axis=AX.X))
                em.op('dve', lambda e: e.reciprocal(out=mc[:, 1:2], in_=mc[:, 1:2]))
                em.op('dve', lambda e, p=p, it=it: e.tensor_scalar(out=a2buf[:, p * 512 + it * 256: p * 512 + (it + 1) * 256], in0=a2buf[:, p * 512 + it * 256: p * 512 + (it + 1) * 256], scalar1=mc[:, 1:2], scalar2=None, op0=ALU.mult))
                em.op('dve', lambda e, p=p, it=it: e.reduce_sum(out=rsbuf[:, p * 2 + it:p * 2 + it + 1], in_=a2buf[:, p * 512 + it * 256: p * 512 + (it + 1) * 256], axis=AX.X))
            for it in range(2):
                em.op('pe', lambda e, p=p, it=it: e.matmul(pcs[0:1, :], ones[:, 0:1], a2buf[:, p * 512 + it * 256: p * 512 + (it + 1) * 256], start=(it == 0), stop=(it == 1)))
            em.op('dve', lambda e, p=p: e.tensor_copy(out=csbuf[0:1, p * 256:(p + 1) * 256], in_=pcs[0:1, :]))
        em.op('dve', lambda e: e.reduce_max(out=rsbuf[:, 32:33], in_=rsbuf[:, 0:32], axis=AX.X))
        em.op('pe', lambda e: e.transpose(pcs[0:1, 0:128], rsbuf[:, 32:33], identf[:]))
        em.op('dve', lambda e: e.reduce_max(out=sc[0:1, 0:1], in_=pcs[0:1, 0:128], axis=AX.X))
        em.op('dve', lambda e: e.reduce_max(out=sc[0:1, 1:2], in_=csbuf[0:1, :], axis=AX.X))
        em.op('dve', lambda e: e.tensor_tensor(out=sc[0:1, 0:1], in0=sc[0:1, 0:1], in1=sc[0:1, 1:2], op=ALU.mult))
        em.op('dve', lambda e: e.reciprocal(out=sc[0:1, 0:1], in_=sc[0:1, 0:1]))
        em.op('pe', lambda e: e.matmul(pl2[:, 0:1], ones[0:1, :], sc[0:1, 0:1], start=True, stop=True))
        em.op('dve', lambda e: e.tensor_copy(out=rsc_col[:], in_=pl2[:, 0:1]))

    xb_ = _sb7("xb_", [128, 512], F32)
    em.op('dve', lambda e: e.tensor_copy(out=xb_[:], in_=a2buf[:, 0:512]))
    xT_ = _sb7("xT_", [128, 512], F32)
    zb_ = _sb7("zb_", [128, 512], F32)
    zT_ = _sb7("zT_", [128, 512], F32)
    xz_ = _sb7("xz_", [128, 512], F32)
    xzT_ = _sb7("xzT_", [128, 512], F32)
    t1_ = _sb7("t1_", [128, 512], F32)
    t2_ = _sb7("t2_", [128, 512], F32)

    with contextlib.ExitStack() as st:
        ptA = st.enter_context(nc.psum_tensor("ptA", [128, 256], F32))
        ptB = st.enter_context(nc.psum_tensor("ptB", [128, 128], F32))

        def T256(dst, src):
            for a_ in range(2):
                for b_ in range(2):
                    em.op('pe', lambda e, a_=a_, b_=b_: e.transpose(ptB[:, :], src[:, a_ * 256 + b_ * 128: a_ * 256 + (b_ + 1) * 128], identf[:]))
                    em.op('dve', lambda e, a_=a_, b_=b_: e.tensor_copy(out=dst[:, b_ * 256 + a_ * 128: b_ * 256 + (a_ + 1) * 128], in_=ptB[:, :]))

        def MM(dst, lhsT_buf, rhs_buf, scale=1.0, diag=None):
            for it in range(2):
                for kt in range(2):
                    em.op('pe', lambda e, it=it, kt=kt: e.matmul(
                        ptA[:, :], lhsT_buf[:, kt * 256 + it * 128: kt * 256 + (it + 1) * 128],
                        rhs_buf[:, kt * 256: (kt + 1) * 256], start=(kt == 0), stop=(kt == 1)))
                if diag is None:
                    em.op('act', lambda e, it=it, scale=scale: e.activation(out=dst[:, it * 256:(it + 1) * 256], in_=ptA[:, :], func=AF.Copy, scale=scale))
                else:
                    em.op('act', lambda e, it=it: e.activation(out=dst[:, it * 256:(it + 1) * 256], in_=ptA[:, :], func=AF.Copy, scale=-1.0))
                    em.op('dve', lambda e, it=it, diag=diag: e.scalar_tensor_tensor(
                        out=dst[:, it * 256 + it * 128: it * 256 + (it + 1) * 128], in0=identf[:], scalar=float(diag),
                        in1=dst[:, it * 256 + it * 128: it * 256 + (it + 1) * 128], op0=ALU.mult, op1=ALU.add))

        T256(xT_, xb_)
        em.op('dve', lambda e: e.tensor_scalar(out=zb_[:], in0=xT_[:], scalar1=rsc_col[:, 0:1], scalar2=None, op0=ALU.mult))
        for _i in range(6):
            MM(xz_, xT_, zb_)
            T256(xzT_, xz_)
            em.op('act', lambda e: e.activation(out=t1_[:], in_=xz_[:], func=AF.Copy, scale=-1.0))
            for it in range(2):
                em.op('dve', lambda e, it=it: e.scalar_tensor_tensor(
                    out=t1_[:, it * 256 + it * 128: it * 256 + (it + 1) * 128], in0=identf[:], scalar=7.0,
                    in1=t1_[:, it * 256 + it * 128: it * 256 + (it + 1) * 128], op0=ALU.mult, op1=ALU.add))
            MM(t2_, xzT_, t1_, diag=15.0)
            MM(t1_, xzT_, t2_, diag=13.0)
            T256(zT_, zb_)
            MM(zb_, zT_, t1_, scale=0.25)
    with contextlib.ExitStack() as st:
        sb = lambda n, s, dt=F32: st.enter_context(nc.sbuf_tensor(n, s, dt))
        ptm = st.enter_context(nc.psum_tensor("ptm3", [128, 512], F32))
        pst = st.enter_context(nc.psum_tensor("pst3", [128, 256], F32))
        pz3 = st.enter_context(nc.psum_tensor("pz3", [64, 256], F32))
        pzz = st.enter_context(nc.psum_tensor("pzz", [1, 256], F32))
        ptr_ = st.enter_context(nc.psum_tensor("ptr_", [1, 128], F32))
        pzb3 = st.enter_context(nc.psum_tensor("pzb3", [64, 256], F32))
        mcols = sb("mcols", [128, 18])
        mrow = sb("mrow", [1, 256])
        expt = sb("expt", [128, 256], BF16)
        z3f = sb("z3f", [64, 256])
        z3n = sb("z3n", [128, 128])
        zbc3 = sb("zbc3", [64, 256])
        zr3 = sb("zr3", [1, 256])
        for it in range(2):
            for jc in range(16):
                em.op('pe', lambda e, it=it, jc=jc: e.matmul(
                    ptm[:, :], lm_sb[0:64, it * 128:(it + 1) * 128], k_sb[0:64, jc * 512:(jc + 1) * 512], start=True, stop=True))
                em.op('dve', lambda e, jc=jc: e.reduce_max(out=mcols[:, jc:jc + 1], in_=ptm[:], axis=AX.X))
            em.op('dve', lambda e: e.reduce_max(out=mcols[:, 16:17], in_=mcols[:, 0:16], axis=AX.X, negate=True))
            em.op('pe', lambda e: e.transpose(ptr_[0:1, 0:128], mcols[:, 16:17], identf[:]))
            em.op('dve', lambda e, it=it: e.tensor_copy(out=mrow[0:1, it * 128:(it + 1) * 128], in_=ptr_[0:1, 0:128]))
        for jt in range(64):
            em.op('pe', lambda e, jt=jt: e.matmul(pst[:, :], k_sb[0:64, jt * 128:(jt + 1) * 128], lm_sb[0:64, 0:256], start=True, stop=False))
            em.op('pe', lambda e: e.matmul(pst[:, :], ones[0:1, :], mrow[0:1, :], start=False, stop=True))
            em.op('act', lambda e: e.activation(out=expt[:], in_=pst[:], func=AF.Exp))
            em.op('pe', lambda e, jt=jt: e.matmul(pz3[0:64, :], v_sb[:, jt * 64:(jt + 1) * 64], expt[:], start=(jt == 0), stop=(jt == 63)))
            em.op('pe', lambda e, jt=jt: e.matmul(pzz[0:1, :], ones_bf[:, 0:1], expt[:], start=(jt == 0), stop=(jt == 63)))
        em.op('dve', lambda e: e.tensor_copy(out=zr3[0:1, :], in_=pzz[0:1, :]))
        em.op('dve', lambda e: e.reciprocal(out=zr3[0:1, :], in_=zr3[0:1, :]))
        em.op('pe', lambda e: e.matmul(pzb3[0:64, :], ones[0:1, 0:64], zr3[0:1, :], start=True, stop=True))
        em.op('dve', lambda e: e.tensor_copy(out=zbc3[0:64, :], in_=pzb3[0:64, :]))
        em.op('dve', lambda e: e.tensor_copy(out=z3f[0:64, :], in_=pz3[0:64, :]))
        em.op('dve', lambda e: e.tensor_tensor(out=z3f[0:64, :], in0=z3f[0:64, :], in1=zbc3[0:64, :], op=ALU.mult))
        for l in range(2):
            em.op('pe', lambda e, l=l: e.transpose(ptm[:, 0:64], z3f[0:64, l * 128:(l + 1) * 128], identf[0:64, 0:64]))
            em.op('dve', lambda e, l=l: e.tensor_copy(out=z3n[:, l * 64:(l + 1) * 64], in_=ptm[:, 0:64]))
        z2b = sb("z2b", [128, 128])
        T256(xT_, zb_)
        for it in range(2):
            for kt in range(2):
                em.op('pe', lambda e, it=it, kt=kt: e.matmul(
                    pst[:, 0:64], xT_[:, kt * 256 + it * 128: kt * 256 + (it + 1) * 128], z3n[:, kt * 64:(kt + 1) * 64], start=(kt == 0), stop=(kt == 1)))
            em.op('dve', lambda e, it=it: e.tensor_copy(out=z2b[:, it * 64:(it + 1) * 64], in_=pst[:, 0:64]))
        em.dma('sp', lambda e: e.dma_start(out=z2_o[:, :], in_=z2b[:]))
    em.build()
    cx.st.close()
    return nc



def build_b1post():
    nc = bass.Bass("TRN2", target_bir_lowering=False)
    cx = Ctx(nc)
    em = cx.em
    import contextlib
    ap = lambda n, s, dt=F32, kind="ExternalInput": nc.dram_tensor(n, s, dt, kind=kind).ap()
    z2_in = ap("z2_in", [128, 8 * 128])
    q_in = ap("q_in", [512, TC])
    kl_in = ap("kl_in", [128, 4 * 256])
    v_in = ap("v_in", [512, TC + 32])
    cw_in = ap("cw_in", [128, 4 * 33])
    outw = ap("outw", [8, 4, 128, 128])
    outb = ap("outb", [128, 8])
    xn1T = ap("xn1T", [D, TC])
    wq_in = ap("wq_in", [8, 128, 256])
    wk_in = ap("wk_in", [8, 128, 256])
    enc_out = ap("enc_o", [D, TC], kind="ExternalOutput")
    qe_out = ap("qe_o", [TC, 256], kind="ExternalOutput")
    ke_out = ap("ke_o", [TC, 256], kind="ExternalOutput")

    ones = cx.ones_col()
    identf = cx.ident(F32)
    ones_bf = cx.sb("ones_bf", [128, 1], BF16)
    em.op('pool', lambda e: e.memset(ones_bf[:], 1.0))

    z2_sb = cx.sb("z2_sb", [128, 8 * 128], BF16)
    em.dma('pool', lambda e: e.dma_start(out=z2_sb[:], in_=z2_in[:, :]))
    qb = cx.sb("qb", [128, 4 * TC], BF16)
    for g in range(4):
        em.dma('pool', lambda e, g=g: e.dma_start(out=qb[:, g * TC:(g + 1) * TC], in_=q_in[g * 128:(g + 1) * 128, :]))
    kl_sb = cx.sb("kl_sb", [128, 4 * 256], BF16)
    em.dma('pool', lambda e: e.dma_start(out=kl_sb[:], in_=kl_in[:, :]))
    vb = cx.sb("vb", [128, 4 * (TC + 32)], F32)
    for g in range(4):
        em.dma('sp', lambda e, g=g: e.dma_start(out=vb[:, g * (TC + 32):(g + 1) * (TC + 32)], in_=v_in[g * 128:(g + 1) * 128, :]))
    cw_sb = cx.sb("cw_sb", [128, 4 * 33], F32)
    em.dma('sp', lambda e: e.dma_start(out=cw_sb[:], in_=cw_in[:, :]))
    ob_sb = cx.sb("ob_sb", [128, 8], F32)
    em.dma('sp', lambda e: e.dma_start(out=ob_sb[:], in_=outb[:, :]))

    nyT = cx.sb("nyT", [128, 4 * TC], BF16)
    with contextlib.ExitStack() as st:
        sb = lambda n, s, dt=F32: st.enter_context(nc.sbuf_tensor(n, s, dt))
        cacc = sb("cacc", [128, TC])
        ptm = st.enter_context(nc.psum_tensor("ptm4", [128, 512], F32))
        pst = st.enter_context(nc.psum_tensor("pst4", [128, 512], F32))
        pat = st.enter_context(nc.psum_tensor("pat4", [128, TC], F32))
        pz = st.enter_context(nc.psum_tensor("pz4", [1, 512], F32))
        pzb = st.enter_context(nc.psum_tensor("pzb4", [128, 512], F32))
        mc = sb("mc4", [128, 16])
        mrow = sb("mrow4", [1, TC])
        a1t = sb("a1t", [128, 2 * 512], BF16)
        zrow = sb("zrow4", [128, TC])
        zbcs = sb("zbcs4", [128, 512])
        tno = sb("tno", [128, 512])
        for h in range(8):
            g, bh = h // 2, (h % 2) * 64
            for it in range(16):
                em.op('pe', lambda e, g=g, bh=bh, it=it: e.matmul(
                    ptm[:, 0:256], qb[bh:bh + 64, g * TC + it * 128: g * TC + (it + 1) * 128],
                    kl_sb[bh:bh + 64, g * 256:(g + 1) * 256], start=True, stop=True))
                em.op('dve', lambda e, it=it: e.reduce_max(out=mc[:, 0:1], in_=ptm[:, 0:256], axis=AX.X, negate=True))
                em.op('pe', lambda e: e.transpose(pz[0:1, 0:128], mc[:, 0:1], identf[:]))
                em.op('dve', lambda e, it=it: e.tensor_copy(out=mrow[0:1, it * 128:(it + 1) * 128], in_=pz[0:1, 0:128]))
            for (off, w) in _chunks(TC, 512):
                for lt in range(2):
                    em.op('pe', lambda e, g=g, bh=bh, lt=lt, off=off, w=w: e.matmul(
                        pst[:, 0:w], kl_sb[bh:bh + 64, g * 256 + lt * 128: g * 256 + (lt + 1) * 128],
                        qb[bh:bh + 64, g * TC + off: g * TC + off + w], start=True, stop=False))
                    em.op('pe', lambda e, lt=lt, off=off, w=w: e.matmul(
                        pst[:, 0:w], ones[0:1, :], mrow[0:1, off:off + w], start=False, stop=True))
                    em.op('act', lambda e, lt=lt, w=w: e.activation(out=a1t[:, lt * 512:lt * 512 + w], in_=pst[:, 0:w], func=AF.Exp))
                for lt in range(2):
                    em.op('pe', lambda e, lt=lt, h=h, bh=bh, off=off, w=w: e.matmul(
                        pat[bh:bh + 64, off:off + w], z2_sb[:, (h * 2 + lt) * 64:(h * 2 + lt) * 64 + 64],
                        a1t[:, lt * 512:lt * 512 + w], start=(lt == 0), stop=(lt == 1)))
                    em.op('pe', lambda e, lt=lt, off=off, w=w: e.matmul(
                        pz[0:1, 0:w], ones_bf[:, 0:1], a1t[:, lt * 512:lt * 512 + w], start=(lt == 0), stop=(lt == 1)))
                em.op('dve', lambda e, off=off, w=w: e.tensor_copy(out=zrow[0:1, off:off + w], in_=pz[0:1, 0:w]))
            em.op('dve', lambda e: e.reciprocal(out=zrow[0:1, :], in_=zrow[0:1, :]))
            if h % 2 == 1:
                for half in range(2):
                    bh2 = half * 64
                    pass
            for (off, w) in _chunks(TC, 512):
                em.op('pe', lambda e, off=off, w=w, bh=bh: e.matmul(pzb[bh:bh + 64, 0:w], ones[0:1, 0:64], zrow[0:1, off:off + w], start=True, stop=True))
                em.op('dve', lambda e, off=off, w=w, bh=bh: e.tensor_copy(out=zbcs[bh:bh + 64, 0:w], in_=pzb[bh:bh + 64, 0:w]))
                em.op('dve', lambda e, off=off, w=w, bh=bh, g=g: e.scalar_tensor_tensor(
                    out=cacc[bh:bh + 64, off:off + w], in0=vb[bh:bh + 64, g * (TC + 32) + off: g * (TC + 32) + off + w],
                    scalar=cw_sb[bh:bh + 64, g * 33:g * 33 + 1], in1=vb[bh:bh + 64, g * (TC + 32) + off: g * (TC + 32) + off + w], op0=ALU.mult, op1=ALU.bypass))
                for k in range(1, 33):
                    em.op('dve', lambda e, off=off, w=w, bh=bh, g=g, k=k: e.scalar_tensor_tensor(
                        out=cacc[bh:bh + 64, off:off + w], in0=vb[bh:bh + 64, g * (TC + 32) + off + k: g * (TC + 32) + off + k + w],
                        scalar=cw_sb[bh:bh + 64, g * 33 + k:g * 33 + k + 1], in1=cacc[bh:bh + 64, off:off + w], op0=ALU.mult, op1=ALU.add))
                em.op('dve', lambda e, off=off, w=w, bh=bh: e.tensor_tensor(out=tno[bh:bh + 64, 0:w], in0=pat[bh:bh + 64, off:off + w], in1=zbcs[bh:bh + 64, 0:w], op=ALU.mult))
                em.op('dve', lambda e, off=off, w=w, bh=bh, g=g: e.tensor_tensor(out=nyT[bh:bh + 64, g * TC + off: g * TC + off + w], in0=tno[bh:bh + 64, 0:w], in1=cacc[bh:bh + 64, off:off + w], op=ALU.add))

    encb = cx.sb("encb", [128, 8 * TC], BF16)
    encf_sp = nc.dram_tensor("encf_sp", [D, TC], F32, kind="Internal").ap()
    with contextlib.ExitStack() as st:
        sb = lambda n, s, dt=F32: st.enter_context(nc.sbuf_tensor(n, s, dt))
        wt = sb("wt5", [128, 4 * 128], BF16)
        xnf = sb("xnf5", [128, 512])
        ef = sb("ef5", [128, 512])
        pxg = st.enter_context(nc.psum_tensor("pxg", [128, 512], F32))
        for ot in range(8):
            em.dma('pool', lambda e, ot=ot: e.dma_start(out=wt.ap().rearrange("p (a f) -> p a f", a=4), in_=outw[ot].rearrange("a p f -> p a f")))
            for (off, w) in _chunks(TC, 512):
                for it in range(4):
                    em.op('pe', lambda e, ot=ot, off=off, w=w, it=it: e.matmul(
                        pxg[:, 0:w], wt[:, it * 128:(it + 1) * 128], nyT[:, it * TC + off:it * TC + off + w],
                        start=(it == 0), stop=(it == 3)))
                em.dma('sp', lambda e, ot=ot, off=off, w=w: e.dma_start(out=xnf[:, 0:w], in_=xn1T[ot * 128:(ot + 1) * 128, off:off + w]))
                em.op('dve', lambda e, ot=ot, off=off, w=w: e.tensor_scalar(out=ef[:, 0:w], in0=pxg[:, 0:w], scalar1=ob_sb[:, ot:ot + 1], scalar2=None, op0=ALU.add))
                em.op('dve', lambda e, off=off, w=w: e.tensor_tensor(out=ef[:, 0:w], in0=ef[:, 0:w], in1=xnf[:, 0:w], op=ALU.add))
                em.dma('sp', lambda e, ot=ot, off=off, w=w: e.dma_start(out=enc_out[ot * 128:(ot + 1) * 128, off:off + w], in_=ef[:, 0:w]))
                em.dma('sp', lambda e, ot=ot, off=off, w=w: e.dma_start(out=encf_sp[ot * 128:(ot + 1) * 128, off:off + w], in_=ef[:, 0:w]))

    with contextlib.ExitStack() as st:
        sb = lambda n, s, dt=F32: st.enter_context(nc.sbuf_tensor(n, s, dt))
        wq_sb = sb("wq_sb", [128, 8 * 256])
        wk_sb = sb("wk_sb", [128, 8 * 256])
        enct = sb("enct", [128, 8 * 128])
        qe = sb("qe", [128, 256])
        pqe = st.enter_context(nc.psum_tensor("pqe", [128, 256], F32))
        pke = st.enter_context(nc.psum_tensor("pke", [128, 256], F32))
        for i in range(8):
            em.dma('sp', lambda e, i=i: e.dma_start(out=wq_sb[:, i * 256:(i + 1) * 256], in_=wq_in[i]))
            em.dma('sp', lambda e, i=i: e.dma_start(out=wk_sb[:, i * 256:(i + 1) * 256], in_=wk_in[i]))
        for tt in range(16):
            for i in range(8):
                em.dma('sp', lambda e, tt=tt, i=i: e.dma_start(out=enct[:, i * 128:(i + 1) * 128], in_=encf_sp[i * 128:(i + 1) * 128, tt * 128:(tt + 1) * 128]))
            for i in range(8):
                em.op('pe', lambda e, tt=tt, i=i: e.matmul(pqe[:], enct[:, i * 128:(i + 1) * 128], wq_sb[:, i * 256:(i + 1) * 256], start=(i == 0), stop=(i == 7)))
            for i in range(8):
                em.op('pe', lambda e, tt=tt, i=i: e.matmul(pke[:], enct[:, i * 128:(i + 1) * 128], wk_sb[:, i * 256:(i + 1) * 256], start=(i == 0), stop=(i == 7)))
            em.op('dve', lambda e: e.tensor_copy(out=qe[:], in_=pqe[:]))
            em.dma('sp', lambda e, tt=tt: e.dma_start(out=qe_out[tt * 128:(tt + 1) * 128, :], in_=qe[:]))
            em.op('dve', lambda e: e.tensor_copy(out=qe[:], in_=pke[:]))
            em.dma('sp', lambda e, tt=tt: e.dma_start(out=ke_out[tt * 128:(tt + 1) * 128, :], in_=qe[:]))
    em.build()
    cx.st.close()
    return nc



NEDGE = 16384
NG = NEDGE // 128


def build_b1edge():
    nc = bass.Bass("TRN2", target_bir_lowering=False)
    cx = Ctx(nc)
    em = cx.em
    import contextlib
    ap = lambda n, s, dt=F32, kind="ExternalInput": nc.dram_tensor(n, s, dt, kind=kind).ap()
    qe_all = ap("qe_all", [T, 256])
    ke_all = ap("ke_all", [T, 256])
    ridx = ap("ridx", [128, NG], U32)
    cidx = ap("cidx", [128, NG], U32)
    vals = ap("vals", [128, NG])
    apart = ap("apart", [128, 64], kind="ExternalOutput")

    i64hi = cx.sb("i64hi", [128, 128], F32)
    i64lo = cx.sb("i64lo", [128, 64], F32)
    em.op('pool', lambda e: e.iota(i64hi[:], [[64, 128]], base=0, channel_multiplier=0, allow_small_or_imprecise_dtypes=True))
    em.op('pool', lambda e: e.iota(i64lo[:], [[1, 64]], base=0, channel_multiplier=0, allow_small_or_imprecise_dtypes=True))

    rix = cx.sb("rix", [128, NG], U32)
    cix = cx.sb("cix", [128, NG], U32)
    vls = cx.sb("vls", [128, NG], F32)
    rfl = cx.sb("rfl", [128, NG], F32)
    em.dma('sp', lambda e: e.dma_start(out=rix[:], in_=ridx[:, :]))
    em.dma('sp', lambda e: e.dma_start(out=cix[:], in_=cidx[:, :]))
    em.dma('sp', lambda e: e.dma_start(out=vls[:], in_=vals[:, :]))
    em.op('dve', lambda e: e.tensor_copy(out=rfl[:], in_=rix[:]))

    with contextlib.ExitStack() as st:
        sb = lambda n, s, dt=F32: st.enter_context(nc.sbuf_tensor(n, s, dt))
        qg = sb("qg", [128, 256]); kg = sb("kg", [128, 256])
        dcol = sb("dcol", [128, 4])
        dtile = sb("dtile", [128, 128]); m1 = sb("m1", [128, 128])
        ohhi = sb("ohhi", [128, 128]); ohlo = sb("ohlo", [128, 64])
        trash = sb("trash", [128, 256])
        pA = st.enter_context(nc.psum_tensor("pA", [128, 64], F32))
        for g_ in range(NG):
            em.dma('pool', lambda e, g_=g_: e.indirect_dma_start(out=qg[:], out_offset=None, in_=qe_all[:, :], in_offset=IndirectOffsetOnAxis(ap=rix[:, g_:g_ + 1], axis=0)))
            em.dma('pool', lambda e, g_=g_: e.indirect_dma_start(out=kg[:], out_offset=None, in_=ke_all[:, :], in_offset=IndirectOffsetOnAxis(ap=cix[:, g_:g_ + 1], axis=0)))
            em.op('dve', lambda e: e.scalar_tensor_tensor(out=trash[:], in0=qg[:], scalar=1.0, in1=kg[:], op0=ALU.mult, op1=ALU.mult, accum_out=dcol[:, 0:1]))
            em.op('dve', lambda e, g_=g_: e.tensor_scalar(out=dcol[:, 1:2], in0=dcol[:, 0:1], scalar1=vls[:, g_:g_ + 1], scalar2=0.0625, op0=ALU.mult, op1=ALU.mult))
            em.op('dve', lambda e, g_=g_: e.tensor_scalar(out=dtile[:], in0=i64hi[:], scalar1=rfl[:, g_:g_ + 1], scalar2=None, op0=ALU.subtract))
            em.op('dve', lambda e: e.tensor_scalar(out=m1[:], in0=dtile[:], scalar1=-63.5, scalar2=None, op0=ALU.is_ge))
            em.op('dve', lambda e: e.scalar_tensor_tensor(out=ohhi[:], in0=dtile[:], scalar=0.5, in1=m1[:], op0=ALU.is_lt, op1=ALU.mult))
            em.op('dve', lambda e: e.scalar_tensor_tensor(out=m1[:], in0=ohhi[:], scalar=1.0, in1=i64hi[:], op0=ALU.mult, op1=ALU.mult, accum_out=dcol[:, 2:3]))
            em.op('dve', lambda e, g_=g_: e.tensor_scalar(out=dcol[:, 3:4], in0=rfl[:, g_:g_ + 1], scalar1=dcol[:, 2:3], scalar2=None, op0=ALU.subtract))
            em.op('dve', lambda e: e.tensor_scalar(out=ohlo[:], in0=i64lo[:], scalar1=dcol[:, 3:4], scalar2=None, op0=ALU.is_equal))
            em.op('dve', lambda e: e.tensor_scalar(out=ohlo[:], in0=ohlo[:], scalar1=dcol[:, 1:2], scalar2=None, op0=ALU.mult))
            em.op('pe', lambda e, g_=g_: e.matmul(pA[:], ohhi[:], ohlo[:], start=(g_ == 0), stop=(g_ == NG - 1)))
        acp = sb("acp", [128, 64])
        em.op('dve', lambda e: e.tensor_copy(out=acp[:], in_=pA[:]))
        em.dma('sp', lambda e: e.dma_start(out=apart[:, :], in_=acp[:]))
    em.build()
    cx.st.close()
    return nc



def build_b1mix():
    nc = bass.Bass("TRN2", target_bir_lowering=False)
    cx = Ctx(nc)
    em = cx.em
    import contextlib
    ap = lambda n, s, dt=F32, kind="ExternalInput": nc.dram_tensor(n, s, dt, kind=kind).ap()
    a_all = ap("a_all", [8, T])
    a_loc = ap("a_loc", [8, TC])
    valT = ap("valT", [D, TC])
    encT = ap("encT", [D, TC])
    s1_out = ap("s1_o", [D, TC], kind="ExternalOutput")

    ones = cx.ones_col()
    a_sb = cx.sb("a_sb", [8, T], F32)
    em.dma('sp', lambda e: e.dma_start(out=a_sb[:], in_=a_all[:, :]))
    al_sb = cx.sb("al_sb", [8, TC], F32)
    em.dma('sp', lambda e: e.dma_start(out=al_sb[:], in_=a_loc[:, :]))
    srow = cx.sb("srow", [1, T], F32)
    lrow = cx.sb("lrow", [1, TC], F32)
    stat = cx.sb("stat", [1, 4], F32)

    with contextlib.ExitStack() as st:
        sb = lambda n, s, dt=F32: st.enter_context(nc.sbuf_tensor(n, s, dt))
        pr_ = st.enter_context(nc.psum_tensor("pr5", [1, 512], F32))
        pb_ = st.enter_context(nc.psum_tensor("pb5", [128, 512], F32))
        for (off, w) in _chunks(T, 512):
            em.op('pe', lambda e, off=off, w=w: e.matmul(pr_[0:1, 0:w], ones[0:8, 0:1], a_sb[0:8, off:off + w], start=True, stop=True))
            em.op('dve', lambda e, off=off, w=w: e.tensor_copy(out=srow[0:1, off:off + w], in_=pr_[0:1, 0:w]))
        for (off, w) in _chunks(TC, 512):
            em.op('pe', lambda e, off=off, w=w: e.matmul(pr_[0:1, 0:w], ones[0:8, 0:1], al_sb[0:8, off:off + w], start=True, stop=True))
            em.op('dve', lambda e, off=off, w=w: e.tensor_copy(out=lrow[0:1, off:off + w], in_=pr_[0:1, 0:w]))
        em.op('dve', lambda e: e.reduce_max(out=stat[0:1, 0:1], in_=srow[0:1, :], axis=AX.X, negate=True))
        em.op('act', lambda e: e.activation(out=srow[0:1, :], in_=srow[0:1, :], func=AF.Exp, bias=stat[0:1, 0:1]))
        em.op('dve', lambda e: e.reduce_sum(out=stat[0:1, 1:2], in_=srow[0:1, :], axis=AX.X))
        em.op('dve', lambda e: e.reciprocal(out=stat[0:1, 1:2], in_=stat[0:1, 1:2]))
        em.op('act', lambda e: e.activation(out=lrow[0:1, :], in_=lrow[0:1, :], func=AF.Exp, bias=stat[0:1, 0:1]))
        em.op('dve', lambda e: e.tensor_scalar(out=lrow[0:1, :], in0=lrow[0:1, :], scalar1=stat[0:1, 1:2], scalar2=None, op0=ALU.mult))
        vf = sb("vf", [128, 512]); ef = sb("ef", [128, 512])
        xl = sb("xl", [128, 512]); sw = sb("sw", [128, 512]); t1 = sb("t1", [128, 512])
        for i in range(8):
            for (off, w) in _chunks(TC, 512):
                em.dma('sp', lambda e, i=i, off=off, w=w: e.dma_start(out=vf[:, 0:w], in_=valT[i * 128:(i + 1) * 128, off:off + w]))
                em.dma('sp', lambda e, i=i, off=off, w=w: e.dma_start(out=ef[:, 0:w], in_=encT[i * 128:(i + 1) * 128, off:off + w]))
                em.op('pe', lambda e, off=off, w=w: e.matmul(pb_[:, 0:w], ones[0:1, :], lrow[0:1, off:off + w], start=True, stop=True))
                em.op('dve', lambda e, w=w: e.tensor_tensor(out=xl[:, 0:w], in0=vf[:, 0:w], in1=pb_[:, 0:w], op=ALU.mult))
                em.op('act', lambda e, w=w: e.activation(out=sw[:, 0:w], in_=xl[:, 0:w], func=AF.Sigmoid, scale=-1.0))
                em.op('act', lambda e, w=w: e.activation(out=sw[:, 0:w], in_=sw[:, 0:w], func=AF.Square))
                em.op('dve', lambda e, w=w: e.tensor_tensor(out=t1[:, 0:w], in0=xl[:, 0:w], in1=ef[:, 0:w], op=ALU.subtract))
                em.op('dve', lambda e, w=w: e.tensor_tensor(out=t1[:, 0:w], in0=t1[:, 0:w], in1=sw[:, 0:w], op=ALU.mult))
                em.op('dve', lambda e, w=w: e.tensor_tensor(out=t1[:, 0:w], in0=t1[:, 0:w], in1=ef[:, 0:w], op=ALU.add))
                em.op('dve', lambda e, w=w: e.tensor_scalar(out=t1[:, 0:w], in0=t1[:, 0:w], scalar1=2.0, scalar2=None, op0=ALU.mult))
                em.dma('sp', lambda e, i=i, off=off, w=w: e.dma_start(out=s1_out[i * 128:(i + 1) * 128, off:off + w], in_=t1[:, 0:w]))
    em.build()
    cx.st.close()
    return nc



def build_final():
    nc = bass.Bass("TRN2", target_bir_lowering=False)
    cx = Ctx(nc)
    em = cx.em
    import contextlib
    ap = lambda n, s, dt=F32, kind="ExternalInput": nc.dram_tensor(n, s, dt, kind=kind).ap()
    x1T = ap("x1T", [D, TC])
    s1T = ap("s1T", [D, TC])
    y6T = ap("y6T", [D, TC])
    out = ap("out", [D, TC], kind="ExternalOutput")
    with contextlib.ExitStack() as st:
        sb = lambda n, s, dt=F32: st.enter_context(nc.sbuf_tensor(n, s, dt))
        a = sb("a6", [128, TC]); b = sb("b6", [128, TC]); c = sb("c6", [128, TC])
        for i in range(8):
            em.dma('sp', lambda e, i=i: e.dma_start(out=a[:], in_=y6T[i * 128:(i + 1) * 128, :]))
            em.dma('sp', lambda e, i=i: e.dma_start(out=b[:], in_=s1T[i * 128:(i + 1) * 128, :]))
            em.dma('sp', lambda e, i=i: e.dma_start(out=c[:], in_=x1T[i * 128:(i + 1) * 128, :]))
            em.op('act', lambda e: e.activation(out=a[:], in_=a[:], func=AF.Silu))
            em.op('dve', lambda e: e.tensor_tensor(out=a[:], in0=a[:], in1=b[:], op=ALU.mult))
            em.op('dve', lambda e: e.tensor_tensor(out=a[:], in0=a[:], in1=c[:], op=ALU.add))
            em.dma('sp', lambda e, i=i: e.dma_start(out=out[i * 128:(i + 1) * 128, :], in_=a[:]))
    em.build()
    cx.st.close()
    return nc



def _asm_fm(parts, batch_rows=True):
    return np.concatenate(parts[0:4], axis=1)


def _run_b1(inp, x1, prf):
    pr1 = _prep_b1(inp)
    res_pre = _run_b1pre(x1, pr1)
    xn1 = [res_pre[c]["xn1"] for c in range(NC)]
    val = [res_pre[c]["val_o"] for c in range(NC)]
    q_b0 = np.concatenate([res_pre[c]["q_o"] for c in range(4)], axis=1)
    k_b0 = np.concatenate([res_pre[c]["k_o"] for c in range(4)], axis=1)
    v_b0 = np.concatenate([res_pre[c]["v_o"] for c in range(4)], axis=1)
    ql = np.zeros((2, 8, 64, 256), np.float32)
    kl = np.zeros((2, 8, 64, 256), np.float32)
    for c in range(NC):
        b, q = c // 4, c % 4
        lm = res_pre[c]["lm_o"]
        ql[b, :, :, q * 64:(q + 1) * 64] = lm[:, 0:64].reshape(8, 64, 64)
        kl[b, :, :, q * 64:(q + 1) * 64] = lm[:, 64:128].reshape(8, 64, 64)

    nc_mid = build_b1mid()
    maps = []
    for c in range(NC):
        hsel = c
        pairs = [(0, hsel)] + [(b, h) for b in range(2) for h in range(8) if (b, h) != (0, hsel)]
        lm_in = np.zeros((128, 8 * 512), np.float32)
        for p, (b, h) in enumerate(pairs):
            tile, bh = p // 2, (p % 2) * 64
            lm_in[bh:bh + 64, tile * 512:tile * 512 + 256] = ql[b, h]
            lm_in[bh:bh + 64, tile * 512 + 256:tile * 512 + 512] = kl[b, h]
        maps.append({"lm_in": np.ascontiguousarray(lm_in),
                     "k_head": np.ascontiguousarray(k_b0[hsel * 64:(hsel + 1) * 64, :]),
                     "v_head": np.ascontiguousarray(v_b0[hsel * 64:(hsel + 1) * 64, :].T)})
    r = run_bass_kernel_spmd(nc_mid, maps, core_ids=list(range(NC)))
    z2 = [r.results[c]["z2_o"] for c in range(NC)]

    nc_post = build_b1post()
    z2_in = np.concatenate(z2, axis=1)
    cw = inp["L1_attn_conv_w"][:, 0, :, 0]
    cw_in = np.zeros((128, 4 * 33), np.float32)
    for g in range(4):
        cw_in[0:64, g * 33:(g + 1) * 33] = cw[2 * g][None, :]
        cw_in[64:128, g * 33:(g + 1) * 33] = cw[2 * g + 1][None, :]
    outw = inp["L1_attn_out_w"].reshape(4, 128, 8, 128).transpose(2, 0, 1, 3)
    outb = inp["L1_attn_out_b"].reshape(8, 128).T
    wq_in = inp["L1_attn_wq"].reshape(8, 128, 256)
    wk_in = inp["L1_attn_wk"].reshape(8, 128, 256)
    maps = []
    for c in range(NC):
        b, q = c // 4, c % 4
        t0 = q * TC
        lo, hi = t0 - 16, t0 + TC + 16
        vsl = v_b0[:, max(lo, 0):min(hi, T)]
        if lo < 0:
            vsl = np.concatenate([np.zeros((512, -lo), np.float32), vsl], 1)
        if hi > T:
            vsl = np.concatenate([vsl, np.zeros((512, hi - T), np.float32)], 1)
        kl_f = np.zeros((128, 4 * 256), np.float32)
        klb0 = kl[0].reshape(512, 256)
        for g in range(4):
            kl_f[:, g * 256:(g + 1) * 256] = klb0[g * 128:(g + 1) * 128, :]
        maps.append({"z2_in": np.ascontiguousarray(z2_in),
                     "q_in": np.ascontiguousarray(q_b0[:, t0:t0 + TC]),
                     "kl_in": np.ascontiguousarray(kl_f),
                     "v_in": np.ascontiguousarray(vsl),
                     "cw_in": np.ascontiguousarray(cw_in),
                     "outw": np.ascontiguousarray(outw), "outb": np.ascontiguousarray(outb),
                     "xn1T": xn1[c], "wq_in": np.ascontiguousarray(wq_in), "wk_in": np.ascontiguousarray(wk_in)})
    r = run_bass_kernel_spmd(nc_post, maps, core_ids=list(range(NC)))
    enc = [r.results[c]["enc_o"] for c in range(NC)]
    qe_all = np.concatenate([r.results[c]["qe_o"] for c in range(4)], axis=0)
    ke_all = np.concatenate([r.results[c]["ke_o"] for c in range(4)], axis=0)

    nc_edge = build_b1edge()
    rows = inp["L1_attn_adj_rows"].astype(np.uint32)
    cols_ = inp["L1_attn_adj_cols"].astype(np.uint32)
    vals_ = inp["L1_attn_adj_vals"].astype(np.float32)
    maps = []
    for c in range(NC):
        sl = slice(c * NEDGE, (c + 1) * NEDGE)
        maps.append({"qe_all": np.ascontiguousarray(qe_all), "ke_all": np.ascontiguousarray(ke_all),
                     "ridx": np.ascontiguousarray(rows[sl].reshape(NG, 128).T),
                     "cidx": np.ascontiguousarray(cols_[sl].reshape(NG, 128).T),
                     "vals": np.ascontiguousarray(vals_[sl].reshape(NG, 128).T)})
    r = run_bass_kernel_spmd(nc_edge, maps, core_ids=list(range(NC)))
    aparts = [r.results[c]["apart"].reshape(T) for c in range(NC)]
    a_all = np.stack(aparts)

    nc_mix = build_b1mix()
    maps = []
    for c in range(NC):
        q = c % 4
        t0 = q * TC
        maps.append({"a_all": np.ascontiguousarray(a_all),
                     "a_loc": np.ascontiguousarray(a_all[:, t0:t0 + TC]),
                     "valT": val[c], "encT": enc[c]})
    r = run_bass_kernel_spmd(nc_mix, maps, core_ids=list(range(NC)))
    s1 = [r.results[c]["s1_o"] for c in range(NC)]

    xff = [np.ascontiguousarray(xn1[c]) for c in range(NC)]
    y6 = _run_ffn(xff, prf)

    nc_fin = build_final()
    maps = [{"x1T": x1[c], "s1T": s1[c], "y6T": y6[c]} for c in range(NC)]
    r = run_bass_kernel_spmd(nc_fin, maps, core_ids=list(range(NC)))
    return [r.results[c]["out"] for c in range(NC)]


def kernel(**inp):
    if _DEV_STAGE == "b0":
        pr = _prep_common(inp)
        return [np.ascontiguousarray(v) for v in _run_b0(inp, pr)]
    if _FUSED:
        return kernel_fused(inp)
    return kernel_unfused(inp)



def build_fused():
    nc = bass.Bass("TRN2", target_bir_lowering=False)
    cx = Ctx(nc)
    em = cx.em
    import contextlib
    ap = lambda n, s, dt=F32, kind="ExternalInput": nc.dram_tensor(n, s, dt, kind=kind).ap()
    internal = lambda n, s, dt=F32: nc.dram_tensor(n, s, dt, kind="Internal").ap()
    RG = [list(range(NC))]

    xT = ap("xT", [D, TX])
    WrTt = ap("WrTt", [8, 8, 128, 128]); WkTt = ap("WkTt", [8, 8, 128, 128])
    WvTt = ap("WvTt", [8, 8, 128, 128]); WoTt = ap("WoTt", [8, 8, 128, 128])
    colsd = ap("colsd", [128, 64]); trem = ap("trem", [128, TC])
    cmask = ap("cmask", [128, 8])
    routerTt = ap("routerTt", [128, 64])
    sgTt = ap("sgTt", [16, 8, 128, 128]); suTt = ap("suTt", [16, 8, 128, 128])
    sdT = ap("sdT", [2 * D, D])
    egTt = ap("egTt", [8, 16, 8, 128, 128]); euTt = ap("euTt", [8, 16, 8, 128, 128])
    edT = ap("edT", [8, 2 * D, D])
    qkvw = ap("qkvw", [12, 8, 128, 128]); wvw = ap("wvw", [8, 8, 128, 128]); wvb = ap("wvb", [128, 8])
    sel16 = ap("sel16", [128, 16])
    kh_idx = ap("kh_idx", [64, 4], U32)
    vh_idx = ap("vh_idx", [128, 64], U32)
    qq_idx = ap("qq_idx", [128, 4], U32)
    vv_idx = ap("vv_idx", [128, 18], U32)
    al_idx = ap("al_idx", [32, 1], U32)
    hal_idx = ap("hal_idx", [128, 8], U32)
    rsel = ap("rsel", [128, 8])
    hmask = ap("hmask", [128, 2])
    ridx = ap("ridx", [128, NG], U32); cidx = ap("cidx", [128, NG], U32); vals = ap("vals", [128, NG])
    outw = ap("outw", [8, 4, 128, 128]); outb = ap("outb", [128, 8])
    cw_in = ap("cw_in", [128, 4 * 33])
    wq_in = ap("wq_in", [8, 128, 256]); wk_in = ap("wk_in", [8, 128, 256])
    fwqk = ap("fwqk", [6, 16, 8, 128, 128]); fwv = ap("fwv", [6, 8, 128, D]); fwo = ap("fwo", [6, 8, 8, 128, 128])
    fqscol = ap("fqscol", [6, 128, 2]); ftabq = ap("ftabq", [128, 2 * 2048]); ftabk = ap("ftabk", [128, 2 * TE])
    fpswap = ap("fpswap", [128, 128]); fmasks = ap("fmasks", [128, 8 * 128])
    fw1a = ap("fw1a", [6, 22, 8, 128, 128]); fw1g = ap("fw1g", [6, 22, 8, 128, 128]); fw2t = ap("fw2t", [6, 22, 128, D])
    flncols = ap("flncols", [6, 128, 16]); fbcols1 = ap("fbcols1", [6, 128, 44]); fbcols2 = ap("fbcols2", [6, 128, 8])
    out_f = ap("out", [D, TC], kind="ExternalOutput")
    if _FUSE_DEBUG:
        dbg_x1 = ap("dbg_x1", [D, TC], kind="ExternalOutput")
        dbg_xn1 = ap("dbg_xn1", [D, TC], kind="ExternalOutput")
        dbg_s1 = ap("dbg_s1", [D, TC], kind="ExternalOutput")
        dbg_y = ap("dbg_y", [D, TC], kind="ExternalOutput")
        dbg_enc = ap("dbg_enc", [D, TC], kind="ExternalOutput")
        dbg_val = ap("dbg_val", [D, TC], kind="ExternalOutput")
        dbg_z2 = ap("dbg_z2", [NC * 128, 128], kind="ExternalOutput")
        dbg_ar = ap("dbg_ar", [128, 64], kind="ExternalOutput")
        dbg_tot = ap("dbg_tot", [NC * 128, 16], kind="ExternalOutput")

    xn_sp = internal("xn_sp", [D, TX])
    s_sp = internal("s_sp", [D, TC])
    x1_sp = internal("x1_sp", [D, TC])
    xn1_sp = internal("xn1_sp", [D, TC])
    val_sp = internal("val_sp", [D, TC])
    enc_sp = internal("enc_sp", [D, TC])
    s1_sp = internal("s1_sp", [D, TC])
    y_a = internal("y_a", [D, TC]); y_b = internal("y_b", [D, TC])
    tot_in = internal("tot_in", [128, 16]); tot_out = internal("tot_out", [NC * 128, 16])
    lm_in_c = internal("lm_in_c", [512, 128]); lm_out_c = internal("lm_out_c", [NC * 512, 128])
    q_in_c = internal("q_in_c", [512, TC]); q_out_c = internal("q_out_c", [NC * 512, TC])
    k_in_c = internal("k_in_c", [512, TC]); k_out_c = internal("k_out_c", [NC * 512, TC])
    v_in_c = internal("v_in_c", [512, TC]); v_out_c = internal("v_out_c", [NC * 512, TC])
    vt_in_c = internal("vt_in_c", [TC, 512]); vt_out_c = internal("vt_out_c", [NC * TC, 512])
    z2_in_c = internal("z2_in_c", [128, 128]); z2_out_c = internal("z2_out_c", [NC * 128, 128])
    qe_in_c = internal("qe_in_c", [TC, 256]); qe_out_c = internal("qe_out_c", [NC * TC, 256])
    ke_in_c = internal("ke_in_c", [TC, 256]); ke_out_c = internal("ke_out_c", [NC * TC, 256])
    ar_in_c = internal("ar_in_c", [128, 64]); ar_out_c = internal("ar_out_c", [128, 64])
    tl_in_c = internal("tl_in_c", [D, 272]); tl_out_c = internal("tl_out_c", [NC * D, 272])
    al_sp = internal("al_sp", [1, TC])

    ones = cx.ones_col()
    identf = cx.ident(F32)
    ones_bf = cx.sb("g_ones_bf", [128, 1], BF16)
    em.op('pool', lambda e: e.memset(ones_bf[:], 1.0))
    c30 = cx.sb("g_c30", [128, 1])
    em.op('pool', lambda e: e.memset(c30[:], 1e-30))

    st_b0 = contextlib.ExitStack()
    _sb0 = lambda n, s, dt=F32: st_b0.enter_context(nc.sbuf_tensor(n, s, dt))
    xn = _sb0("xn", [128, 8 * TX], BF16)
    cols_all = _sb0("cols_all", [128, 64])
    trem_sb = _sb0("trem_sb", [128, TC])
    tot_sb = _sb0("tot_sb", [128, 16])
    ccol_e = _sb0("ccol_e", [128, 8]); ccol_v = _sb0("ccol_v", [128, 8])
    contrib = _sb0("contrib", [128, 8 * TC], BF16)
    emit_ln(cx, xT, TX, xn, spill=xn_sp)
    em.dma('sp', lambda e: e.dma_start(out=cols_all[:], in_=colsd[:, :]))
    em.dma('sp', lambda e: e.dma_start(out=trem_sb[:], in_=trem[:, :]))
    for i in range(8):
        em.op('act', lambda e, i=i: e.activation(out=cols_all[:, i * 8 + 5:i * 8 + 6], in_=cols_all[:, i * 8 + 3:i * 8 + 4], func=AF.Relu))
        em.op('dve', lambda e, i=i: e.tensor_scalar(out=cols_all[:, i * 8 + 5:i * 8 + 6], in0=cols_all[:, i * 8 + 5:i * 8 + 6], scalar1=-1.0, scalar2=None, op0=ALU.mult))
    wT = {"r": WrTt, "k": WkTt, "v": WvTt}
    for i in range(8):
        _emit_rwkv_tile(cx, i, xn, {"k": WkTt, "v": WvTt}, cols_all[:, i * 8:(i + 1) * 8], trem_sb, None, None, None, tot_sb, with_carry=False, tag="fa")
    em.dma('sp', lambda e: e.dma_start(out=tot_in[:, :], in_=tot_sb[:]))
    em.cc(lambda e: e.collective_compute("AllGather", ALU.bypass, replica_groups=RG, ins=[tot_in[:, :]], outs=[tot_out[:, :]]))

    with contextlib.ExitStack() as st:
        totg = st.enter_context(nc.sbuf_tensor("totg", [128, 16], F32))
        cm_sb = st.enter_context(nc.sbuf_tensor("cm_sb", [128, 8], F32))
        em.dma('sp', lambda e: e.dma_start(out=cm_sb[:], in_=cmask[:, :]))
        em.op('pool', lambda e: e.memset(ccol_e[:], 0.0))
        em.op('pool', lambda e: e.memset(ccol_v[:], 0.0))
        for r in range(NC):
            em.dma('sp', lambda e, r=r: e.dma_start(out=totg[:], in_=tot_out[r * 128:(r + 1) * 128, :]))
            em.op('dve', lambda e, r=r: e.scalar_tensor_tensor(out=ccol_e[:], in0=totg.ap().rearrange("p (k two) -> p k two", two=2)[:, :, 0], scalar=cm_sb[:, r:r + 1], in1=ccol_e[:], op0=ALU.mult, op1=ALU.add))
            em.op('dve', lambda e, r=r: e.scalar_tensor_tensor(out=ccol_v[:], in0=totg.ap().rearrange("p (k two) -> p k two", two=2)[:, :, 1], scalar=cm_sb[:, r:r + 1], in1=ccol_v[:], op0=ALU.mult, op1=ALU.add))
    for i in range(8):
        _emit_rwkv_tile(cx, i, xn, wT, cols_all[:, i * 8:(i + 1) * 8], trem_sb, ccol_e, ccol_v, contrib, None, with_carry=True, tag="fb")
    with contextlib.ExitStack() as st:
        wo_sb = st.enter_context(nc.sbuf_tensor("wo_sb", [128, 8 * 128], BF16))
        s_f = st.enter_context(nc.sbuf_tensor("s_f", [128, TC], F32))
        pp = st.enter_context(nc.psum_tensor("ps_s", [128, 512], F32))
        for ot in range(8):
            em.dma('pool', lambda e, ot=ot: e.dma_start(out=wo_sb.ap().rearrange("p (a f) -> p a f", a=8), in_=WoTt[ot].rearrange("a p f -> p a f")))
            for (off, w) in _chunks(TC, 512):
                for dt_ in range(8):
                    em.op('pe', lambda e, ot=ot, off=off, w=w, dt_=dt_: e.matmul(
                        pp[:, 0:w], wo_sb[:, dt_ * 128:(dt_ + 1) * 128],
                        contrib[:, dt_ * TC + off: dt_ * TC + off + w], start=(dt_ == 0), stop=(dt_ == 7)))
                em.op('dve', lambda e, ot=ot, off=off, w=w: e.tensor_scalar(out=s_f[:, off:off + w], in0=pp[:, 0:w], scalar1=cols_all[:, ot * 8 + 6:ot * 8 + 7], scalar2=None, op0=ALU.add))
            em.dma('sp', lambda e, ot=ot: e.dma_start(out=s_sp[ot * 128:(ot + 1) * 128, :], in_=s_f[:]))

    st_b02 = contextlib.ExitStack()
    comb = st_b02.enter_context(nc.sbuf_tensor("comb", [128, 128], F32))
    acc = st_b02.enter_context(nc.sbuf_tensor("acc", [128, 16 * 1024], F32))
    with contextlib.ExitStack() as st:
        rt_sb = st.enter_context(nc.sbuf_tensor("rt_sb", [128, 64], F32))
        em.dma('sp', lambda e: e.dma_start(out=rt_sb[:], in_=routerTt[:, :]))
        xf_t = st.enter_context(nc.sbuf_tensor("xf_t", [128, 8 * 128], F32))
        pl = st.enter_context(nc.psum_tensor("pl", [128, 512], F32))
        lg = st.enter_context(nc.sbuf_tensor("lg", [128, 8], F32))
        elg = st.enter_context(nc.sbuf_tensor("elg", [128, 8], F32))
        mcol = st.enter_context(nc.sbuf_tensor("mcol", [128, 5], F32))
        for tt in range(16):
            for i in range(8):
                em.dma('sp', lambda e, tt=tt, i=i: e.dma_start(out=xf_t[:, i * 128:(i + 1) * 128], in_=xn_sp[i * 128:(i + 1) * 128, 1 + tt * 128:1 + (tt + 1) * 128]))
            for i in range(8):
                em.op('pe', lambda e, i=i: e.matmul(pl[:, 0:8], xf_t[:, i * 128:(i + 1) * 128], rt_sb[:, i * 8:(i + 1) * 8], start=(i == 0), stop=(i == 7)))
            em.op('dve', lambda e: e.reduce_max(out=mcol[:, 0:1], in_=pl[:, 0:8], axis=AX.X, negate=True))
            em.op('act', lambda e: e.activation(out=elg[:], in_=pl[:, 0:8], func=AF.Exp, bias=mcol[:, 0:1]))
            em.op('dve', lambda e: e.reduce_max(out=mcol[:, 4:5], in_=elg[:], axis=AX.X))
            em.op('dve', lambda e: e.tensor_scalar(out=lg[:], in0=elg[:], scalar1=mcol[:, 4:5], scalar2=None, op0=ALU.is_ge))
            em.op('dve', lambda e: e.scalar_tensor_tensor(out=lg[:], in0=lg[:], scalar=-2.0, in1=elg[:], op0=ALU.mult, op1=ALU.add))
            em.op('dve', lambda e: e.reduce_max(out=mcol[:, 1:2], in_=lg[:], axis=AX.X))
            em.op('dve', lambda e: e.tensor_scalar(out=lg[:], in0=elg[:], scalar1=mcol[:, 1:2], scalar2=None, op0=ALU.is_ge))
            em.op('dve', lambda e: e.scalar_tensor_tensor(out=lg[:], in0=elg[:], scalar=1.0, in1=lg[:], op0=ALU.bypass, op1=ALU.mult, accum_out=mcol[:, 2:3]))
            em.op('dve', lambda e: e.reciprocal(out=mcol[:, 3:4], in_=mcol[:, 2:3]))
            em.op('dve', lambda e, tt=tt: e.tensor_scalar(out=comb[:, tt * 8:(tt + 1) * 8], in0=lg[:], scalar1=mcol[:, 3:4], scalar2=None, op0=ALU.mult))
    for h in range(4):
        em.op('pool', lambda e, h=h: e.memset(acc[:, h * 4096:(h + 1) * 4096], 0.0))
    with contextlib.ExitStack() as st:
        wg = st.enter_context(nc.sbuf_tensor("wg", [128, 8 * 256], BF16))
        ed_sb = st.enter_context(nc.sbuf_tensor("ed_sb", [128, 16 * 1024], BF16))
        hb = st.enter_context(nc.sbuf_tensor("hb", [128, 16 * 1024], BF16))
        pg = st.enter_context(nc.psum_tensor("pg", [128, 512], F32))
        pu = st.enter_context(nc.psum_tensor("pu", [128, 512], F32))
        py = st.enter_context(nc.psum_tensor("py", [128, 512], F32))
        sg_t = st.enter_context(nc.sbuf_tensor("sg_t", [128, 512], BF16))

        def expert(eidx):
            if eidx == 8:
                g_dram, u_dram, d_dram = sgTt, suTt, sdT
            else:
                g_dram, u_dram, d_dram = egTt[eidx], euTt[eidx], edT[eidx]
            for i in range(16):
                em.dma('pool', lambda e, i=i, d_dram=d_dram: e.dma_start(out=ed_sb[:, i * 1024:(i + 1) * 1024], in_=d_dram[i * 128:(i + 1) * 128, :]))
            for half in range(2):
                for ht in range(16):
                    em.dma('pool', lambda e, ht=ht, g_dram=g_dram: e.dma_start(
                        out=wg.ap().rearrange("p (a two f) -> p a two f", a=8, two=2)[:, :, 0, :],
                        in_=g_dram[ht].rearrange("a p f -> p a f")))
                    em.dma('pool', lambda e, ht=ht, u_dram=u_dram: e.dma_start(
                        out=wg.ap().rearrange("p (a two f) -> p a two f", a=8, two=2)[:, :, 1, :],
                        in_=u_dram[ht].rearrange("a p f -> p a f")))
                    for cch in range(2):
                        off = 1 + half * 1024 + cch * 512
                        for i in range(8):
                            em.op('pe', lambda e, i=i, off=off: e.matmul(pg[:], wg[:, i * 256:i * 256 + 128], xn[:, i * TX + off:i * TX + off + 512], start=(i == 0), stop=(i == 7)))
                        for i in range(8):
                            em.op('pe', lambda e, i=i, off=off: e.matmul(pu[:], wg[:, i * 256 + 128:i * 256 + 256], xn[:, i * TX + off:i * TX + off + 512], start=(i == 0), stop=(i == 7)))
                        em.op('act', lambda e: e.activation(out=sg_t[:], in_=pg[:], func=AF.Silu))
                        em.op('dve', lambda e, ht=ht, cch=cch: e.tensor_tensor(out=hb[:, ht * 1024 + cch * 512: ht * 1024 + (cch + 1) * 512], in0=pu[:], in1=sg_t[:], op=ALU.mult))
                for ts in range(8):
                    tt_g = half * 8 + ts
                    for och in range(2):
                        for h2 in range(16):
                            em.op('pe', lambda e, ts=ts, och=och, h2=h2: e.matmul(
                                py[:], hb[:, h2 * 1024 + ts * 128: h2 * 1024 + (ts + 1) * 128],
                                ed_sb[:, h2 * 1024 + och * 512: h2 * 1024 + (och + 1) * 512],
                                start=(h2 == 0), stop=(h2 == 15)))
                        if eidx == 8:
                            em.op('dve', lambda e, tt_g=tt_g, och=och: e.tensor_tensor(
                                out=acc[:, tt_g * 1024 + och * 512: tt_g * 1024 + (och + 1) * 512],
                                in0=acc[:, tt_g * 1024 + och * 512: tt_g * 1024 + (och + 1) * 512],
                                in1=py[:], op=ALU.add))
                        else:
                            em.op('dve', lambda e, tt_g=tt_g, och=och, eidx=eidx: e.scalar_tensor_tensor(
                                out=acc[:, tt_g * 1024 + och * 512: tt_g * 1024 + (och + 1) * 512],
                                in0=py[:], scalar=comb[:, tt_g * 8 + eidx: tt_g * 8 + eidx + 1],
                                in1=acc[:, tt_g * 1024 + och * 512: tt_g * 1024 + (och + 1) * 512],
                                op0=ALU.mult, op1=ALU.add))
        for eidx in range(9):
            expert(eidx)

    with contextlib.ExitStack() as st:
        ptr = st.enter_context(nc.psum_tensor("ptr", [128, 128], F32))
        gfm = st.enter_context(nc.sbuf_tensor("gfm", [128, TC], F32))
        sfm = st.enter_context(nc.sbuf_tensor("sfm", [128, TC], F32))
        xfm = st.enter_context(nc.sbuf_tensor("xfm", [128, TC], F32))
        for ot in range(8):
            for tt in range(16):
                em.op('pe', lambda e, ot=ot, tt=tt: e.transpose(ptr[:], acc[:, tt * 1024 + ot * 128:tt * 1024 + (ot + 1) * 128], identf[:]))
                em.op('act', lambda e, tt=tt: e.activation(out=gfm[:, tt * 128:(tt + 1) * 128], in_=ptr[:], func=AF.Silu))
            em.dma('sp', lambda e, ot=ot: e.dma_start(out=sfm[:], in_=s_sp[ot * 128:(ot + 1) * 128, :]))
            em.dma('sp', lambda e, ot=ot: e.dma_start(out=xfm[:], in_=xT[ot * 128:(ot + 1) * 128, 1:TC + 1]))
            em.op('dve', lambda e: e.tensor_tensor(out=gfm[:], in0=gfm[:], in1=sfm[:], op=ALU.mult))
            em.op('dve', lambda e: e.tensor_tensor(out=gfm[:], in0=gfm[:], in1=xfm[:], op=ALU.add))
            em.dma('sp', lambda e, ot=ot: e.dma_start(out=x1_sp[ot * 128:(ot + 1) * 128, :], in_=gfm[:]))
    st_b02.close()
    st_b0.close()
    if _FUSE_UPTO >= 2:
        _fused_part2(cx, dict(locals()))
    em = cx.em
    em.build()
    cx.st.close()
    return nc


def _fused_part2(cx, L):
    nc, em = cx.nc, cx.em
    import contextlib
    g = lambda n: L[n]
    RG = [list(range(NC))]
    ones = cx.ones_col(); identf = cx.ident(F32)
    ones_bf = g("ones_bf"); c30 = g("c30")
    x1_sp = g("x1_sp"); xn1_sp = g("xn1_sp"); val_sp = g("val_sp"); enc_sp = g("enc_sp"); s1_sp = g("s1_sp")
    qkvw = g("qkvw"); wvw = g("wvw"); wvb = g("wvb")

    st_p56 = contextlib.ExitStack()
    xn1b = st_p56.enter_context(nc.sbuf_tensor("xn1b", [128, 8 * TC], BF16))
    emit_ln(cx, x1_sp, TC, xn1b, spill=xn1_sp, name="ln1")

    lm_in_c = g("lm_in_c"); q_in_c = g("q_in_c"); k_in_c = g("k_in_c"); v_in_c = g("v_in_c"); vt_in_c = g("vt_in_c")
    with contextlib.ExitStack() as st:
        sb = lambda n, s, dt=F32: st.enter_context(nc.sbuf_tensor(n, s, dt))
        wt = sb("p6wt", [128, 8 * 128], BF16)
        pp = st.enter_context(nc.psum_tensor("p6pp", [128, 512], F32))
        ob = sb("p6ob", [128, TC])
        obb = sb("p6obb", [128, TC], BF16)
        red = sb("p6red", [128, 64])
        SC = 0.125
        for it in range(12):
            em.dma('pool', lambda e, it=it: e.dma_start(out=wt.ap().rearrange("p (a f) -> p a f", a=8), in_=qkvw[it].rearrange("a p f -> p a f")))
            for (off, w) in _chunks(TC, 512):
                for dt_ in range(8):
                    em.op('pe', lambda e, it=it, off=off, w=w, dt_=dt_: e.matmul(
                        pp[:, 0:w], wt[:, dt_ * 128:(dt_ + 1) * 128], xn1b[:, dt_ * TC + off:dt_ * TC + off + w],
                        start=(dt_ == 0), stop=(dt_ == 7)))
                em.op('act', lambda e, it=it, off=off, w=w: e.activation(out=ob[:, off:off + w], in_=pp[:, 0:w], func=AF.Copy, scale=(SC if it < 4 else 1.0)))
            dst = q_in_c if it < 4 else (k_in_c if it < 8 else v_in_c)
            em.dma('sp', lambda e, it=it, dst=dst: e.dma_start(out=dst[(it % 4) * 128:(it % 4) * 128 + 128, :], in_=ob[:]))
            if it < 8:
                em.op('dve', lambda e: e.tensor_reduce(out=red[:], in_=ob.ap().rearrange("p (g s) -> p g s", s=32), axis=AX.X, op=ALU.add))
                em.op('dve', lambda e: e.tensor_scalar(out=red[:], in0=red[:], scalar1=1.0 / 32, scalar2=None, op0=ALU.mult))
                em.dma('sp', lambda e, it=it: e.dma_start(out=lm_in_c[(it % 4) * 128:(it % 4) * 128 + 128, (0 if it < 4 else 64):(64 if it < 4 else 128)], in_=red[:]))
        wvmov = sb("p6wvm", [128, 8 * 512], BF16)
        for dt_ in range(8):
            for vi in range(4):
                em.dma('pool', lambda e, dt_=dt_, vi=vi: e.dma_start(
                    out=wvmov[:, dt_ * 512 + vi * 128: dt_ * 512 + (vi + 1) * 128],
                    in_=qkvw[8 + vi, dt_]))
        for tt in range(16):
            for dt_ in range(8):
                em.op('pe', lambda e, tt=tt, dt_=dt_: e.matmul(
                    pp[:], xn1b[:, dt_ * TC + tt * 128:dt_ * TC + (tt + 1) * 128], wvmov[:, dt_ * 512:(dt_ + 1) * 512],
                    start=(dt_ == 0), stop=(dt_ == 7)))
            em.op('dve', lambda e: e.tensor_copy(out=ob[:, 0:512], in_=pp[:]))
            em.dma('sp', lambda e, tt=tt: e.dma_start(out=vt_in_c[tt * 128:(tt + 1) * 128, :], in_=ob[:, 0:512]))
        wvb_sb = sb("p6wvb", [128, 8], F32)
        em.dma('sp', lambda e: e.dma_start(out=wvb_sb[:], in_=wvb[:, :]))
        for ot in range(8):
            em.dma('pool', lambda e, ot=ot: e.dma_start(out=wt.ap().rearrange("p (a f) -> p a f", a=8), in_=wvw[ot].rearrange("a p f -> p a f")))
            for (off, w) in _chunks(TC, 512):
                for dt_ in range(8):
                    em.op('pe', lambda e, ot=ot, off=off, w=w, dt_=dt_: e.matmul(
                        pp[:, 0:w], wt[:, dt_ * 128:(dt_ + 1) * 128], xn1b[:, dt_ * TC + off:dt_ * TC + off + w],
                        start=(dt_ == 0), stop=(dt_ == 7)))
                em.op('dve', lambda e, ot=ot, off=off, w=w: e.tensor_scalar(out=ob[:, off:off + w], in0=pp[:, 0:w], scalar1=wvb_sb[:, ot:ot + 1], scalar2=None, op0=ALU.add))
            em.dma('sp', lambda e, ot=ot: e.dma_start(out=val_sp[ot * 128:(ot + 1) * 128, :], in_=ob[:]))
    for (cin, cout) in ((lm_in_c, g("lm_out_c")), (q_in_c, g("q_out_c")), (k_in_c, g("k_out_c")), (v_in_c, g("v_out_c")), (vt_in_c, g("vt_out_c"))):
        em.cc(lambda e, cin=cin, cout=cout: e.collective_compute("AllGather", ALU.bypass, replica_groups=RG, ins=[cin[:, :]], outs=[cout[:, :]]))

    st_p56.close()
    st_p7 = contextlib.ExitStack()
    _sb7 = lambda n, s, dt=F32: st_p7.enter_context(nc.sbuf_tensor(n, s, dt))
    lm_out_c = g("lm_out_c"); k_out_c = g("k_out_c"); vt_out_c = g("vt_out_c")
    sel16 = g("sel16"); kh_idx = g("kh_idx"); vh_idx = g("vh_idx")
    z2_in_c = g("z2_in_c")
    sel_sb = _sb7("sel_sb", [128, 16], F32)
    em.dma('sp', lambda e: e.dma_start(out=sel_sb[:], in_=sel16[:, :]))
    lm_sb = _sb7("lm_sb", [128, 8 * 512], BF16)
    for p in range(16):
        b, h = p // 8, p % 8
        tile, bh = p // 2, (p % 2) * 64
        for q in range(4):
            r = b * 4 + q
            em.dma('pool', lambda e, tile=tile, bh=bh, q=q, r=r, h=h: e.dma_start(
                out=lm_sb.ap().rearrange("p (t two c) -> p t two c", t=8, two=2)[bh:bh + 64, tile, :, q * 64:(q + 1) * 64],
                in_=lm_out_c[r * 512 + h * 64: r * 512 + (h + 1) * 64, :].rearrange("p (two c) -> p two c", two=2)))
    k_sb = _sb7("k_sb", [128, T], BF16)
    kh_sb = _sb7("kh_sb", [64, 4], U32)
    em.dma('sp', lambda e: e.dma_start(out=kh_sb[:], in_=kh_idx[:, :]))
    for r in range(4):
        em.dma('pool', lambda e, r=r: e.indirect_dma_start(out=k_sb[0:64, r * TC:(r + 1) * TC], out_offset=None, in_=k_out_c[:, :], in_offset=IndirectOffsetOnAxis(ap=kh_sb[:, r:r + 1], axis=0)))
    v_sb = _sb7("v_sb", [128, 64 * 64], BF16)
    vh_sb = _sb7("vh_sb", [128, 64], U32)
    em.dma('sp', lambda e: e.dma_start(out=vh_sb[:], in_=vh_idx[:, :]))
    for tt in range(64):
        em.dma('pool', lambda e, tt=tt: e.indirect_dma_start(
            out=v_sb[:, tt * 64:(tt + 1) * 64], out_offset=None,
            in_=vt_out_c.rearrange("t (h c) -> (t h) c", h=8), in_offset=IndirectOffsetOnAxis(ap=vh_sb[:, tt:tt + 1], axis=0)))

    a2buf = _sb7("a2buf", [128, 16 * 512], F32)
    rsbuf = _sb7("rsbuf", [128, 34], F32)
    csbuf = _sb7("csbuf", [1, 16 * 256], F32)
    sc_ = _sb7("sc_", [1, 8], F32)
    rsc_col = _sb7("rsc_col", [128, 1], F32)
    with contextlib.ExitStack() as st:
        sb = lambda n, s, dt=F32: st.enter_context(nc.sbuf_tensor(n, s, dt))
        pl2 = st.enter_context(nc.psum_tensor("m_pl2", [128, 256], F32))
        pcs = st.enter_context(nc.psum_tensor("m_pcs", [1, 256], F32))
        mc = sb("m_mc", [128, 2])
        for p in range(16):
            tile, bh = p // 2, (p % 2) * 64
            for it in range(2):
                em.op('pe', lambda e, tile=tile, bh=bh, it=it: e.matmul(
                    pl2[:, :], lm_sb[bh:bh + 64, tile * 512 + it * 128: tile * 512 + (it + 1) * 128],
                    lm_sb[bh:bh + 64, tile * 512 + 256: tile * 512 + 512], start=True, stop=True))
                em.op('dve', lambda e: e.reduce_max(out=mc[:, 0:1], in_=pl2[:], axis=AX.X, negate=True))
                em.op('act', lambda e, p=p, it=it: e.activation(out=a2buf[:, p * 512 + it * 256: p * 512 + (it + 1) * 256], in_=pl2[:], func=AF.Exp, bias=mc[:, 0:1]))
                em.op('dve', lambda e, p=p, it=it: e.reduce_sum(out=mc[:, 1:2], in_=a2buf[:, p * 512 + it * 256: p * 512 + (it + 1) * 256], axis=AX.X))
                em.op('dve', lambda e: e.reciprocal(out=mc[:, 1:2], in_=mc[:, 1:2]))
                em.op('dve', lambda e, p=p, it=it: e.tensor_scalar(out=a2buf[:, p * 512 + it * 256: p * 512 + (it + 1) * 256], in0=a2buf[:, p * 512 + it * 256: p * 512 + (it + 1) * 256], scalar1=mc[:, 1:2], scalar2=None, op0=ALU.mult))
                em.op('dve', lambda e, p=p, it=it: e.reduce_sum(out=rsbuf[:, p * 2 + it:p * 2 + it + 1], in_=a2buf[:, p * 512 + it * 256: p * 512 + (it + 1) * 256], axis=AX.X))
            for it in range(2):
                em.op('pe', lambda e, p=p, it=it: e.matmul(pcs[0:1, :], ones[:, 0:1], a2buf[:, p * 512 + it * 256: p * 512 + (it + 1) * 256], start=(it == 0), stop=(it == 1)))
            em.op('dve', lambda e, p=p: e.tensor_copy(out=csbuf[0:1, p * 256:(p + 1) * 256], in_=pcs[0:1, :]))
        em.op('dve', lambda e: e.reduce_max(out=rsbuf[:, 32:33], in_=rsbuf[:, 0:32], axis=AX.X))
        em.op('pe', lambda e: e.transpose(pcs[0:1, 0:128], rsbuf[:, 32:33], identf[:]))
        em.op('dve', lambda e: e.reduce_max(out=sc_[0:1, 0:1], in_=pcs[0:1, 0:128], axis=AX.X))
        em.op('dve', lambda e: e.reduce_max(out=sc_[0:1, 1:2], in_=csbuf[0:1, :], axis=AX.X))
        em.op('dve', lambda e: e.tensor_tensor(out=sc_[0:1, 0:1], in0=sc_[0:1, 0:1], in1=sc_[0:1, 1:2], op=ALU.mult))
        em.op('dve', lambda e: e.reciprocal(out=sc_[0:1, 0:1], in_=sc_[0:1, 0:1]))
        em.op('pe', lambda e: e.matmul(pl2[:, 0:1], ones[0:1, :], sc_[0:1, 0:1], start=True, stop=True))
        em.op('dve', lambda e: e.tensor_copy(out=rsc_col[:], in_=pl2[:, 0:1]))
    xb_ = _sb7("f_xb", [128, 512], F32)
    ql0 = _sb7("f_ql0", [128, 256], BF16)
    ql0f = _sb7("f_ql0f", [128, 256], F32)
    em.op('pool', lambda e: e.memset(xb_[:], 0.0))
    em.op('pool', lambda e: e.memset(ql0f[:], 0.0))
    for p in range(16):
        tile, bh = p // 2, (p % 2) * 64
        em.op('dve', lambda e, p=p: e.scalar_tensor_tensor(out=xb_[:], in0=a2buf[:, p * 512:(p + 1) * 512], scalar=sel_sb[:, p:p + 1], in1=xb_[:], op0=ALU.mult, op1=ALU.add))
        em.op('dve', lambda e, p=p, tile=tile, bh=bh: e.scalar_tensor_tensor(out=ql0f[bh:bh + 64, :], in0=lm_sb[bh:bh + 64, tile * 512:tile * 512 + 256], scalar=sel_sb[bh:bh + 64, p:p + 1], in1=ql0f[bh:bh + 64, :], op0=ALU.mult, op1=ALU.add))
    fold64 = _sb7("fold64", [128, 64], F32)
    fcol = _sb7("fcol", [128, 2], F32)
    em.op('pool', lambda e: e.iota(fold64[:], [[1, 64]], base=0, channel_multiplier=0, allow_small_or_imprecise_dtypes=True))
    em.op('pool', lambda e: e.iota(fcol[:, 0:1], [[0, 1]], base=0, channel_multiplier=1, allow_small_or_imprecise_dtypes=True))
    em.op('dve', lambda e: e.tensor_scalar(out=fcol[:, 1:2], in0=fcol[:, 0:1], scalar1=63.5, scalar2=None, op0=ALU.is_gt))
    em.op('dve', lambda e: e.scalar_tensor_tensor(out=fcol[:, 0:1], in0=fcol[:, 1:2], scalar=-64.0, in1=fcol[:, 0:1], op0=ALU.mult, op1=ALU.add))
    em.op('dve', lambda e: e.tensor_scalar(out=fold64[:], in0=fold64[:], scalar1=fcol[:, 0:1], scalar2=None, op0=ALU.is_equal))
    with nc.psum_tensor("p_ql0", [64, 256], F32) as p_ql0:
        em.op('pe', lambda e: e.matmul(p_ql0[0:64, :], fold64[:], ql0f[:], start=True, stop=True))
        em.op('dve', lambda e: e.tensor_copy(out=ql0[0:64, :], in_=p_ql0[0:64, :]))

    xT_ = _sb7("f_xT", [128, 512], F32)
    zb_ = _sb7("f_zb", [128, 512], F32)
    zT_ = _sb7("f_zT", [128, 512], F32)
    xz_ = _sb7("f_xz", [128, 512], F32)
    xzT_ = _sb7("f_xzT", [128, 512], F32)
    t1_ = _sb7("f_t1", [128, 512], F32)
    t2_ = _sb7("f_t2", [128, 512], F32)
    with contextlib.ExitStack() as st:
        sb = lambda n, s, dt=F32: st.enter_context(nc.sbuf_tensor(n, s, dt))
        ptA = st.enter_context(nc.psum_tensor("f_ptA", [128, 256], F32))
        ptB = st.enter_context(nc.psum_tensor("f_ptB", [128, 128], F32))

        def T256(dst, src):
            for a_ in range(2):
                for b_ in range(2):
                    em.op('pe', lambda e, a_=a_, b_=b_: e.transpose(ptB[:, :], src[:, a_ * 256 + b_ * 128: a_ * 256 + (b_ + 1) * 128], identf[:]))
                    em.op('dve', lambda e, a_=a_, b_=b_: e.tensor_copy(out=dst[:, b_ * 256 + a_ * 128: b_ * 256 + (a_ + 1) * 128], in_=ptB[:, :]))

        def MM(dst, lhsT_buf, rhs_buf, scale=1.0, diag=None):
            for it in range(2):
                for kt in range(2):
                    em.op('pe', lambda e, it=it, kt=kt: e.matmul(
                        ptA[:, :], lhsT_buf[:, kt * 256 + it * 128: kt * 256 + (it + 1) * 128],
                        rhs_buf[:, kt * 256: (kt + 1) * 256], start=(kt == 0), stop=(kt == 1)))
                if diag is None:
                    em.op('act', lambda e, it=it, scale=scale: e.activation(out=dst[:, it * 256:(it + 1) * 256], in_=ptA[:, :], func=AF.Copy, scale=scale))
                else:
                    em.op('act', lambda e, it=it: e.activation(out=dst[:, it * 256:(it + 1) * 256], in_=ptA[:, :], func=AF.Copy, scale=-1.0))
                    em.op('dve', lambda e, it=it, diag=diag: e.scalar_tensor_tensor(
                        out=dst[:, it * 256 + it * 128: it * 256 + (it + 1) * 128], in0=identf[:], scalar=float(diag),
                        in1=dst[:, it * 256 + it * 128: it * 256 + (it + 1) * 128], op0=ALU.mult, op1=ALU.add))

        T256(xT_, xb_)
        em.op('dve', lambda e: e.tensor_scalar(out=zb_[:], in0=xT_[:], scalar1=rsc_col[:, 0:1], scalar2=None, op0=ALU.mult))
        for _i in range(6):
            MM(xz_, xT_, zb_)
            T256(xzT_, xz_)
            em.op('act', lambda e: e.activation(out=t1_[:], in_=xz_[:], func=AF.Copy, scale=-1.0))
            for it in range(2):
                em.op('dve', lambda e, it=it: e.scalar_tensor_tensor(
                    out=t1_[:, it * 256 + it * 128: it * 256 + (it + 1) * 128], in0=identf[:], scalar=7.0,
                    in1=t1_[:, it * 256 + it * 128: it * 256 + (it + 1) * 128], op0=ALU.mult, op1=ALU.add))
            MM(t2_, xzT_, t1_, diag=15.0)
            MM(t1_, xzT_, t2_, diag=13.0)
            T256(zT_, zb_)
            MM(zb_, zT_, t1_, scale=0.25)
        ptm = st.enter_context(nc.psum_tensor("f_ptm", [128, 512], F32))
        pz3 = st.enter_context(nc.psum_tensor("f_pz3", [64, 256], F32))
        pzz = st.enter_context(nc.psum_tensor("f_pzz", [1, 256], F32))
        mcols = sb("f_mcols", [128, 18])
        mrow = sb("f_mrow", [1, 256])
        expt = sb("f_expt", [128, 256], BF16)
        z3f = sb("f_z3f", [64, 256])
        z3n = sb("f_z3n", [128, 128])
        zbc3 = sb("f_zbc3", [64, 256])
        zr3 = sb("f_zr3", [1, 256])
        z2b = sb("f_z2b", [128, 128])
        for it in range(2):
            for jc in range(16):
                em.op('pe', lambda e, it=it, jc=jc: e.matmul(
                    ptm[:, :], ql0[0:64, it * 128:(it + 1) * 128], k_sb[0:64, jc * 512:(jc + 1) * 512], start=True, stop=True))
                em.op('dve', lambda e, jc=jc: e.reduce_max(out=mcols[:, jc:jc + 1], in_=ptm[:], axis=AX.X))
            em.op('dve', lambda e: e.reduce_max(out=mcols[:, 16:17], in_=mcols[:, 0:16], axis=AX.X, negate=True))
            em.op('pe', lambda e: e.transpose(ptA[0:1, 0:128], mcols[:, 16:17], identf[:]))
            em.op('dve', lambda e, it=it: e.tensor_copy(out=mrow[0:1, it * 128:(it + 1) * 128], in_=ptA[0:1, 0:128]))
        for jt in range(64):
            em.op('pe', lambda e, jt=jt: e.matmul(ptA[:, :], k_sb[0:64, jt * 128:(jt + 1) * 128], ql0[0:64, 0:256], start=True, stop=False))
            em.op('pe', lambda e: e.matmul(ptA[:, :], ones[0:1, :], mrow[0:1, :], start=False, stop=True))
            em.op('act', lambda e: e.activation(out=expt[:], in_=ptA[:, :], func=AF.Exp))
            em.op('pe', lambda e, jt=jt: e.matmul(pz3[0:64, :], v_sb[:, jt * 64:(jt + 1) * 64], expt[:], start=(jt == 0), stop=(jt == 63)))
            em.op('pe', lambda e, jt=jt: e.matmul(pzz[0:1, :], ones_bf[:, 0:1], expt[:], start=(jt == 0), stop=(jt == 63)))
        em.op('dve', lambda e: e.tensor_copy(out=zr3[0:1, :], in_=pzz[0:1, :]))
        em.op('dve', lambda e: e.reciprocal(out=zr3[0:1, :], in_=zr3[0:1, :]))
        em.op('pe', lambda e: e.matmul(ptB[0:64, 0:256] if False else pz3[0:64, :] if False else ptm[0:64, 0:256], ones[0:1, 0:64], zr3[0:1, :], start=True, stop=True))
        em.op('dve', lambda e: e.tensor_copy(out=zbc3[0:64, :], in_=ptm[0:64, 0:256]))
        em.op('dve', lambda e: e.tensor_copy(out=z3f[0:64, :], in_=pz3[0:64, :]))
        em.op('dve', lambda e: e.tensor_tensor(out=z3f[0:64, :], in0=z3f[0:64, :], in1=zbc3[0:64, :], op=ALU.mult))
        for l in range(2):
            em.op('pe', lambda e, l=l: e.transpose(ptm[:, 0:64], z3f[0:64, l * 128:(l + 1) * 128], identf[0:64, 0:64]))
            em.op('dve', lambda e, l=l: e.tensor_copy(out=z3n[:, l * 64:(l + 1) * 64], in_=ptm[:, 0:64]))
        T256(xT_, zb_)
        for it in range(2):
            for kt in range(2):
                em.op('pe', lambda e, it=it, kt=kt: e.matmul(
                    ptA[:, 0:64], xT_[:, kt * 256 + it * 128: kt * 256 + (it + 1) * 128], z3n[:, kt * 64:(kt + 1) * 64], start=(kt == 0), stop=(kt == 1)))
            em.op('dve', lambda e, it=it: e.tensor_copy(out=z2b[:, it * 64:(it + 1) * 64], in_=ptA[:, 0:64]))
        em.dma('sp', lambda e: e.dma_start(out=z2_in_c[:, :], in_=z2b[:]))
    em.cc(lambda e: e.collective_compute("AllGather", ALU.bypass, replica_groups=RG, ins=[z2_in_c[:, :]], outs=[g("z2_out_c")[:, :]]))
    st_p7.close()
    if _FUSE_UPTO >= 3:
        _fused_part3(cx, L)
    if _FUSE_UPTO >= 4:
        _fused_part4(cx, L)
    if _FUSE_UPTO >= 5:
        _fused_part5(cx, L)


def _fused_part3(cx, L):
    nc, em = cx.nc, cx.em
    import contextlib
    g = lambda n: L[n]
    RG = [list(range(NC))]
    ones = cx.ones_col(); identf = cx.ident(F32)
    ones_bf = g("ones_bf"); c30 = g("c30")

    q_out_c = g("q_out_c"); v_out_c = g("v_out_c"); lm_out_c = g("lm_out_c"); z2_out_c = g("z2_out_c")
    qe_in_c = g("qe_in_c"); ke_in_c = g("ke_in_c")
    xn1_sp = g("xn1_sp"); enc_sp = g("enc_sp")
    qq_idx = g("qq_idx"); vv_idx = g("vv_idx")
    outw = g("outw"); outb = g("outb"); cw_in = g("cw_in"); wq_in = g("wq_in"); wk_in = g("wk_in")
    with contextlib.ExitStack() as stp:
        sbp = lambda n, s, dt=F32: stp.enter_context(nc.sbuf_tensor(n, s, dt))
        z2_sb = sbp("o_z2", [128, 8 * 128], BF16)
        for h in range(8):
            em.dma('pool', lambda e, h=h: e.dma_start(out=z2_sb[:, h * 128:(h + 1) * 128], in_=z2_out_c[h * 128:(h + 1) * 128, :]))
        qqi = sbp("o_qqi", [128, 4], U32)
        em.dma('sp', lambda e: e.dma_start(out=qqi[:], in_=qq_idx[:, :]))
        qb = sbp("o_qb", [128, 4 * TC], BF16)
        qbf = sbp("o_qbf", [128, TC])
        for gq in range(4):
            em.dma('pool', lambda e, gq=gq: e.indirect_dma_start(out=qbf[:, :], out_offset=None, in_=q_out_c[:, :], in_offset=IndirectOffsetOnAxis(ap=qqi[:, gq:gq + 1], axis=0)))
            em.op('dve', lambda e, gq=gq: e.tensor_copy(out=qb[:, gq * TC:(gq + 1) * TC], in_=qbf[:, :]))
        kl_sb = sbp("o_kl", [128, 4 * 256], BF16)
        for gg in range(4):
            for q in range(4):
                em.dma('pool', lambda e, gg=gg, q=q: e.dma_start(
                    out=kl_sb[:, gg * 256 + q * 64: gg * 256 + (q + 1) * 64],
                    in_=lm_out_c[q * 512 + gg * 128: q * 512 + (gg + 1) * 128, 64:128]))
        vvi = sbp("o_vvi", [128, 18], U32)
        em.dma('sp', lambda e: e.dma_start(out=vvi[:], in_=vv_idx[:, :]))
        vb = sbp("o_vb", [128, 4 * (TC + 32)], F32)
        hm = sbp("o_hm", [128, 2], F32)
        em.dma('sp', lambda e: e.dma_start(out=hm[:], in_=g("hmask")[:, :]))
        for gq in range(4):
            em.dma('pool', lambda e, gq=gq: e.indirect_dma_start(out=qbf[:, :], out_offset=None, in_=v_out_c[:, :], in_offset=IndirectOffsetOnAxis(ap=qqi[:, gq:gq + 1], axis=0)))
            em.op('dve', lambda e, gq=gq: e.tensor_copy(out=vb[:, gq * (TC + 32) + 16: gq * (TC + 32) + 16 + TC], in_=qbf[:, :]))
        vhl = sbp("o_vhl", [128, 16])
        for gq in range(4):
            em.dma('pool', lambda e, gq=gq: e.indirect_dma_start(out=vhl[:, :], out_offset=None, in_=v_out_c[:, :], in_offset=IndirectOffsetOnAxis(ap=vvi[:, gq:gq + 1], axis=0), element_offset=TC - 16))
            em.op('dve', lambda e, gq=gq: e.tensor_scalar(out=vb[:, gq * (TC + 32): gq * (TC + 32) + 16], in0=vhl[:, :], scalar1=hm[:, 0:1], scalar2=None, op0=ALU.mult))
        for gq in range(4):
            em.dma('pool', lambda e, gq=gq: e.indirect_dma_start(out=vhl[:, :], out_offset=None, in_=v_out_c[:, :], in_offset=IndirectOffsetOnAxis(ap=vvi[:, 4 + gq:5 + gq], axis=0), element_offset=0))
            em.op('dve', lambda e, gq=gq: e.tensor_scalar(out=vb[:, gq * (TC + 32) + 16 + TC: (gq + 1) * (TC + 32)], in0=vhl[:, :], scalar1=hm[:, 1:2], scalar2=None, op0=ALU.mult))
        cw_sb = sbp("o_cw", [128, 4 * 33], F32)
        em.dma('sp', lambda e: e.dma_start(out=cw_sb[:], in_=cw_in[:, :]))
        ob_sb = sbp("o_ob", [128, 8], F32)
        em.dma('sp', lambda e: e.dma_start(out=ob_sb[:], in_=outb[:, :]))
        nyT = sbp("o_nyT", [128, 4 * TC], BF16)
        cacc = sbp("o_cacc", [128, TC])
        st_at = contextlib.ExitStack()
        ptm = st_at.enter_context(nc.psum_tensor("o_ptm", [128, 512], F32))
        pst = st_at.enter_context(nc.psum_tensor("o_pst", [128, 512], F32))
        pat = st_at.enter_context(nc.psum_tensor("o_pat", [128, TC], F32))
        pz = st_at.enter_context(nc.psum_tensor("o_pz", [1, 512], F32))
        pzb = st_at.enter_context(nc.psum_tensor("o_pzb", [128, 512], F32))
        mc = sbp("o_mc", [128, 16])
        mrow = sbp("o_mrow", [1, TC])
        a1t = sbp("o_a1t", [128, 2 * 512], BF16)
        zrow = sbp("o_zrow", [1, TC])
        zbcs = sbp("o_zbcs", [128, 512])
        tno = sbp("o_tno", [128, 512])
        for h in range(8):
            gq, bh = h // 2, (h % 2) * 64
            for it in range(16):
                em.op('pe', lambda e, gq=gq, bh=bh, it=it: e.matmul(
                    ptm[:, 0:256], qb[bh:bh + 64, gq * TC + it * 128: gq * TC + (it + 1) * 128],
                    kl_sb[bh:bh + 64, gq * 256:(gq + 1) * 256], start=True, stop=True))
                em.op('dve', lambda e, it=it: e.reduce_max(out=mc[:, 0:1], in_=ptm[:, 0:256], axis=AX.X, negate=True))
                em.op('pe', lambda e: e.transpose(pz[0:1, 0:128], mc[:, 0:1], identf[:]))
                em.op('dve', lambda e, it=it: e.tensor_copy(out=mrow[0:1, it * 128:(it + 1) * 128], in_=pz[0:1, 0:128]))
            for (off, w) in _chunks(TC, 512):
                for lt in range(2):
                    em.op('pe', lambda e, gq=gq, bh=bh, lt=lt, off=off, w=w: e.matmul(
                        pst[:, 0:w], kl_sb[bh:bh + 64, gq * 256 + lt * 128: gq * 256 + (lt + 1) * 128],
                        qb[bh:bh + 64, gq * TC + off: gq * TC + off + w], start=True, stop=False))
                    em.op('pe', lambda e, lt=lt, off=off, w=w: e.matmul(
                        pst[:, 0:w], ones[0:1, :], mrow[0:1, off:off + w], start=False, stop=True))
                    em.op('act', lambda e, lt=lt, w=w: e.activation(out=a1t[:, lt * 512:lt * 512 + w], in_=pst[:, 0:w], func=AF.Exp))
                for lt in range(2):
                    em.op('pe', lambda e, lt=lt, h=h, bh=bh, off=off, w=w: e.matmul(
                        pat[bh:bh + 64, off:off + w], z2_sb[:, (h * 2 + lt) * 64:(h * 2 + lt) * 64 + 64],
                        a1t[:, lt * 512:lt * 512 + w], start=(lt == 0), stop=(lt == 1)))
                    em.op('pe', lambda e, lt=lt, off=off, w=w: e.matmul(
                        pz[0:1, 0:w], ones_bf[:, 0:1], a1t[:, lt * 512:lt * 512 + w], start=(lt == 0), stop=(lt == 1)))
                em.op('dve', lambda e, off=off, w=w: e.tensor_copy(out=zrow[0:1, off:off + w], in_=pz[0:1, 0:w]))
            em.op('dve', lambda e: e.reciprocal(out=zrow[0:1, :], in_=zrow[0:1, :]))
            for (off, w) in _chunks(TC, 512):
                em.op('pe', lambda e, off=off, w=w, bh=bh: e.matmul(pzb[bh:bh + 64, 0:w], ones[0:1, 0:64], zrow[0:1, off:off + w], start=True, stop=True))
                em.op('dve', lambda e, off=off, w=w, bh=bh: e.tensor_copy(out=zbcs[bh:bh + 64, 0:w], in_=pzb[bh:bh + 64, 0:w]))
                em.op('dve', lambda e, off=off, w=w, bh=bh, gq=gq: e.scalar_tensor_tensor(
                    out=cacc[bh:bh + 64, off:off + w], in0=vb[bh:bh + 64, gq * (TC + 32) + off: gq * (TC + 32) + off + w],
                    scalar=cw_sb[bh:bh + 64, gq * 33:gq * 33 + 1], in1=vb[bh:bh + 64, gq * (TC + 32) + off: gq * (TC + 32) + off + w], op0=ALU.mult, op1=ALU.bypass))
                for k in range(1, 33):
                    em.op('dve', lambda e, off=off, w=w, bh=bh, gq=gq, k=k: e.scalar_tensor_tensor(
                        out=cacc[bh:bh + 64, off:off + w], in0=vb[bh:bh + 64, gq * (TC + 32) + off + k: gq * (TC + 32) + off + k + w],
                        scalar=cw_sb[bh:bh + 64, gq * 33 + k:gq * 33 + k + 1], in1=cacc[bh:bh + 64, off:off + w], op0=ALU.mult, op1=ALU.add))
                em.op('dve', lambda e, off=off, w=w, bh=bh: e.tensor_tensor(out=tno[bh:bh + 64, 0:w], in0=pat[bh:bh + 64, off:off + w], in1=zbcs[bh:bh + 64, 0:w], op=ALU.mult))
                em.op('dve', lambda e, off=off, w=w, bh=bh, gq=gq: e.tensor_tensor(out=nyT[bh:bh + 64, gq * TC + off: gq * TC + off + w], in0=tno[bh:bh + 64, 0:w], in1=cacc[bh:bh + 64, off:off + w], op=ALU.add))
        st_at.close()
        encf_sp = nc.dram_tensor("fencf_sp", [D, TC], F32, kind="Internal").ap()
        wt5 = sbp("o_wt5", [128, 4 * 128], BF16)
        xnf5 = sbp("o_xnf5", [128, 512])
        ef5 = sbp("o_ef5", [128, 512])
        pxg = stp.enter_context(nc.psum_tensor("o_pxg", [128, 512], F32))
        for ot in range(8):
            em.dma('pool', lambda e, ot=ot: e.dma_start(out=wt5.ap().rearrange("p (a f) -> p a f", a=4), in_=outw[ot].rearrange("a p f -> p a f")))
            for (off, w) in _chunks(TC, 512):
                for it in range(4):
                    em.op('pe', lambda e, ot=ot, off=off, w=w, it=it: e.matmul(
                        pxg[:, 0:w], wt5[:, it * 128:(it + 1) * 128], nyT[:, it * TC + off:it * TC + off + w],
                        start=(it == 0), stop=(it == 3)))
                em.dma('sp', lambda e, ot=ot, off=off, w=w: e.dma_start(out=xnf5[:, 0:w], in_=xn1_sp[ot * 128:(ot + 1) * 128, off:off + w]))
                em.op('dve', lambda e, ot=ot, off=off, w=w: e.tensor_scalar(out=ef5[:, 0:w], in0=pxg[:, 0:w], scalar1=ob_sb[:, ot:ot + 1], scalar2=None, op0=ALU.add))
                em.op('dve', lambda e, off=off, w=w: e.tensor_tensor(out=ef5[:, 0:w], in0=ef5[:, 0:w], in1=xnf5[:, 0:w], op=ALU.add))
                em.dma('sp', lambda e, ot=ot, off=off, w=w: e.dma_start(out=enc_sp[ot * 128:(ot + 1) * 128, off:off + w], in_=ef5[:, 0:w]))
                em.dma('sp', lambda e, ot=ot, off=off, w=w: e.dma_start(out=encf_sp[ot * 128:(ot + 1) * 128, off:off + w], in_=ef5[:, 0:w]))
        wq_sb = sbp("o_wq", [128, 8 * 256])
        wk_sb = sbp("o_wk", [128, 8 * 256])
        enct = sbp("o_enct", [128, 8 * 128])
        qe5 = sbp("o_qe5", [128, 256])
        pqe = stp.enter_context(nc.psum_tensor("o_pqe", [128, 256], F32))
        pke = stp.enter_context(nc.psum_tensor("o_pke", [128, 256], F32))
        for i in range(8):
            em.dma('sp', lambda e, i=i: e.dma_start(out=wq_sb[:, i * 256:(i + 1) * 256], in_=wq_in[i]))
            em.dma('sp', lambda e, i=i: e.dma_start(out=wk_sb[:, i * 256:(i + 1) * 256], in_=wk_in[i]))
        for tt in range(16):
            for i in range(8):
                em.dma('sp', lambda e, tt=tt, i=i: e.dma_start(out=enct[:, i * 128:(i + 1) * 128], in_=encf_sp[i * 128:(i + 1) * 128, tt * 128:(tt + 1) * 128]))
            for i in range(8):
                em.op('pe', lambda e, tt=tt, i=i: e.matmul(pqe[:], enct[:, i * 128:(i + 1) * 128], wq_sb[:, i * 256:(i + 1) * 256], start=(i == 0), stop=(i == 7)))
            for i in range(8):
                em.op('pe', lambda e, tt=tt, i=i: e.matmul(pke[:], enct[:, i * 128:(i + 1) * 128], wk_sb[:, i * 256:(i + 1) * 256], start=(i == 0), stop=(i == 7)))
            em.op('dve', lambda e: e.tensor_copy(out=qe5[:], in_=pqe[:]))
            em.dma('sp', lambda e, tt=tt: e.dma_start(out=qe_in_c[tt * 128:(tt + 1) * 128, :], in_=qe5[:]))
            em.op('dve', lambda e: e.tensor_copy(out=qe5[:], in_=pke[:]))
            em.dma('sp', lambda e, tt=tt: e.dma_start(out=ke_in_c[tt * 128:(tt + 1) * 128, :], in_=qe5[:]))
    for (cin, cout) in ((qe_in_c, g("qe_out_c")), (ke_in_c, g("ke_out_c"))):
        em.cc(lambda e, cin=cin, cout=cout: e.collective_compute("AllGather", ALU.bypass, replica_groups=RG, ins=[cin[:, :]], outs=[cout[:, :]]))


def _fused_part4(cx, L):
    nc, em = cx.nc, cx.em
    import contextlib
    g = lambda n: L[n]
    RG = [list(range(NC))]
    ones = cx.ones_col(); identf = cx.ident(F32)
    ones_bf = g("ones_bf"); c30 = g("c30")

    qe_out_c = g("qe_out_c"); ke_out_c = g("ke_out_c")
    ridx = g("ridx"); cidx = g("cidx"); vals = g("vals")
    ar_in_c = g("ar_in_c"); ar_out_c = g("ar_out_c")
    with contextlib.ExitStack() as st:
        sb = lambda n, s, dt=F32: st.enter_context(nc.sbuf_tensor(n, s, dt))
        i64hi = sb("e_i64hi", [128, 128]); i64lo = sb("e_i64lo", [128, 64])
        em.op('pool', lambda e: e.iota(i64hi[:], [[64, 128]], base=0, channel_multiplier=0, allow_small_or_imprecise_dtypes=True))
        em.op('pool', lambda e: e.iota(i64lo[:], [[1, 64]], base=0, channel_multiplier=0, allow_small_or_imprecise_dtypes=True))
        rix = sb("e_rix", [128, NG], U32); cix = sb("e_cix", [128, NG], U32)
        vls = sb("e_vls", [128, NG]); rfl = sb("e_rfl", [128, NG])
        em.dma('sp', lambda e: e.dma_start(out=rix[:], in_=ridx[:, :]))
        em.dma('sp', lambda e: e.dma_start(out=cix[:], in_=cidx[:, :]))
        em.dma('sp', lambda e: e.dma_start(out=vls[:], in_=vals[:, :]))
        em.op('dve', lambda e: e.tensor_copy(out=rfl[:], in_=rix[:]))
        qg = sb("e_qg", [128, 256]); kg = sb("e_kg", [128, 256])
        dcol = sb("e_dcol", [128, 4])
        dtile = sb("e_dtile", [128, 128]); m1 = sb("e_m1", [128, 128])
        ohhi = sb("e_ohhi", [128, 128]); ohlo = sb("e_ohlo", [128, 64])
        trash = sb("e_trash", [128, 256])
        pA = st.enter_context(nc.psum_tensor("e_pA", [128, 64], F32))
        for g_ in range(NG):
            em.dma('pool', lambda e, g_=g_: e.indirect_dma_start(out=qg[:], out_offset=None, in_=qe_out_c[0:T, :], in_offset=IndirectOffsetOnAxis(ap=rix[:, g_:g_ + 1], axis=0)))
            em.dma('pool', lambda e, g_=g_: e.indirect_dma_start(out=kg[:], out_offset=None, in_=ke_out_c[0:T, :], in_offset=IndirectOffsetOnAxis(ap=cix[:, g_:g_ + 1], axis=0)))
            em.op('dve', lambda e: e.scalar_tensor_tensor(out=trash[:], in0=qg[:], scalar=1.0, in1=kg[:], op0=ALU.mult, op1=ALU.mult, accum_out=dcol[:, 0:1]))
            em.op('dve', lambda e, g_=g_: e.tensor_scalar(out=dcol[:, 1:2], in0=dcol[:, 0:1], scalar1=vls[:, g_:g_ + 1], scalar2=0.0625, op0=ALU.mult, op1=ALU.mult))
            em.op('dve', lambda e, g_=g_: e.tensor_scalar(out=dtile[:], in0=i64hi[:], scalar1=rfl[:, g_:g_ + 1], scalar2=None, op0=ALU.subtract))
            em.op('dve', lambda e: e.tensor_scalar(out=m1[:], in0=dtile[:], scalar1=-63.5, scalar2=None, op0=ALU.is_ge))
            em.op('dve', lambda e: e.scalar_tensor_tensor(out=ohhi[:], in0=dtile[:], scalar=0.5, in1=m1[:], op0=ALU.is_lt, op1=ALU.mult))
            em.op('dve', lambda e: e.scalar_tensor_tensor(out=m1[:], in0=ohhi[:], scalar=1.0, in1=i64hi[:], op0=ALU.mult, op1=ALU.mult, accum_out=dcol[:, 2:3]))
            em.op('dve', lambda e, g_=g_: e.tensor_scalar(out=dcol[:, 3:4], in0=rfl[:, g_:g_ + 1], scalar1=dcol[:, 2:3], scalar2=None, op0=ALU.subtract))
            em.op('dve', lambda e: e.tensor_scalar(out=ohlo[:], in0=i64lo[:], scalar1=dcol[:, 3:4], scalar2=None, op0=ALU.is_equal))
            em.op('dve', lambda e: e.tensor_scalar(out=ohlo[:], in0=ohlo[:], scalar1=dcol[:, 1:2], scalar2=None, op0=ALU.mult))
            em.op('pe', lambda e, g_=g_: e.matmul(pA[:], ohhi[:], ohlo[:], start=(g_ == 0), stop=(g_ == NG - 1)))
        acp = sb("e_acp", [128, 64])
        em.op('dve', lambda e: e.tensor_copy(out=acp[:], in_=pA[:]))
        em.dma('sp', lambda e: e.dma_start(out=ar_in_c[:, :], in_=acp[:]))
    em.cc(lambda e: e.collective_compute("AllReduce", ALU.add, replica_groups=RG, ins=[ar_in_c[:, :]], outs=[ar_out_c[:, :]]))

    al_idx = g("al_idx"); al_sp = g("al_sp")
    val_sp = g("val_sp"); enc_sp = g("enc_sp"); s1_sp = g("s1_sp")
    with contextlib.ExitStack() as st:
        sb = lambda n, s, dt=F32: st.enter_context(nc.sbuf_tensor(n, s, dt))
        a_sb = sb("x_a_sb", [128, 64])
        em.dma('sp', lambda e: e.dma_start(out=a_sb[:], in_=ar_out_c[:, :]))
        ali = sb("x_ali", [32, 1], U32)
        em.dma('sp', lambda e: e.dma_start(out=ali[:], in_=al_idx[:, :]))
        alv = sb("x_alv", [32, 64])
        em.dma('pool', lambda e: e.indirect_dma_start(out=alv[:], out_offset=None, in_=ar_out_c[:, :], in_offset=IndirectOffsetOnAxis(ap=ali[:, 0:1], axis=0)))
        em.dma('sp', lambda e: e.dma_start(out=al_sp[0:1, :].rearrange("one (a b) -> (one a) b", a=32), in_=alv[:]))
        stat = sb("x_stat", [1, 4])
        col = sb("x_col", [128, 2])
        pr_ = st.enter_context(nc.psum_tensor("x_pr", [1, 512], F32))
        pb_ = st.enter_context(nc.psum_tensor("x_pb", [128, 512], F32))
        em.op('dve', lambda e: e.reduce_max(out=col[:, 0:1], in_=a_sb[:], axis=AX.X))
        em.op('pe', lambda e: e.transpose(pr_[0:1, 0:128], col[:, 0:1], identf[:]))
        em.op('dve', lambda e: e.reduce_max(out=stat[0:1, 0:1], in_=pr_[0:1, 0:128], axis=AX.X, negate=True))
        expf = sb("x_expf", [128, 64])
        em.op('pe', lambda e: e.matmul(pb_[:, 0:1], ones[0:1, :], stat[0:1, 0:1], start=True, stop=True))
        em.op('dve', lambda e: e.tensor_copy(out=col[:, 1:2], in_=pb_[:, 0:1]))
        em.op('act', lambda e: e.activation(out=expf[:], in_=a_sb[:], func=AF.Exp, bias=col[:, 1:2]))
        em.op('pe', lambda e: e.matmul(pr_[0:1, 0:64], ones[:, 0:1], expf[:], start=True, stop=True))
        em.op('dve', lambda e: e.reduce_sum(out=stat[0:1, 1:2], in_=pr_[0:1, 0:64], axis=AX.X))
        em.op('dve', lambda e: e.reciprocal(out=stat[0:1, 1:2], in_=stat[0:1, 1:2]))
        lrow = sb("x_lrow", [1, TC])
        em.dma('sp', lambda e: e.dma_start(out=lrow[0:1, :], in_=al_sp[0:1, :]))
        em.op('act', lambda e: e.activation(out=lrow[0:1, :], in_=lrow[0:1, :], func=AF.Exp, bias=stat[0:1, 0:1]))
        em.op('dve', lambda e: e.tensor_scalar(out=lrow[0:1, :], in0=lrow[0:1, :], scalar1=stat[0:1, 1:2], scalar2=None, op0=ALU.mult))
        vf = sb("x_vf", [128, 512]); ef = sb("x_ef", [128, 512])
        xl = sb("x_xl", [128, 512]); sw = sb("x_sw", [128, 512]); t1 = sb("x_t1", [128, 512])
        for i in range(8):
            for (off, w) in _chunks(TC, 512):
                em.dma('sp', lambda e, i=i, off=off, w=w: e.dma_start(out=vf[:, 0:w], in_=val_sp[i * 128:(i + 1) * 128, off:off + w]))
                em.dma('sp', lambda e, i=i, off=off, w=w: e.dma_start(out=ef[:, 0:w], in_=enc_sp[i * 128:(i + 1) * 128, off:off + w]))
                em.op('pe', lambda e, off=off, w=w: e.matmul(pb_[:, 0:w], ones[0:1, :], lrow[0:1, off:off + w], start=True, stop=True))
                em.op('dve', lambda e, w=w: e.tensor_tensor(out=xl[:, 0:w], in0=vf[:, 0:w], in1=pb_[:, 0:w], op=ALU.mult))
                em.op('act', lambda e, w=w: e.activation(out=sw[:, 0:w], in_=xl[:, 0:w], func=AF.Sigmoid, scale=-1.0))
                em.op('act', lambda e, w=w: e.activation(out=sw[:, 0:w], in_=sw[:, 0:w], func=AF.Square))
                em.op('dve', lambda e, w=w: e.tensor_tensor(out=t1[:, 0:w], in0=xl[:, 0:w], in1=ef[:, 0:w], op=ALU.subtract))
                em.op('dve', lambda e, w=w: e.tensor_tensor(out=t1[:, 0:w], in0=t1[:, 0:w], in1=sw[:, 0:w], op=ALU.mult))
                em.op('dve', lambda e, w=w: e.tensor_tensor(out=t1[:, 0:w], in0=t1[:, 0:w], in1=ef[:, 0:w], op=ALU.add))
                em.op('dve', lambda e, w=w: e.tensor_scalar(out=t1[:, 0:w], in0=t1[:, 0:w], scalar1=2.0, scalar2=None, op0=ALU.mult))
                em.dma('sp', lambda e, i=i, off=off, w=w: e.dma_start(out=s1_sp[i * 128:(i + 1) * 128, off:off + w], in_=t1[:, 0:w]))


def _emit_ffn_layer_fused(cx, L, li, x_src, x_dst):
    _t = "L%d" % li
    """One L1_ffn layer (attn + ffn sublayers) reading x_src, writing x_dst.
    Uses tail-AllGather for the halo. L holds the fused build's handles."""
    nc, em = cx.nc, cx.em
    import contextlib
    g = lambda n: L[n]
    RG = [list(range(NC))]
    ones = cx.ones_col(); identf = cx.ident(F32)
    ones_bf = g("ones_bf"); c30 = g("c30")
    tl_in_c = g("tl_in_c"); tl_out_c = g("tl_out_c")
    hal_idx = g("hal_idx"); hmask = g("hmask"); rsel = g("rsel")
    wqk = g("fwqk")[li]; wvL = g("fwv")[li]; woL = g("fwo")[li]
    qscolL = g("fqscol")[li]
    mid_sp = nc.dram_tensor("fmid_sp%d" % li, [D, TC], F32, kind="Internal").ap()

    if _FUSE_NOHALO != 1:
        with contextlib.ExitStack() as st:
            tb = st.enter_context(nc.sbuf_tensor("ft_tb" + _t, [128, 272], F32))
            for i in range(8):
                em.dma('sp', lambda e, i=i: e.dma_start(out=tb[:, 0:257], in_=x_src[i * 128:(i + 1) * 128, TC - 257:TC]))
                em.dma('sp', lambda e, i=i: e.dma_start(out=tl_in_c[i * 128:(i + 1) * 128, :], in_=tb[:]))
        em.cc(lambda e: e.collective_compute("AllGather", ALU.bypass, replica_groups=RG, ins=[tl_in_c[:, :]], outs=[tl_out_c[:, :]]))

    with contextlib.ExitStack() as cst:
        sbc = lambda n, s, dt=F32: cst.enter_context(nc.sbuf_tensor(n, s, dt))
        xh = sbc("fa_xh" + _t, [128, 8 * 272])
        rs_sb = sbc("fa_rs" + _t, [128, 8])
        em.dma('sp', lambda e: e.dma_start(out=rs_sb[:], in_=rsel[:, :]))
        tgl = sbc("fa_tgl" + _t, [128, 272])
        for i in range(8):
            em.op('pool', lambda e, i=i: e.memset(xh[:, i * 272:(i + 1) * 272], 0.0))
            for r in range(NC):
                em.dma('sp', lambda e, i=i, r=r: e.dma_start(out=tgl[:], in_=tl_out_c[r * D + i * 128: r * D + (i + 1) * 128, :]))
                em.op('dve', lambda e, i=i, r=r: e.scalar_tensor_tensor(out=xh[:, i * 272:(i + 1) * 272], in0=tgl[:], scalar=rs_sb[:, r:r + 1], in1=xh[:, i * 272:(i + 1) * 272], op0=ALU.mult, op1=ALU.add))

        if _FUSE_NOHALO == 4:
            mini = sbc("fa_mini" + _t, [128, 272])
            for i in range(8):
                em.op('dve', lambda e, i=i: e.tensor_scalar(out=mini[:, 0:257], in0=xh[:, i * 272:i * 272 + 257], scalar1=1.0, scalar2=None, op0=ALU.mult))
            xcp = sbc("fa_xcp" + _t, [128, TC])
            for i in range(8):
                em.dma('sp', lambda e, i=i: e.dma_start(out=xcp[:], in_=x_src[i * 128:(i + 1) * 128, :]))
                em.dma('sp', lambda e, i=i: e.dma_start(out=mid_sp[i * 128:(i + 1) * 128, :], in_=xcp[:]))
                em.dma('sp', lambda e, i=i: e.dma_start(out=x_dst[i * 128:(i + 1) * 128, :], in_=xcp[:]))
            return
        mask_sb = sbc("fa_mask" + _t, [128, 8 * 128], F32)
        em.dma('sp', lambda e: e.dma_start(out=mask_sb[:], in_=g("fmasks")[:, :]))
        qs_sb = sbc("fa_qs" + _t, [128, 2], F32)
        em.dma('sp', lambda e: e.dma_start(out=qs_sb[:], in_=qscolL[:, :]))
        psw = sbc("fa_psw" + _t, [128, 128], F32)
        em.dma('sp', lambda e: e.dma_start(out=psw[:], in_=g("fpswap")[:, :]))
        tq = sbc("fa_tq" + _t, [128, 2 * 2048], BF16)
        tk = sbc("fa_tk" + _t, [128, 2 * TE], BF16)
        em.dma('pool', lambda e: e.dma_start(out=tq[:], in_=g("ftabq")[:, :]))
        em.dma('pool', lambda e: e.dma_start(out=tk[:], in_=g("ftabk")[:, :]))
        bd = sbc("fa_bd" + _t, [128, 4]); bdT = sbc("fa_bdT" + _t, [2, 128])
        em.op('pool', lambda e: e.memset(bd[:], 0.0))
        em.op('pool', lambda e: e.memset(bd[0:64, 0:1], 1.0))
        em.op('pool', lambda e: e.memset(bd[64:128, 1:2], 1.0))
        with nc.psum_tensor("fa_pbdT" + _t, [2, 128], F32) as pbdT:
            em.op('pe', lambda e: e.transpose(pbdT[0:2, 0:128], bd[:, 0:2], identf[:]))
            em.op('dve', lambda e: e.tensor_copy(out=bdT[:], in_=pbdT[0:2, 0:128]))
        wv_sb = sbc("fa_wv" + _t, [128, 8 * 1024], BF16)
        for i in range(8):
            em.dma('pool', lambda e, i=i: e.dma_start(out=wv_sb[:, i * 1024:(i + 1) * 1024], in_=wvL[i]))
        qf = sbc("fa_qf" + _t, [128, 8 * 2048], BF16)
        kf = sbc("fa_kf" + _t, [128, 8 * TE], BF16)
        vbuf = sbc("fa_vbuf" + _t, [128, 18 * 16 * 64], BF16)
        with contextlib.ExitStack() as st:
            sb = lambda n, s, dt=F32: st.enter_context(nc.sbuf_tensor(n, s, dt))
            xb = sb("fa_xb" + _t, [128, 8 * TE], BF16)
            for i in range(8):
                sh = 1 if i < 4 else 0
                hw = 257 - sh
                em.op('dve', lambda e, i=i, sh=sh, hw=hw: e.tensor_copy(out=xb[:, i * TE:i * TE + hw], in_=xh[:, i * 272 + sh:i * 272 + 257]))
                em.dma('pool', lambda e, i=i, hw=hw: e.dma_start(out=xb[:, i * TE + hw:(i + 1) * TE], in_=x_src[i * 128:(i + 1) * 128, 0:TE - hw]))
            wt = sb("fa_wt" + _t, [128, 8 * 128], BF16)
            pp = st.enter_context(nc.psum_tensor("fa_ppq" + _t, [128, 512], F32))
            psp = st.enter_context(nc.psum_tensor("fa_psp" + _t, [128, 512], F32))
            pnrm = st.enter_context(nc.psum_tensor("fa_pnrm" + _t, [2, 512], F32))
            pbc = st.enter_context(nc.psum_tensor("fa_pbc" + _t, [128, 512], F32))
            qraw = sb("fa_qraw" + _t, [128, 512]); sqb = sb("fa_sqb" + _t, [128, 512])
            nrm = sb("fa_nrm" + _t, [2, 512]); t1b = sb("fa_t1b" + _t, [128, 512]); t2b = sb("fa_t2b" + _t, [128, 512])
            for it in range(16):
                em.dma('pool', lambda e, it=it: e.dma_start(out=wt.ap().rearrange("p (a f) -> p a f", a=8), in_=wqk[it].rearrange("a p f -> p a f")))
                rng = 2048 if it < 8 else TE
                base_sh = 256 if it < 8 else 0
                tab = tq if it < 8 else tk
                tw = 2048 if it < 8 else TE
                qsc = qs_sb[:, 0:1] if it < 8 else qs_sb[:, 1:2]
                dstbuf = qf if it < 8 else kf
                for (off, w) in _chunks(rng, 512):
                    for dt_ in range(8):
                        em.op('pe', lambda e, it=it, off=off, w=w, dt_=dt_, base_sh=base_sh: e.matmul(
                            pp[:, 0:w], wt[:, dt_ * 128:(dt_ + 1) * 128], xb[:, dt_ * TE + base_sh + off:dt_ * TE + base_sh + off + w],
                            start=(dt_ == 0), stop=(dt_ == 7)))
                    em.op('dve', lambda e, w=w: e.tensor_copy(out=qraw[:, 0:w], in_=pp[:, 0:w]))
                    em.op('act', lambda e, w=w: e.activation(out=sqb[:, 0:w], in_=qraw[:, 0:w], func=AF.Square))
                    em.op('pe', lambda e, w=w: e.matmul(pnrm[0:2, 0:w], bd[:, 0:2], sqb[:, 0:w], start=True, stop=True))
                    em.op('act', lambda e, w=w: e.activation(out=nrm[0:2, 0:w], in_=pnrm[0:2, 0:w], func=AF.Ln, bias=c30[0:2, 0:1]))
                    em.op('act', lambda e, w=w: e.activation(out=nrm[0:2, 0:w], in_=nrm[0:2, 0:w], func=AF.Exp, scale=0.5))
                    em.op('dve', lambda e, w=w: e.tensor_scalar(out=nrm[0:2, 0:w], in0=nrm[0:2, 0:w], scalar1=1e-12, scalar2=None, op0=ALU.max))
                    em.op('dve', lambda e, w=w: e.reciprocal(out=nrm[0:2, 0:w], in_=nrm[0:2, 0:w]))
                    em.op('pe', lambda e, w=w: e.matmul(pbc[:, 0:w], bdT[0:2, :], nrm[0:2, 0:w], start=True, stop=True))
                    em.op('dve', lambda e, w=w, qsc=qsc: e.scalar_tensor_tensor(out=qraw[:, 0:w], in0=qraw[:, 0:w], scalar=qsc, in1=pbc[:, 0:w], op0=ALU.mult, op1=ALU.mult))
                    em.op('pe', lambda e, w=w: e.matmul(psp[:, 0:w], psw[:], qraw[:, 0:w], start=True, stop=True))
                    em.op('dve', lambda e, off=off, w=w, tab=tab: e.tensor_tensor(out=t1b[:, 0:w], in0=qraw[:, 0:w], in1=tab[:, off:off + w], op=ALU.mult))
                    em.op('dve', lambda e, off=off, w=w, tab=tab, tw=tw: e.tensor_tensor(out=t2b[:, 0:w], in0=psp[:, 0:w], in1=tab[:, tw + off:tw + off + w], op=ALU.mult))
                    em.op('dve', lambda e, it=it, off=off, w=w, dstbuf=dstbuf, rng=rng: e.tensor_tensor(
                        out=dstbuf[:, (it % 8) * rng + off:(it % 8) * rng + off + w], in0=t1b[:, 0:w], in1=t2b[:, 0:w], op=ALU.add))
            for tt in range(18):
                for och in range(2):
                    for dt_ in range(8):
                        em.op('pe', lambda e, tt=tt, och=och, dt_=dt_: e.matmul(
                            pp[:], xb[:, dt_ * TE + tt * 128:dt_ * TE + (tt + 1) * 128],
                            wv_sb[:, dt_ * 1024 + och * 512:dt_ * 1024 + (och + 1) * 512], start=(dt_ == 0), stop=(dt_ == 7)))
                    for hh in range(8):
                        h = och * 8 + hh
                        em.op('dve', lambda e, tt=tt, h=h, hh=hh: e.tensor_copy(
                            out=vbuf[:, (tt * 16 + h) * 64:(tt * 16 + h) * 64 + 64], in_=pp[:, hh * 64:(hh + 1) * 64]))
        avT = sbc("fa_avT" + _t, [128, 8 * 2048], BF16)
        with contextlib.ExitStack() as st:
            sb = lambda n, s, dt=F32: st.enter_context(nc.sbuf_tensor(n, s, dt))
            expT = sb("fa_expT" + _t, [128, 4 * 256], BF16)
            exin = sb("fa_exin" + _t, [128, 128])
            mcol = sb("fa_mcol" + _t, [128, 2])
            mrow_s = sb("fa_mrow" + _t, [1, 256])
            zbcs = sb("fa_zbcs" + _t, [128, 2048])
            zrow = sb("fa_zrow" + _t, [128, 2048])
            ptm = st.enter_context(nc.psum_tensor("fa_ptm" + _t, [128, 512], F32))
            pst = st.enter_context(nc.psum_tensor("fa_pst" + _t, [128, 256], F32))
            pav = st.enter_context(nc.psum_tensor("fa_pav" + _t, [128, 2048], F32))
            pz = st.enter_context(nc.psum_tensor("fa_pz" + _t, [1, 256], F32))
            pzb = st.enter_context(nc.psum_tensor("fa_pzb" + _t, [128, 512], F32))
            BLK = {(0, 0): 0, (1, 0): 1, (2, 0): 2, (1, 1): 3, (2, 1): 4, (3, 1): 5}
            STD = {(0, 0): 6, (1, 0): None, (2, 0): 7, (1, 1): 6, (2, 1): None, (3, 1): 7}
            SEQ = [(0, 0, 128), (1, 0, 256), (2, 0, 256), (3, 128, 256)]
            for h in range(16):
                tile, bh = h // 2, (h % 2) * 64
                for w_ in range(8):
                    for it_ in range(2):
                        em.op('pe', lambda e, tile=tile, bh=bh, w_=w_, it_=it_: e.matmul(
                            ptm[:, :],
                            qf[bh:bh + 64, tile * 2048 + w_ * 256 + it_ * 128: tile * 2048 + w_ * 256 + it_ * 128 + 128],
                            kf[bh:bh + 64, tile * TE + w_ * 256: tile * TE + w_ * 256 + 512], start=True, stop=True))
                        em.op('dve', lambda e, it_=it_: e.reduce_max(out=mcol[:, it_:it_ + 1], in_=ptm[:], axis=AX.X, negate=True))
                        em.op('pe', lambda e, it_=it_: e.transpose(pz[0:1, 0:128], mcol[:, it_:it_ + 1], identf[:]))
                        em.op('dve', lambda e, it_=it_: e.tensor_copy(out=mrow_s[0:1, it_ * 128:(it_ + 1) * 128], in_=pz[0:1, 0:128]))
                    for (jt, it_), mstd in STD.items():
                        midx = BLK[(jt, it_)] if w_ == 0 else mstd
                        em.op('pe', lambda e, jt=jt, it_=it_, w_=w_, tile=tile, bh=bh: e.matmul(
                            pst[:, 0:128], kf[bh:bh + 64, tile * TE + w_ * 256 + jt * 128: tile * TE + w_ * 256 + (jt + 1) * 128],
                            qf[bh:bh + 64, tile * 2048 + w_ * 256 + it_ * 128: tile * 2048 + w_ * 256 + (it_ + 1) * 128],
                            start=True, stop=False))
                        em.op('pe', lambda e, jt=jt, it_=it_: e.matmul(
                            pst[:, 0:128], ones[0:1, 0:128], mrow_s[0:1, it_ * 128:(it_ + 1) * 128], start=False, stop=True))
                        if midx is not None:
                            em.op('dve', lambda e, midx=midx: e.tensor_tensor(out=exin[:], in0=pst[:, 0:128], in1=mask_sb[:, midx * 128:(midx + 1) * 128], op=ALU.add))
                            em.op('act', lambda e, jt=jt, it_=it_: e.activation(out=expT[:, jt * 256 + it_ * 128: jt * 256 + (it_ + 1) * 128], in_=exin[:], func=AF.Exp))
                        else:
                            em.op('act', lambda e, jt=jt, it_=it_: e.activation(out=expT[:, jt * 256 + it_ * 128: jt * 256 + (it_ + 1) * 128], in_=pst[:, 0:128], func=AF.Exp))
                    for si, (jt, i0, i1) in enumerate(SEQ):
                        vt = 2 * w_ + jt
                        em.op('pe', lambda e, jt=jt, i0=i0, i1=i1, vt=vt, h=h, w_=w_, si=si, bh=bh: e.matmul(
                            pav[bh:bh + 64, w_ * 256 + i0: w_ * 256 + i1],
                            vbuf[:, (vt * 16 + h) * 64:(vt * 16 + h) * 64 + 64],
                            expT[:, jt * 256 + i0: jt * 256 + i1], start=(si == 0), stop=(si == 3)))
                        em.op('pe', lambda e, jt=jt, i0=i0, i1=i1, w_=w_, si=si: e.matmul(
                            pz[0:1, i0:i1], ones_bf[:, 0:1], expT[:, jt * 256 + i0: jt * 256 + i1], start=(si == 0), stop=(si == 3)))
                    em.op('dve', lambda e, w_=w_: e.tensor_copy(out=zrow[0:1, w_ * 256:(w_ + 1) * 256], in_=pz[0:1, 0:256]))
                em.op('dve', lambda e: e.tensor_scalar(out=zrow[0:1, :], in0=zrow[0:1, :], scalar1=1e-38, scalar2=None, op0=ALU.add))
                em.op('dve', lambda e: e.reciprocal(out=zrow[0:1, :], in_=zrow[0:1, :]))
                for (off, w) in _chunks(2048, 512):
                    em.op('pe', lambda e, off=off, w=w, bh=bh: e.matmul(pzb[bh:bh + 64, 0:w], ones[0:1, 0:64], zrow[0:1, off:off + w], start=True, stop=True))
                    em.op('dve', lambda e, off=off, w=w, bh=bh: e.tensor_copy(out=zbcs[bh:bh + 64, 0:w], in_=pzb[bh:bh + 64, 0:w]))
                    em.op('dve', lambda e, off=off, w=w, tile=tile, bh=bh: e.tensor_tensor(
                        out=avT[bh:bh + 64, tile * 2048 + off: tile * 2048 + off + w], in0=pav[bh:bh + 64, off:off + w], in1=zbcs[bh:bh + 64, 0:w], op=ALU.mult))
        with contextlib.ExitStack() as st:
            sb = lambda n, s, dt=F32: st.enter_context(nc.sbuf_tensor(n, s, dt))
            wt2 = sb("fa_wt2" + _t, [128, 8 * 128], BF16)
            xfr = sb("fa_xfr" + _t, [128, 512])
            ob2 = sb("fa_ob2" + _t, [128, 512])
            pow_ = st.enter_context(nc.psum_tensor("fa_pow" + _t, [128, 512], F32))
            for ot in range(8):
                em.dma('pool', lambda e, ot=ot: e.dma_start(out=wt2.ap().rearrange("p (a f) -> p a f", a=8), in_=woL[ot].rearrange("a p f -> p a f")))
                for (off, w) in _chunks(TC, 512):
                    for it in range(8):
                        em.op('pe', lambda e, ot=ot, off=off, w=w, it=it: e.matmul(
                            pow_[:, 0:w], wt2[:, it * 128:(it + 1) * 128], avT[:, it * 2048 + off:it * 2048 + off + w],
                            start=(it == 0), stop=(it == 7)))
                    em.dma('sp', lambda e, ot=ot, off=off, w=w: e.dma_start(out=xfr[:, 0:w], in_=x_src[ot * 128:(ot + 1) * 128, off:off + w]))
                    em.op('dve', lambda e, off=off, w=w: e.tensor_tensor(out=ob2[:, 0:w], in0=pow_[:, 0:w], in1=xfr[:, 0:w], op=ALU.add))
                    em.dma('sp', lambda e, ot=ot, off=off, w=w: e.dma_start(out=mid_sp[ot * 128:(ot + 1) * 128, off:off + w], in_=ob2[:, 0:w]))

    if _FUSE_NOHALO != 1:
        with contextlib.ExitStack() as st:
            tb = st.enter_context(nc.sbuf_tensor("ft_tb2" + _t, [128, 272], F32))
            for i in range(8):
                em.dma('sp', lambda e, i=i: e.dma_start(out=tb[:, 0:257], in_=mid_sp[i * 128:(i + 1) * 128, TC - 257:TC]))
                em.dma('sp', lambda e, i=i: e.dma_start(out=tl_in_c[i * 128:(i + 1) * 128, :], in_=tb[:]))
        em.cc(lambda e: e.collective_compute("AllGather", ALU.bypass, replica_groups=RG, ins=[tl_in_c[:, :]], outs=[tl_out_c[:, :]]))
    with contextlib.ExitStack() as cst:
        sbc = lambda n, s, dt=F32: cst.enter_context(nc.sbuf_tensor(n, s, dt))
        rs_sb2 = sbc("ff_rs" + _t, [128, 8])
        em.dma('sp', lambda e: e.dma_start(out=rs_sb2[:], in_=rsel[:, :]))
        xh1 = sbc("ff_xh1" + _t, [128, 8 * 272])
        tgl2 = sbc("ff_tgl" + _t, [128, 272])
        for i in range(8):
            em.op('pool', lambda e, i=i: e.memset(xh1[:, i * 272:(i + 1) * 272], 0.0))
            for r in range(NC):
                em.dma('sp', lambda e, i=i, r=r: e.dma_start(out=tgl2[:], in_=tl_out_c[r * D + i * 128: r * D + (i + 1) * 128, :]))
                em.op('dve', lambda e, i=i, r=r: e.scalar_tensor_tensor(out=xh1[:, i * 272:(i + 1) * 272], in0=tgl2[:], scalar=rs_sb2[:, r:r + 1], in1=xh1[:, i * 272:(i + 1) * 272], op0=ALU.mult, op1=ALU.add))
        lnc = sbc("ff_lnc" + _t, [128, 16], F32)
        em.dma('sp', lambda e: e.dma_start(out=lnc[:], in_=g("flncols")[li][:, :]))
        b1c = sbc("ff_b1c" + _t, [128, 44], F32)
        em.dma('sp', lambda e: e.dma_start(out=b1c[:], in_=g("fbcols1")[li][:, :]))
        b2c = sbc("ff_b2c" + _t, [128, 8], F32)
        em.dma('sp', lambda e: e.dma_start(out=b2c[:], in_=g("fbcols2")[li][:, :]))
        xnb = sbc("ff_xnb" + _t, [128, 8 * TC], BF16)
        ln_src_halo = [None] * 8
        with contextlib.ExitStack() as st:
            xc = st.enter_context(nc.sbuf_tensor("ff_xc" + _t, [128, 8 * 512], F32))
            sq = st.enter_context(nc.sbuf_tensor("ff_sq" + _t, [128, 512], F32))
            p1 = st.enter_context(nc.psum_tensor("ff_p1" + _t, [1, 512], F32))
            p2 = st.enter_context(nc.psum_tensor("ff_p2" + _t, [1, 512], F32))
            r1 = st.enter_context(nc.sbuf_tensor("ff_r1" + _t, [1, 512], F32))
            r2 = st.enter_context(nc.sbuf_tensor("ff_r2" + _t, [1, 512], F32))
            bc1 = st.enter_context(nc.psum_tensor("ff_bc1" + _t, [128, 512], F32))
            bc2 = st.enter_context(nc.psum_tensor("ff_bc2" + _t, [128, 512], F32))
            tmp = st.enter_context(nc.sbuf_tensor("ff_tmp" + _t, [128, 512], F32))
            for (off, w) in _chunks(TC, 512):
                for i in range(8):
                    if i < 4:
                        em.dma('sp', lambda e, i=i, off=off, w=w: e.dma_start(out=xc[:, i * 512:i * 512 + w], in_=mid_sp[i * 128:(i + 1) * 128, off:off + w]))
                    else:
                        if off == 0:
                            em.op('dve', lambda e, i=i: e.tensor_copy(out=xc[:, i * 512:i * 512 + 1], in_=xh1[:, i * 272 + 256:i * 272 + 257]))
                            em.dma('sp', lambda e, i=i, w=w: e.dma_start(out=xc[:, i * 512 + 1:i * 512 + w], in_=mid_sp[i * 128:(i + 1) * 128, 0:w - 1]))
                        else:
                            em.dma('sp', lambda e, i=i, off=off, w=w: e.dma_start(out=xc[:, i * 512:i * 512 + w], in_=mid_sp[i * 128:(i + 1) * 128, off - 1:off - 1 + w]))
                for i in range(8):
                    em.op('pe', lambda e, i=i, w=w: e.matmul(p1[0:1, 0:w], ones[:, 0:1], xc[:, i * 512:i * 512 + w], start=(i == 0), stop=(i == 7)))
                for i in range(8):
                    em.op('act', lambda e, i=i, w=w: e.activation(out=sq[:, 0:w], in_=xc[:, i * 512:i * 512 + w], func=AF.Square))
                    em.op('pe', lambda e, i=i, w=w: e.matmul(p2[0:1, 0:w], ones[:, 0:1], sq[:, 0:w], start=(i == 0), stop=(i == 7)))
                em.op('dve', lambda e, w=w: e.tensor_scalar(out=r1[0:1, 0:w], in0=p1[0:1, 0:w], scalar1=1.0 / D, scalar2=None, op0=ALU.mult))
                em.op('act', lambda e, w=w: e.activation(out=r2[0:1, 0:w], in_=r1[0:1, 0:w], func=AF.Square))
                em.op('dve', lambda e, w=w: e.scalar_tensor_tensor(out=r2[0:1, 0:w], in0=p2[0:1, 0:w], scalar=1.0 / D, in1=r2[0:1, 0:w], op0=ALU.mult, op1=ALU.subtract))
                em.op('dve', lambda e, w=w: e.tensor_scalar(out=r2[0:1, 0:w], in0=r2[0:1, 0:w], scalar1=1e-5, scalar2=None, op0=ALU.add))
                em.op('act', lambda e, w=w: e.activation(out=r2[0:1, 0:w], in_=r2[0:1, 0:w], func=AF.Ln))
                em.op('act', lambda e, w=w: e.activation(out=r2[0:1, 0:w], in_=r2[0:1, 0:w], func=AF.Exp, scale=-0.5))
                em.op('dve', lambda e, w=w: e.scalar_tensor_tensor(out=r1[0:1, 0:w], in0=r1[0:1, 0:w], scalar=-1.0, in1=r2[0:1, 0:w], op0=ALU.mult, op1=ALU.mult))
                em.op('pe', lambda e, w=w: e.matmul(bc1[:, 0:w], ones[0:1, :], r2[0:1, 0:w], start=True, stop=True))
                em.op('pe', lambda e, w=w: e.matmul(bc2[:, 0:w], ones[0:1, :], r1[0:1, 0:w], start=True, stop=True))
                for i in range(8):
                    em.op('dve', lambda e, i=i, w=w: e.tensor_tensor(out=tmp[:, 0:w], in0=xc[:, i * 512:i * 512 + w], in1=bc1[:, 0:w], op=ALU.mult))
                    em.op('dve', lambda e, i=i, w=w: e.tensor_tensor(out=tmp[:, 0:w], in0=tmp[:, 0:w], in1=bc2[:, 0:w], op=ALU.add))
                    em.op('dve', lambda e, i=i, w=w: e.tensor_scalar(out=tmp[:, 0:w], in0=tmp[:, 0:w], scalar1=lnc[:, i:i + 1], scalar2=lnc[:, 8 + i:9 + i], op0=ALU.mult, op1=ALU.add))
                    em.op('dve', lambda e, i=i, off=off, w=w: e.tensor_copy(out=xnb[:, i * TC + off:i * TC + off + w], in_=tmp[:, 0:w]))
        w2_sb = sbc("ff_w2" + _t, [128, 22 * 1024], BF16)
        for ht in range(22):
            em.dma('pool', lambda e, ht=ht: e.dma_start(out=w2_sb[:, ht * 1024:(ht + 1) * 1024], in_=g("fw2t")[li][ht]))
        with contextlib.ExitStack() as st:
            sb = lambda n, s, dt=F32: st.enter_context(nc.sbuf_tensor(n, s, dt))
            wta = sb("ff_wta" + _t, [128, 8 * 128], BF16)
            wtg = sb("ff_wtg" + _t, [128, 8 * 128], BF16)
            hbuf = sb("ff_hbuf" + _t, [128, 22 * 512], BF16)
            gel = sb("ff_gel" + _t, [128, 512])
            xfr = sb("ff_xfr" + _t, [128, 512])
            ob = sb("ff_ob" + _t, [128, 512])
            pa = st.enter_context(nc.psum_tensor("ff_pa" + _t, [128, 512], F32))
            pg = st.enter_context(nc.psum_tensor("ff_pg" + _t, [128, 512], F32))
            py = st.enter_context(nc.psum_tensor("ff_py" + _t, [128, 512], F32))
            for (off, w) in _chunks(TC, 512):
                for ht in range(22):
                    em.dma('pool', lambda e, ht=ht: e.dma_start(out=wta.ap().rearrange("p (a f) -> p a f", a=8), in_=g("fw1a")[li][ht].rearrange("a p f -> p a f")))
                    em.dma('pool', lambda e, ht=ht: e.dma_start(out=wtg.ap().rearrange("p (a f) -> p a f", a=8), in_=g("fw1g")[li][ht].rearrange("a p f -> p a f")))
                    for dt_ in range(8):
                        em.op('pe', lambda e, ht=ht, off=off, w=w, dt_=dt_: e.matmul(
                            pa[:, 0:w], wta[:, dt_ * 128:(dt_ + 1) * 128], xnb[:, dt_ * TC + off:dt_ * TC + off + w], start=(dt_ == 0), stop=(dt_ == 7)))
                    for dt_ in range(8):
                        em.op('pe', lambda e, ht=ht, off=off, w=w, dt_=dt_: e.matmul(
                            pg[:, 0:w], wtg[:, dt_ * 128:(dt_ + 1) * 128], xnb[:, dt_ * TC + off:dt_ * TC + off + w], start=(dt_ == 0), stop=(dt_ == 7)))
                    em.op('act', lambda e, ht=ht, w=w: e.activation(out=gel[:, 0:w], in_=pg[:, 0:w], func=AF.Gelu, bias=b1c[:, 22 + ht:23 + ht]))
                    em.op('dve', lambda e, ht=ht, w=w: e.scalar_tensor_tensor(out=hbuf[:, ht * 512:ht * 512 + w], in0=pa[:, 0:w], scalar=b1c[:, ht:ht + 1], in1=gel[:, 0:w], op0=ALU.add, op1=ALU.mult))
                for ot in range(8):
                    for ht in range(22):
                        em.op('pe', lambda e, ot=ot, ht=ht, w=w: e.matmul(
                            py[:, 0:w], w2_sb[:, ht * 1024 + ot * 128: ht * 1024 + (ot + 1) * 128], hbuf[:, ht * 512:ht * 512 + w],
                            start=(ht == 0), stop=(ht == 21)))
                    em.dma('sp', lambda e, ot=ot, off=off, w=w: e.dma_start(out=xfr[:, 0:w], in_=mid_sp[ot * 128:(ot + 1) * 128, off:off + w]))
                    em.op('dve', lambda e, ot=ot, w=w: e.tensor_scalar(out=ob[:, 0:w], in0=py[:, 0:w], scalar1=b2c[:, ot:ot + 1], scalar2=None, op0=ALU.add))
                    em.op('dve', lambda e, w=w: e.tensor_tensor(out=ob[:, 0:w], in0=ob[:, 0:w], in1=xfr[:, 0:w], op=ALU.add))
                    em.dma('sp', lambda e, ot=ot, off=off, w=w: e.dma_start(out=x_dst[ot * 128:(ot + 1) * 128, off:off + w], in_=ob[:, 0:w]))


def _fused_part5(cx, L):
    nc, em = cx.nc, cx.em
    import contextlib
    g = lambda n: L[n]
    xn1_sp = g("xn1_sp"); y_a = g("y_a"); y_b = g("y_b")
    x1_sp = g("x1_sp"); s1_sp = g("s1_sp"); out_f = g("out_f")
    srcs = [xn1_sp, y_b, y_a, y_b, y_a, y_b]
    dsts = [y_b, y_a, y_b, y_a, y_b, y_a]
    for li in range(_FUSE_LAYERS):
        _emit_ffn_layer_fused(cx, L, li, srcs[li], dsts[li])
    yfin = dsts[_FUSE_LAYERS - 1] if _FUSE_LAYERS > 0 else xn1_sp
    if _FUSE_DEBUG:
        with contextlib.ExitStack() as std:
            tbd = std.enter_context(nc.sbuf_tensor("dbg_tb", [128, TC], F32))
            for (nm, srcten) in (("dbg_x1", g("x1_sp")), ("dbg_xn1", g("xn1_sp")), ("dbg_s1", g("s1_sp")), ("dbg_y", yfin), ("dbg_enc", g("enc_sp")), ("dbg_val", g("val_sp"))):
                for i in range(8):
                    em.dma('sp', lambda e, nm=nm, srcten=srcten, i=i: e.dma_start(out=tbd[:], in_=srcten[i * 128:(i + 1) * 128, :]))
                    em.dma('sp', lambda e, nm=nm, i=i: e.dma_start(out=L[nm][i * 128:(i + 1) * 128, :], in_=tbd[:]))
            tz = std.enter_context(nc.sbuf_tensor("dbg_tz", [128, 128], F32))
            for r in range(NC):
                em.dma('sp', lambda e, r=r: e.dma_start(out=tz[:], in_=g("z2_out_c")[r * 128:(r + 1) * 128, :]))
                em.dma('sp', lambda e, r=r: e.dma_start(out=L["dbg_z2"][r * 128:(r + 1) * 128, :], in_=tz[:]))
            tar = std.enter_context(nc.sbuf_tensor("dbg_tar", [128, 64], F32))
            em.dma('sp', lambda e: e.dma_start(out=tar[:], in_=g("ar_out_c")[:, :]))
            em.dma('sp', lambda e: e.dma_start(out=L["dbg_ar"][:, :], in_=tar[:]))
            tt2 = std.enter_context(nc.sbuf_tensor("dbg_tt2", [128, 16], F32))
            for r in range(NC):
                em.dma('sp', lambda e, r=r: e.dma_start(out=tt2[:], in_=g("tot_out")[r * 128:(r + 1) * 128, :]))
                em.dma('sp', lambda e, r=r: e.dma_start(out=L["dbg_tot"][r * 128:(r + 1) * 128, :], in_=tt2[:]))
    with contextlib.ExitStack() as st:
        sb = lambda n, s, dt=F32: st.enter_context(nc.sbuf_tensor(n, s, dt))
        a = sb("fin_a", [128, TC]); b = sb("fin_b", [128, TC]); c = sb("fin_c", [128, TC])
        for i in range(8):
            em.dma('sp', lambda e, i=i: e.dma_start(out=a[:], in_=yfin[i * 128:(i + 1) * 128, :]))
            em.dma('sp', lambda e, i=i: e.dma_start(out=b[:], in_=s1_sp[i * 128:(i + 1) * 128, :]))
            em.dma('sp', lambda e, i=i: e.dma_start(out=c[:], in_=x1_sp[i * 128:(i + 1) * 128, :]))
            em.op('act', lambda e: e.activation(out=a[:], in_=a[:], func=AF.Silu))
            em.op('dve', lambda e: e.tensor_tensor(out=a[:], in0=a[:], in1=b[:], op=ALU.mult))
            em.op('dve', lambda e: e.tensor_tensor(out=a[:], in0=a[:], in1=c[:], op=ALU.add))
            em.dma('sp', lambda e, i=i: e.dma_start(out=g("out_f")[i * 128:(i + 1) * 128, :], in_=a[:]))



def _prep_fused(inp, pr, pr1, prf):
    rope = prf["rope"]
    CS, SS, CI, SI = rope
    rows = inp["L1_attn_adj_rows"].astype(np.uint32)
    cols_ = inp["L1_attn_adj_cols"].astype(np.uint32)
    vals_ = inp["L1_attn_adj_vals"].astype(np.float32)
    cw = inp["L1_attn_conv_w"][:, 0, :, 0]
    cw_in = np.zeros((128, 4 * 33), np.float32)
    for g in range(4):
        cw_in[0:64, g * 33:(g + 1) * 33] = cw[2 * g][None, :]
        cw_in[64:128, g * 33:(g + 1) * 33] = cw[2 * g + 1][None, :]
    outw = np.ascontiguousarray(inp["L1_attn_out_w"].reshape(4, 128, 8, 128).transpose(2, 0, 1, 3))
    outb = np.ascontiguousarray(inp["L1_attn_out_b"].reshape(8, 128).T)
    wq_in = np.ascontiguousarray(inp["L1_attn_wq"].reshape(8, 128, 256))
    wk_in = np.ascontiguousarray(inp["L1_attn_wk"].reshape(8, 128, 256))
    pp_ = np.arange(128)
    maps = []
    for c in range(NC):
        b, q = c // 4, c % 4
        t0 = q * TC
        hsel = c
        m = {"xT": pr["xT"][c], "trem": pr["trem"][c]}
        for nm in ("WrTt", "WkTt", "WvTt", "WoTt", "colsd", "routerTt", "sgTt", "suTt", "sdT", "egTt", "euTt", "edT"):
            m[nm] = pr[nm]
        cmask = np.zeros((128, 8), np.float32)
        for qq in range(q):
            cmask[:, b * 4 + qq] = 1.0
        m["cmask"] = np.ascontiguousarray(cmask)
        m["qkvw"] = pr1["qkvw"]; m["wvw"] = pr1["wvw"]; m["wvb"] = pr1["wvb"]
        sel = np.zeros((128, 16), np.float32); sel[:, hsel] = 1.0
        m["sel16"] = np.ascontiguousarray(sel)
        dh = np.arange(64)
        kh = np.zeros((64, 4), np.uint32)
        for r in range(4):
            kh[:, r] = r * 512 + hsel * 64 + dh
        m["kh_idx"] = np.ascontiguousarray(kh)
        vh = np.zeros((128, 64), np.uint32)
        for tt in range(64):
            vh[:, tt] = (tt * 128 + pp_) * 8 + hsel
        m["vh_idx"] = np.ascontiguousarray(vh)
        qqi = np.zeros((128, 4), np.uint32)
        for gq in range(4):
            qqi[:, gq] = q * 512 + gq * 128 + pp_
        m["qq_idx"] = np.ascontiguousarray(qqi)
        vv = np.zeros((128, 18), np.uint32)
        for gq in range(4):
            vv[:, gq] = max(q - 1, 0) * 512 + gq * 128 + pp_
            vv[:, 4 + gq] = min(q + 1, 3) * 512 + gq * 128 + pp_
        m["vv_idx"] = np.ascontiguousarray(vv)
        m["al_idx"] = np.ascontiguousarray((32 * q + np.arange(32)).astype(np.uint32)[:, None])
        hal = np.zeros((128, 8), np.uint32)
        for i in range(8):
            hal[:, i] = (max(c - 1, 0)) * 1024 + i * 128 + pp_
        m["hal_idx"] = np.ascontiguousarray(hal)
        rsel = np.zeros((128, 8), np.float32)
        if q > 0:
            rsel[:, c - 1] = 1.0
        m["rsel"] = np.ascontiguousarray(rsel)
        hmask = np.zeros((128, 2), np.float32)
        hmask[:, 0] = 1.0 if q > 0 else 0.0
        hmask[:, 1] = 1.0 if q < 3 else 0.0
        m["hmask"] = np.ascontiguousarray(hmask)
        sl = slice(c * NEDGE, (c + 1) * NEDGE)
        m["ridx"] = np.ascontiguousarray(rows[sl].reshape(NG, 128).T)
        m["cidx"] = np.ascontiguousarray(cols_[sl].reshape(NG, 128).T)
        m["vals"] = np.ascontiguousarray(vals_[sl].reshape(NG, 128).T)
        m["outw"] = outw; m["outb"] = outb; m["cw_in"] = cw_in
        m["wq_in"] = wq_in; m["wk_in"] = wk_in
        m["fwqk"] = prf["wqk"]; m["fwv"] = prf["wv"]; m["fwo"] = prf["wo"]
        m["fqscol"] = prf["qscol"]
        fc = _ffn_core_inputs(prf, c, 0)
        m["ftabq"] = fc["tabq"]; m["ftabk"] = fc["tabk"]
        m["fpswap"] = prf["pswap"]; m["fmasks"] = fc["masks"]
        m["fw1a"] = prf["w1a"]; m["fw1g"] = prf["w1g"]; m["fw2t"] = prf["w2t"]
        m["flncols"] = prf["lncols"]; m["fbcols1"] = prf["bcols1"]; m["fbcols2"] = prf["bcols2"]
        maps.append(m)
    return maps


_FUSED = True
_FUSE_UPTO = 5
_FUSE_LAYERS = 6
_FUSE_NOHALO = False
_HMC = 1.0
_FUSE_DEBUG = False


def kernel_unfused(inp):
    pr = _prep_common(inp)
    x1 = _run_b0(inp, pr)
    x1 = [np.ascontiguousarray(v) for v in x1]
    prf = _prep_ffn(inp)
    outs = _run_b1(inp, x1, prf)
    full = np.zeros((2, T, D), np.float32)
    for c in range(NC):
        b, q = c // 4, c % 4
        full[b, q * TC:(q + 1) * TC, :] = outs[c].T
    return full


def kernel_fused(inp):
    pr = _prep_common(inp)
    pr1 = _prep_b1(inp)
    prf = _prep_ffn(inp)
    maps = _prep_fused(inp, pr, pr1, prf)
    nc = build_fused()
    r = run_bass_kernel_spmd(nc, maps, core_ids=list(range(NC)))
    full = np.zeros((2, T, D), np.float32)
    for c in range(NC):
        b, q = c // 4, c % 4
        full[b, q * TC:(q + 1) * TC, :] = r.results[c]["out"].T
    return full
